# version 02
# baseline (speedup 1.0000x reference)
import math
import numpy as np
from contextlib import ExitStack

import concourse.bass as bass
import concourse.mybir as mybir
from concourse.bass_utils import run_bass_kernel_spmd

F32 = mybir.dt.float32
BF16 = mybir.dt.bfloat16
AF = mybir.ActivationFunctionType
AX = mybir.AxisListType
OP = mybir.AluOpType

T = 2048
D = 1024
NT = T // 128
ND = D // 128
NCORES = 8


def _rel_bucket(delta):
    NUM_BUCKETS, MAX_DISTANCE = 32, 64
    rp = max(delta, 0)
    max_exact = NUM_BUCKETS // 2
    if rp < max_exact:
        return rp
    rp_f = max(rp, 1)
    large = max_exact + int(math.log(rp_f / max_exact) / math.log(MAX_DISTANCE / max_exact) * (NUM_BUCKETS - max_exact))
    return min(large, NUM_BUCKETS - 1)


def host_constants():
    c = {}
    c["c_ident"] = np.eye(128, dtype=np.float32)
    c["c_ones"] = np.ones((128, 128), dtype=np.float32)
    c["c_J"] = np.eye(128, dtype=np.float32)[::-1].copy()
    R = np.zeros((128, 128), dtype=np.float32)
    for h in range(2):
        b = h * 64
        R[b:b+32, b+32:b+64] = np.eye(32)
        R[b+32:b+64, b:b+32] = np.eye(32)
    c["c_R"] = R
    half = 32
    inv = 1.0 / (10000.0 ** (np.arange(half, dtype=np.float64) / half))
    ang = np.arange(T, dtype=np.float64)[:, None] * inv[None, :]
    sin, cos = np.sin(ang).T, np.cos(ang).T
    cos_ext = np.zeros((128, T), dtype=np.float32)
    sin_ext = np.zeros((128, T), dtype=np.float32)
    for h in range(2):
        b = h * 64
        cos_ext[b:b+32] = cos; cos_ext[b+32:b+64] = cos
        sin_ext[b:b+32] = -sin; sin_ext[b+32:b+64] = sin
    cos_rep = np.zeros((128, T), dtype=np.float32)
    for h in range(2):
        b = h * 64
        cos_rep[b:b+64] = np.vstack([cos, cos])
    c["c_ropecos"] = cos_rep
    c["c_ropesin"] = sin_ext
    tri = np.where(np.arange(128)[None, :] > np.arange(128)[:, None], -1e9, 0.0)
    c["c_tri"] = tri.astype(np.float32)
    c["c_triT"] = np.ascontiguousarray(c["c_tri"].T)
    G = np.zeros((32, 4096), dtype=np.float32)
    for d in range(4095):
        delta = d - 2047
        G[_rel_bucket(delta), d] = 8.0
    c["c_G"] = G
    c["c_Gr"] = np.ascontiguousarray(G[:, ::-1][:, 1:].copy())
    Gr = np.zeros((32, 4096), dtype=np.float32)
    Gr[:, 0:4095] = G[:, 0:4095][:, ::-1]
    c["c_Gr"] = Gr
    return c



class Sem:
    __slots__ = ("h", "n", "name")
    def __init__(self, h, name):
        self.h = h; self.n = 0; self.name = name


class P:
    ENGINES = ("sync", "scalar", "vector", "tensor", "gpsimd")

    def __init__(self, nc, es):
        self.nc = nc
        self.es = es
        self.items = []
        self._semid = 0
        self._reg = {}
        self._pools = {}

    SEM_ALIAS = {
        "l0_vq": "chv", "l0_qgf": "chv", "l0_pk": "chv", "l0_kve": "chv",
        "l0_nd": "chv", "l0_cch": "chv", "l0_sch": "chv", "l0_ech": "chv",
        "x2_ch": "chv", "lg_ch": "chv", "rm_ch": "chv", "l0_zv": "chv",
        "l0_sm": "chm", "lg_vs": "chm", "rm_vs": "chm", "ln_vs": "chm",
        "l3_ones": "vx_on", "l2_ones": "vx_on", "lg_on": "vx_on",
        "l3_b2": "gb2", "l0_b2": "gb2", "l0_r0": "gb2", "bR_ld": "gb2",
        "bb_ld": "gb2", "vc_ld": "gb2",
        "l0_cl": "lf", "l0_cw": "lf", "l0_pkl": "lf",
        "l0_ve0": "at_ve0", "l0_ve1": "at_ve1",
        "l0_as0": "vx_st0", "l0_as1": "vx_st1",
        "l0_a2s0": "vx_st0", "l0_a2s1": "vx_st1",
        "bR_st0": "vx_st0", "bR_st1": "vx_st1",
        "t2f_l0gpsimd": "gld0", "t2f_l1gpsimd": "gld1",
        "l0_sl0": "up_ld0", "l0_sl1": "up_ld1",
        "l0_xl0": "x2_ld0", "l0_xl1": "x2_ld1",
        "t2f_l0sync": "up_ld0", "t2f_l1sync": "up_ld1",
        "t2f_s0": "ln_st0", "t2f_s1": "ln_st1",
        "rm_x0": "ln_x0", "rm_x1": "ln_x1", "lg_x0": "ln_x0", "lg_x1": "ln_x1",
        "rm_us": "ln_us", "rm_uv": "ln_uv", "lg_us": "ln_us", "lg_uv": "ln_uv",
        "rm_tm": "ln_tm", "rm_tp": "ln_tp", "rm_cp": "ln_cp",
        "lg_tm": "ln_tm", "lg_tp": "ln_tp", "lg_cp": "ln_cp",
        "t2f_tp": "ln_tp", "t2f_cp": "ln_cp",
        "rm_st0": "ln_st0", "rm_st1": "ln_st1",
        "lg_st0": "ln_st0", "lg_st1": "ln_st1",
        "lg_ex0": "vx_st0", "lg_ex1": "vx_st1",
        "l0_g": "chv", "l0_gd": "gb2", "l0_zb": "chv", "l0_zbd": "gld0",
    }

    def sem(self, name=None):
        if name is None:
            import inspect
            fr = inspect.currentframe().f_back
            name = f"{fr.f_code.co_name}_{fr.f_lineno}"
        name = self.SEM_ALIAS.get(name, name)
        if name in self._reg:
            return self._reg[name]
        self._semid += 1
        s = Sem(self.es.enter_context(self.nc.semaphore(name)), name)
        self._reg[name] = s
        return s

    def pool(self, name, factory):
        if name not in self._pools:
            self._pools[name] = factory()
        return self._pools[name]

    def emit(self, engine, build, waits=(), incs=()):
        wcap = [(s.h, int(v)) for (s, v) in waits]
        icap = []
        for s, a in incs:
            s.n += a
            icap.append((s.h, a))
        def thunk(eng, _w=wcap, _i=icap, _b=build):
            for h, v in _w:
                eng.wait_ge(h, v)
            ins = _b(eng)
            for h, a in _i:
                if ins is None:
                    raise RuntimeError("inc on build that returned None")
                ins.then_inc(h, a)
        self.items.append((engine, thunk))

    def wait(self, engine, waits):
        wcap = [(s.h, int(v)) for (s, v) in waits]
        def thunk(eng, _w=wcap):
            for h, v in _w:
                eng.wait_ge(h, v)
        self.items.append((engine, thunk))

    def dma(self, engine, out, in_, sem, waits=(), **kw):
        self.emit(engine, lambda e, o=out, i=in_: e.dma_start(out=o, in_=i, **kw),
                  waits=waits, incs=[(sem, 16)])

    def finalize_block(self):
        with self.nc.Block() as block:
            for name in self.ENGINES:
                def body(eng, _nm=name):
                    for e, fn in self.items:
                        if e == _nm:
                            fn(eng)
                getattr(block, name)(body)



class Pool:
    def __init__(self, p, regions, name):
        self.p = p
        self.regions = regions
        self.free = p.sem(name)
        self.uses = 0

    def acquire(self):
        i = self.uses
        self.uses += 1
        r = self.regions[i % len(self.regions)]
        wv = i - len(self.regions) + 1
        return r, max(0, wv)


class SlotPool:
    def __init__(self, p, regions, name, inc=1):
        self.p = p
        self.regions = regions
        self.inc = inc
        self.sems = [p.sem(f"{name}{i}") for i in range(len(regions))]
        self.kuse = [0] * len(regions)
        self.uses = 0

    def acquire(self):
        n = len(self.regions)
        slot = self.uses % n
        self.uses += 1
        k = self.kuse[slot]
        self.kuse[slot] += 1
        waits = [(self.sems[slot], self.inc * k)] if k > 0 else []
        return self.regions[slot], waits, (self.sems[slot], self.inc)

    def all_done(self):
        return [(self.sems[i], self.inc * self.kuse[i]) for i in range(len(self.regions))
                if self.kuse[i] > 0]



class Ctx:
    pass


def make_inputs_dram(nc, input_specs):
    aps = {}
    for name, shape in input_specs:
        shp = list(shape) if len(shape) > 0 else [1]
        aps[name] = nc.dram_tensor(name, shp, F32, kind="ExternalInput").ap()
    return aps


def build_ctx(nc, es, p):
    cx = Ctx()
    cx.nc, cx.es, cx.p = nc, es, p
    sb = lambda nm, shp, dt: es.enter_context(nc.sbuf_tensor(nm, shp, dt))
    ps = lambda nm, shp, dt=F32: es.enter_context(nc.psum_tensor(nm, shp, dt))
    cx.SRC = sb("SRC", [128, 16384], BF16)
    cx.SRC2 = sb("SRC2", [128, 11264], BF16)
    cx.WRES = sb("WRES", [128, 22528], BF16)
    cx.OUTS = sb("OUTS", [128, 4096], F32)
    cx.SBF = sb("SBF", [128, 16384], BF16)
    cx.SF = sb("SF", [128, 8192], F32)
    cx.SMALL = sb("SMALL", [128, 4096], F32)
    cx.CONST = sb("CONST", [128, 512], F32)
    cx.CBF = sb("CBF", [128, 512], BF16)
    cx.ROWS = sb("ROWS", [128, 4096], BF16)
    cx.PA = ps("PA", [128, 2048])
    cx.PB = ps("PB", [128, 1536])
    cx.PT = ps("PT", [128, 1024], BF16)
    return cx



def precast_weights(p, nc, inp, names_shapes, wbf):
    sem = p.sem("precast")
    for name, shape in names_shapes:
        src = inp[name]
        dst = wbf[name]
        n_el = int(np.prod(shape)) if shape else 1
        flat_src = src
        CH = 1 << 21
        off = 0
        sflat = src.tensor.ap()
        dflat = dst.tensor.ap()
        while off < n_el:
            ln = min(CH, n_el - off)
            s = sflat.copy(); s.offset = off; s.ap = _vec([[1, ln]])
            d = dflat.copy(); d.offset = off; d.ap = _vec([[1, ln]])
            p.dma("gpsimd", d, s, sem)
            off += ln
    return sem


import bass_rust
def _vec(lst):
    return bass_rust.VecI64Pair(lst)


def flat_view(ap, offset, pattern):
    v = ap.tensor.ap().copy() if hasattr(ap, "tensor") else ap.copy()
    v.offset = int(offset)
    v.ap = _vec(pattern)
    return v


def load_fm(cx, dram, K, Tn, dst_sb, dst_col0, sem, waits=()):
    p = cx.p
    nk = K // 128
    src = flat_view(dram, 0, [[Tn, 128], [128 * Tn, nk], [1, Tn]])
    dst = dst_sb[:, dst_col0:dst_col0 + nk * Tn].rearrange("p (k t) -> p k t", k=nk)
    p.dma("sync", dst, src, sem, waits=list(waits) + [(sem, sem.n)])


def w_strip_load(cx, wdram, dc, n0, ncols, dst_ap, sem, waits=(), wT=False):
    p = cx.p
    if not wT:
        p.dma("sync", dst_ap, wdram[dc * 128:(dc + 1) * 128, n0:n0 + ncols], sem, waits=waits)
    else:
        p.dma("sync", dst_ap, wdram[n0:n0 + ncols, dc * 128:(dc + 1) * 128], sem,
              waits=waits, transpose=True)


class ProjPools:
    def __init__(self, cx):
        p = cx.p
        self.pb512 = SlotPool(p, [cx.PB[:, i * 512:(i + 1) * 512] for i in range(3)], "pb", 1)
        self.pa1024 = SlotPool(p, [cx.PA[:, 0:1024], cx.PA[:, 1024:2048]], "pa", 1)
        self.stage1024 = SlotPool(p, [cx.OUTS[:, i * 1024:(i + 1) * 1024] for i in range(4)],
                                  "stg", 16)
        self.wsem = p.sem("wsem")
        self.ss = [p.sem("ss0"), p.sem("ss1")]
        self.ssk = [0, 0]
        self.mm = p.sem("mm")
        self.ev = p.sem("ev")
        self.evs = p.sem("evs")

    def wguard(self):
        return self.pb512.all_done() + self.pa1024.all_done()


def projB(cx, pp, *, src_tile, K, Tn, wdram, wT, N, dst_dram=None, src_ready=(),
          bias_col=None, act=None, scale=1.0, evac_cb=None, store=True, sb_out=None,
          wres_off=0):
    p = cx.p
    nk = K // 128
    wguard = pp.wguard()
    NG = min(N, max(128, ((22528 - wres_off) // nk) // 128 * 128))
    n_groups = (N + NG - 1) // NG
    TC = 512
    ntc = (Tn + TC - 1) // TC
    done = []
    for g in range(n_groups):
        n0g = g * NG
        ng = min(NG, N - n0g)
        w0 = pp.wsem.n
        for dc in range(nk):
            wts = wguard if dc == 0 else ()
            w_strip_load(cx, wdram, dc, n0g, ng,
                         cx.WRES[:, wres_off + dc * ng: wres_off + dc * ng + ng],
                         pp.wsem, waits=wts, wT=wT)
        wtarget = pp.wsem.n
        wguard = None
        for oc in range(ng // 128):
            for tc in range(ntc):
                t0 = tc * TC
                tw = min(TC, Tn - t0)
                ps, pw, prel = pp.pb512.acquire()
                psum = ps[:, 0:tw]
                waits = [(pp.wsem, wtarget)] + pw + list(src_ready)
                for dc in range(nk):
                    def b(e, dc=dc, oc=oc, t0=t0, tw=tw, psum=psum, ng=ng):
                        return e.matmul(psum,
                                        cx.WRES[:, wres_off + dc * ng + oc * 128:
                                                wres_off + dc * ng + oc * 128 + 128],
                                        src_tile(dc)[:, t0:t0 + tw],
                                        start=(dc == 0), stop=(dc == nk - 1))
                    p.emit("tensor", b, waits=waits if dc == 0 else (),
                           incs=[(pp.mm, 1)] if dc == nk - 1 else ())
                mmv = pp.mm.n
                st, sw, srel = pp.stage1024.acquire()
                stage = st[:, 0:tw]
                ocg = (n0g + oc * 128) // 128
                ewaits = [(pp.mm, mmv)] + sw
                if evac_cb is not None:
                    evac_cb(ocg, t0, tw, psum, stage, ewaits, prel)
                else:
                    bc = bias_col(ocg) if bias_col is not None else 0.0
                    fn = act if act is not None else AF.Identity
                    p.emit("scalar",
                           lambda e, psum=psum, stage=stage, bc=bc, fn=fn:
                               e.activation(out=stage, in_=psum, func=fn, bias=bc, scale=scale),
                           waits=ewaits, incs=[prel])
                sdone = [(prel[0], prel[0].n)]
                if sb_out is not None:
                    sbuf, col0 = sb_out
                    dst = sbuf[:, col0 + ocg * Tn + t0: col0 + ocg * Tn + t0 + tw]
                    p.emit("vector", lambda e, d=dst, s=stage: e.tensor_copy(d, s),
                           waits=sdone, incs=[(pp.ev, 1)])
                    sdone = [(pp.ev, pp.ev.n)]
                if store and dst_dram is not None:
                    p.dma("gpsimd", dst_dram[n0g + oc * 128:n0g + oc * 128 + 128, t0:t0 + tw],
                          stage, srel[0], waits=sdone)
                else:
                    gi = cx.gctr[0] % 8192
                    cx.gctr[0] += 1
                    p.dma("gpsimd", cx.ghost[gi:gi + 1], cx.gsrc, srel[0], waits=sdone)
        wguard = pp.wguard()
    return pp.stage1024.all_done()


def projA(cx, pp, *, src_tile=None, src_stream=None, K, Tn=T, wdram, wT=False, N,
          dst_dram=None, src_ready=(), evac_cb=None, store=True, act=None,
          bias_bcast=None, extra_row=None, wres_off=0):
    p = cx.p
    nk = K // 128
    wguard = pp.wguard()
    for dc in range(nk):
        wts = wguard if dc == 0 else ()
        w_strip_load(cx, wdram, dc, 0, N,
                     cx.WRES[:, wres_off + dc * N: wres_off + dc * N + N],
                     pp.wsem, waits=wts, wT=wT)
    wtarget = pp.wsem.n
    nnch = (N + 511) // 512
    smm = []
    stdone0 = pp.stage1024.all_done()
    for tt in range(Tn // 128):
        if src_stream is not None:
            sdram, slotA, slotB, rowoff = src_stream
            si = tt % 2
            slot = (slotA, slotB)[si]
            sw = ([(pp.mm, smm[tt - 2])] if tt >= 2 else list(wguard) + list(src_ready)) + stdone0
            srcv = flat_view(sdram, rowoff * Tn + tt * 128, [[Tn, 128], [128 * Tn, nk], [1, 128]])
            p.dma("sync", slot.rearrange("p (k t) -> p k t", k=nk), srcv, pp.ss[si], waits=sw)
            pp.ssk[si] += 1
            tile = lambda dc, slot=slot: slot[:, dc * 128:(dc + 1) * 128]
            sready = [(pp.ss[si], 16 * pp.ssk[si])] + list(src_ready)
        else:
            tile = lambda dc, tt=tt: src_tile(dc, tt)
            sready = list(src_ready)
        ps, pw, prel = pp.pa1024.acquire()
        psum = ps[:, 0:N]
        waits = [(pp.wsem, wtarget)] + pw + sready
        first = True
        for nch in range(nnch):
            n0 = nch * 512
            nw = min(512, N - n0)
            lastch = (nch == nnch - 1)
            for dc in range(nk):
                def b(e, dc=dc, n0=n0, nw=nw, psum=psum, tile=tile):
                    return e.matmul(psum[:, n0:n0 + nw], tile(dc),
                                    cx.WRES[:, wres_off + dc * N + n0: wres_off + dc * N + n0 + nw],
                                    start=(dc == 0), stop=(dc == nk - 1 and extra_row is None))
                lastmm = lastch and (dc == nk - 1) and extra_row is None
                p.emit("tensor", b, waits=waits if first else (),
                       incs=[(pp.mm, 1)] if lastmm else ())
                first = False
            if extra_row is not None:
                p.emit("tensor", lambda e, n0=n0, nw=nw, psum=psum: e.matmul(
                           psum[:, n0:n0 + nw], cx.CBF[0:1, 384:512], extra_row[:, n0:n0 + nw],
                           start=False, stop=True),
                       incs=[(pp.mm, 1)] if lastch else ())
        mmv = pp.mm.n
        smm.append(mmv)
        st, sw2, srel = pp.stage1024.acquire()
        stage = st[:, 0:N]
        ewaits = [(pp.mm, mmv)] + sw2
        if evac_cb is not None:
            evac_cb(tt, psum, stage, ewaits, prel)
            sdone = [(prel[0], prel[0].n)]
        else:
            if bias_bcast is not None:
                p.emit("vector", lambda e, psum=psum, stage=stage: e.tensor_add(
                           stage, psum, bias_bcast[:, 0:N]),
                       waits=ewaits, incs=[prel])
                sdone = [(prel[0], prel[0].n)]
                if act is not None:
                    p.emit("scalar", lambda e, stage=stage, fn=act: e.activation(
                               out=stage, in_=stage, func=fn),
                           waits=sdone, incs=[(pp.evs, 1)])
                    sdone = [(pp.evs, pp.evs.n)]
            elif act is not None:
                p.emit("scalar", lambda e, psum=psum, stage=stage, fn=act:
                           e.activation(out=stage, in_=psum, func=fn),
                       waits=ewaits, incs=[prel])
                sdone = [(prel[0], prel[0].n)]
            else:
                p.emit("vector", lambda e, psum=psum, stage=stage: e.tensor_copy(stage, psum),
                       waits=ewaits, incs=[prel])
                sdone = [(prel[0], prel[0].n)]
        if store and dst_dram is not None:
            p.dma("gpsimd", dst_dram[tt * 128:(tt + 1) * 128, 0:N], stage, srel[0],
                  waits=sdone)
        else:
            gi = cx.gctr[0] % 8192
            cx.gctr[0] += 1
            p.dma("gpsimd", cx.ghost[gi:gi + 1], cx.gsrc, srel[0], waits=sdone)
    return pp.stage1024.all_done()


def vecs_to_cols(cx, pp, vecs, small_col0):
    p = cx.p
    nv = len(vecs)
    assert nv <= 120
    rows = cx.ROWS
    ld = p.sem("vc_ld")
    ld0 = ld.n
    for v, (dr, off, ln, sc) in enumerate(vecs):
        src = flat_view(dr, off, [[1, 1], [1, ln]])
        p.dma("gpsimd", rows[v:v + 1, 0:ln], src, ld, waits=[(cx.pp.mm, cx.pp.mm.n)] if v == 0 else ())
    offs = []
    off = 0
    for v, (_, _, ln, _) in enumerate(vecs):
        offs.append(off)
        off += (ln + 127) // 128
    nch_max = max((ln + 127) // 128 for _, _, ln, _ in vecs)
    tp = p.pool("at_mT", lambda: Pool(p, [cx.PT[:, 0:128]], "at_mT"))
    tpf0 = None
    for ch in range(nch_max):
        reg, wv = tp.acquire()
        p.emit("tensor", lambda e, ch=ch, reg=reg: e.transpose(
                   reg, rows[:, ch * 128:(ch + 1) * 128], ident(cx)),
               waits=[(ld, ld0 + 16 * nv), (tp.free, wv), (cx.consts_sem, 112)], incs=[(pp.mm, 1)])
        mmv = pp.mm.n
        movers = [v for v in range(nv) if ch < (vecs[v][2] + 127) // 128]
        for j, v in enumerate(movers):
            dst = cx.SMALL[:, small_col0 + offs[v] + ch: small_col0 + offs[v] + ch + 1]
            sc = vecs[v][3]
            p.emit("vector", lambda e, dst=dst, reg=reg, v=v, sc=sc: e.tensor_scalar_mul(
                       dst, reg[:, v:v + 1], sc),
                   waits=[(pp.mm, mmv)] if j == 0 else (),
                   incs=[(tp.free, 1)] if j == len(movers) - 1 else ())
    def col_fn(v, ch=0):
        cbase = small_col0 + offs[v] + ch
        return cx.SMALL[:, cbase:cbase + 1]
    return col_fn, [(tp.free, tp.uses)], off


def emit_bias_bcast(cx, pp, vec_dram, off, n, dst_ap):
    p = cx.p
    ld = p.sem("bb_ld")
    p.dma("gpsimd", cx.ROWS[0:1, 2176:2176 + n],
          flat_view(vec_dram, off, [[1, 1], [1, n]]), ld,
          waits=[(pp.mm, pp.mm.n)])
    ldt = ld.n
    done = []
    for c0 in range(0, n, 512):
        cw = min(512, n - c0)
        ps, pw, prel = pp.pb512.acquire()
        p.emit("tensor", lambda e, ps=ps, c0=c0, cw=cw: e.matmul(
                   ps[:, 0:cw], cx.CBF[0:1, 384:512],
                   cx.ROWS[0:1, 2176 + c0:2176 + c0 + cw], start=True, stop=True),
               waits=[(ld, ldt), (cx.consts_sem, 112)] + pw, incs=[(pp.mm, 1)])
        p.emit("vector", lambda e, ps=ps, c0=c0, cw=cw: e.tensor_copy(
                   dst_ap[:, c0:c0 + cw], ps[:, 0:cw]),
               waits=[(pp.mm, pp.mm.n)], incs=[prel])
        done = [(prel[0], prel[0].n)]
    return done



def emit_load_consts(cx, pp, cdram):
    p = cx.p
    sem = p.sem("consts")
    p.dma("gpsimd", cx.CBF[:, 0:128], cdram["c_ident"][:, :], sem)
    p.dma("gpsimd", cx.CBF[:, 128:256], cdram["c_J"][:, :], sem)
    p.dma("gpsimd", cx.CBF[:, 256:384], cdram["c_R"][:, :], sem)
    p.dma("gpsimd", cx.CBF[:, 384:512], cdram["c_ones"][:, :], sem)
    p.dma("gpsimd", cx.CONST[:, 0:128], cdram["c_tri"][:, :], sem)
    p.dma("gpsimd", cx.CONST[:, 128:256], cdram["c_triT"][:, :], sem)
    p.dma("gpsimd", cx.CONST[:, 256:272], cdram["c_ones"][:, 0:16], sem)
    return sem


def ident(cx):
    return cx.CBF[:, 0:128]



def emit_ln(cx, pp, x_dram, xn_dram, MC, RC, consts_sem, prev_done=()):
    p = cx.p
    SC, QC = MC + 32, MC + 48
    xs2 = [p.sem("ln_x0"), p.sem("ln_x1")]
    us0, uv0 = None, None
    used_s = p.sem("ln_us")
    used_v = p.sem("ln_uv")
    us0, uv0 = used_s.n, used_v.n
    slots = [cx.SF[:, 0:1024], cx.SF[:, 1024:2048]]
    trash = cx.SF[:, 2048:3072]
    for tt in range(NT):
        i = tt % 2
        slot = slots[i]
        waits = ([(used_s, us0 + tt - 1), (used_v, uv0 + tt - 1)] if tt >= 2 else []) + list(prev_done)
        p.dma("sync", slot, x_dram[tt * 128:(tt + 1) * 128, :], xs2[i], waits=waits)
        wv = [(xs2[i], xs2[i].n)]
        p.emit("scalar", lambda e, slot=slot, tt=tt: e.activation(
                   out=trash, in_=slot, func=AF.Square,
                   accum_out=cx.SMALL[:, QC + tt:QC + tt + 1]),
               waits=wv + [(used_s, us0 + tt)], incs=[(used_s, 1)])
        p.emit("vector", lambda e, slot=slot, tt=tt: e.reduce_sum(
                   cx.SMALL[:, SC + tt:SC + tt + 1], slot, axis=AX.X),
               waits=wv, incs=[(used_v, 1)])
    mean = cx.SMALL[:, MC:MC + 16]
    rstd = cx.SMALL[:, RC:RC + 16]
    sumc = cx.SMALL[:, SC:SC + 16]
    sqc = cx.SMALL[:, QC:QC + 16]
    vs = p.sem("ln_vs")
    v0 = vs.n
    p.emit("vector", lambda e: e.tensor_scalar_mul(mean, sumc, 1.0 / D),
           waits=[(used_s, us0 + 16), (used_v, uv0 + 16)], incs=[(vs, 1)])
    p.emit("vector", lambda e: e.tensor_scalar_mul(sqc, sqc, 1.0 / D),
           waits=[(vs, v0 + 1)], incs=[(vs, 1)])
    p.emit("vector", lambda e: e.tensor_mul(sumc, mean, mean),
           waits=[(vs, v0 + 2)], incs=[(vs, 1)])
    p.emit("vector", lambda e: e.tensor_sub(sqc, sqc, sumc),
           waits=[(vs, v0 + 3)], incs=[(vs, 1)])
    p.emit("vector", lambda e: e.tensor_scalar_add(sqc, sqc, 1e-5),
           waits=[(vs, v0 + 4)], incs=[(vs, 1)])
    p.emit("scalar", lambda e: e.activation(out=sumc, in_=sqc, func=AF.Sqrt),
           waits=[(vs, v0 + 5)], incs=[(vs, 1)])
    p.emit("vector", lambda e: e.reciprocal(rstd, sumc),
           waits=[(vs, v0 + 6)], incs=[(vs, 1)])
    tmslots = [cx.SBF[:, 0:1024], cx.SBF[:, 1024:2048]]
    fmstage = [cx.SBF[:, 2048:3072], cx.SBF[:, 3072:4096]]
    tmdone = p.sem("ln_tm")
    tpdone = p.sem("ln_tp")
    cpdone = p.sem("ln_cp")
    t0c, tp0, cp0 = tmdone.n, tpdone.n, cpdone.n
    st2 = [p.sem("ln_st0"), p.sem("ln_st1")]
    for tt in range(NT):
        i = tt % 2
        slot = slots[i]
        tm = tmslots[i]
        fs = fmstage[i]
        waits = [(used_s, us0 + 16), (used_v, uv0 + 16), (vs, v0 + 7)] if tt < 2 else [(tmdone, t0c + tt - 1)]
        p.dma("sync", slot, x_dram[tt * 128:(tt + 1) * 128, :], xs2[i], waits=waits)
        p.emit("vector", lambda e, slot=slot, tm=tm, tt=tt: e.tensor_scalar(
                   out=tm, in0=slot, scalar1=cx.SMALL[:, MC + tt:MC + tt + 1],
                   scalar2=cx.SMALL[:, RC + tt:RC + tt + 1],
                   op0=OP.subtract, op1=OP.mult),
               waits=[(xs2[i], xs2[i].n)] + ([(cpdone, cp0 + tt - 1)] if tt >= 2 else [(consts_sem, 112)]),
               incs=[(tmdone, 1)])
        for dc in range(ND):
            p.emit("tensor", lambda e, tm=tm, dc=dc: e.transpose(
                       cx.PT[:, dc * 128:(dc + 1) * 128], tm[:, dc * 128:(dc + 1) * 128], ident(cx)),
                   waits=([(tmdone, t0c + tt + 1)] if dc == 0 else []) + ([(cpdone, cp0 + tt)] if (dc == 0 and tt >= 1) else []),
                   incs=[(tpdone, 1)] if dc == ND - 1 else ())
        p.emit("vector", lambda e, fs=fs: e.tensor_copy(fs, cx.PT[:, :]),
               waits=[(tpdone, tp0 + tt + 1)] + ([(st2[i], st2[i].n)] if st2[i].n > 0 else []),
               incs=[(cpdone, 1)])
        dst = flat_view(xn_dram, tt * 128, [[T, 128], [128 * T, ND], [1, 128]])
        p.dma("gpsimd", dst, fs.rearrange("p (k t) -> p k t", k=ND), st2[i],
              waits=[(cpdone, cp0 + tt + 1)])
    return [(st2[0], st2[0].n), (st2[1], st2[1].n)]



def emit_update(cx, pp, x_dram, s_dram, m_dram, xout_dram, pre_done=(), m2_dram=None):
    p = cx.p
    ld2 = [p.sem("up_ld0"), p.sem("up_ld1")]
    st2 = [p.sem("up_st0"), p.sem("up_st1")]
    sx = [cx.SF[:, 0:1024], cx.SF[:, 3072:4096]]
    ss = [cx.SF[:, 1024:2048], cx.SF[:, 4096:5120]]
    sm = [cx.SF[:, 2048:3072], cx.SF[:, 5120:6144]]
    sm2 = [cx.SF[:, 6144:7168], cx.SF[:, 7168:8192]]
    evs_ = p.sem("up_evs")
    evv_ = p.sem("up_evv")
    evm_ = p.sem("up_evm")
    eva_ = p.sem("up_eva")
    e0, v0, em0, a0 = evs_.n, evv_.n, evm_.n, eva_.n
    for tt in range(NT):
        i = tt % 2
        waits = ([(evv_, v0 + tt - 1), (st2[i], st2[i].n)] if tt >= 2 else []) \
            + (list(pre_done) if tt == 0 else [])
        r0, r1 = tt * 128, (tt + 1) * 128
        p.dma("sync", sx[i], x_dram[r0:r1, :], ld2[i], waits=waits)
        p.dma("sync", ss[i], s_dram[r0:r1, :], ld2[i])
        p.dma("sync", sm[i], m_dram[r0:r1, :], ld2[i])
        if m2_dram is not None:
            p.dma("sync", sm2[i], m2_dram[r0:r1, :], ld2[i])
            p.emit("vector", lambda e, i=i: e.tensor_add(sm[i], sm[i], sm2[i]),
                   waits=[(ld2[i], ld2[i].n)], incs=[(eva_, 1)])
            swait = [(eva_, a0 + tt + 1)]
        else:
            swait = [(ld2[i], ld2[i].n)]
        p.emit("scalar", lambda e, i=i: e.activation(out=sm[i], in_=sm[i], func=AF.Silu),
               waits=swait, incs=[(evs_, 1)])
        p.emit("vector", lambda e, i=i: e.tensor_mul(ss[i], ss[i], sm[i]),
               waits=[(evs_, e0 + tt + 1)], incs=[(evm_, 1)])
        p.emit("vector", lambda e, i=i: e.tensor_add(ss[i], ss[i], sx[i]),
               waits=[(evm_, em0 + tt + 1)], incs=[(evv_, 1)])
        p.dma("gpsimd", xout_dram[r0:r1, :], ss[i], st2[i], waits=[(evv_, v0 + tt + 1)])
    return [(st2[0], st2[0].n), (st2[1], st2[1].n)]



LAYER_EMITTERS = {}

INPUT_SPECS = {
    "L0_attn_Wq": (1024, 1024), "L0_attn_bq": (1024,), "L0_attn_lnq_g": (1024,),
    "L0_attn_lnq_b": (1024,), "L0_attn_Wk": (1024, 1024), "L0_attn_bk": (1024,),
    "L0_attn_lnk_g": (1024,), "L0_attn_lnk_b": (1024,), "L0_attn_Wv": (1024, 1024),
    "L0_attn_bv": (1024,), "L0_attn_lnv_g": (1024,), "L0_attn_lnv_b": (1024,),
    "L0_attn_Wphi_q": (1024, 1024), "L0_attn_bphi_q": (1024,),
    "L0_attn_Wphi_k": (1024, 1024), "L0_attn_bphi_k": (1024,),
    "L0_attn_lna_g": (1024,), "L0_attn_lna_b": (1024,),
    "L0_attn_W1": (1024, 4096), "L0_attn_b1": (4096,), "L0_attn_W2": (4096, 1024),
    "L0_attn_b2": (1024,), "L0_attn_gamma": (), "L0_attn_conv_w": (1024, 64, 5),
    "L0_attn_conv_b": (1024,), "L0_ffn_gate_w": (16, 1024),
    "L0_ffn_expert_w": (16, 1024, 1024), "L0_ffn_expert_b": (16, 1024),
    "L1_attn_norm1_scale": (1024,), "L1_attn_norm2_scale": (1024,),
    "L1_attn_Wq": (1024, 1024), "L1_attn_bq": (1024,), "L1_attn_Wk": (1024, 1024),
    "L1_attn_bk": (1024,), "L1_attn_Wv": (1024, 1024), "L1_attn_bv": (1024,),
    "L1_attn_Wg": (1024, 1024), "L1_attn_bg": (1024,), "L1_attn_Wo": (1024, 1024),
    "L1_attn_bo": (1024,), "L1_attn_router_W": (1024, 8), "L1_attn_router_b": (8,),
    "L1_attn_W1": (8, 1024, 4096), "L1_attn_b1": (8, 4096),
    "L1_attn_W2": (8, 4096, 1024), "L1_attn_b2": (8, 1024),
    "L1_ffn_w_gate": (2816, 1024), "L1_ffn_w_up": (2816, 1024), "L1_ffn_w_down": (1024, 2816),
    "L2_attn_Wq": (1024, 1024), "L2_attn_bq": (1024,), "L2_attn_Wk": (1024, 1024),
    "L2_attn_bk": (1024,), "L2_attn_Wv": (1024, 1024), "L2_attn_bv": (1024,),
    "L2_attn_Wo": (1024, 1024), "L2_attn_bo": (1024,), "L2_attn_rel_bias": (32, 16),
    "L2_ffn_g1": (1024,), "L2_ffn_Wq": (1024, 1024), "L2_ffn_Wdkv": (1024, 512),
    "L2_ffn_Wuk": (512, 1024), "L2_ffn_Wuv": (512, 1024), "L2_ffn_Wo": (1024, 1024),
    "L2_ffn_g2": (1024,), "L2_ffn_gate_w": (1024, 8),
    "L2_ffn_w1": (8, 1024, 2048), "L2_ffn_w2": (8, 2048, 1024),
    "L3_attn_Wq": (1024, 1024), "L3_attn_bq": (1024,), "L3_attn_Wk": (1024, 1024),
    "L3_attn_bk": (1024,), "L3_attn_Wv": (1024, 1024), "L3_attn_bv": (1024,),
    "L3_attn_Wo": (1024, 1024), "L3_attn_bo": (1024,),
    "L3_ffn_Wr": (8, 1024), "L3_ffn_br": (8,), "L3_ffn_W1": (8, 4096, 1024),
    "L3_ffn_b1": (8, 4096), "L3_ffn_W2": (8, 1024, 4096), "L3_ffn_b2": (8, 1024),
}

CONST_SPECS = {
    "c_ident": (128, 128), "c_ones": (128, 128), "c_J": (128, 128), "c_R": (128, 128),
    "c_ropecos": (128, 2048), "c_ropesin": (128, 2048), "c_tri": (128, 128),
    "c_triT": (128, 128), "c_G": (32, 4096), "c_Gr": (32, 4096),
}

TRANSPOSED = {
    "L3_attn_Wq": ((1024, 1024), (1, 0)),
    "L3_attn_Wk": ((1024, 1024), (1, 0)),
    "L3_attn_Wv": ((1024, 1024), (1, 0)),
    "L3_attn_Wo": ((1024, 1024), (1, 0)),
    "L3_ffn_Wr": ((1024, 8), (1, 0)),
    "L3_ffn_W1": ((8, 1024, 4096), (0, 2, 1)),
    "L3_ffn_W2": ((8, 4096, 1024), (0, 2, 1)),
    "L0_ffn_expert_w": ((16, 1024, 1024), (0, 2, 1)),
    "L0_ffn_gate_w": ((1024, 16), (1, 0)),
    "L0_attn_conv_w": ((64, 5, 1024), (1, 2, 0)),
}


def eff_name(n):
    return ("t" + n) if n in TRANSPOSED else n


def eff_shape(n):
    return TRANSPOSED[n][0] if n in TRANSPOSED else INPUT_SPECS[n]


BF_WEIGHTS = [n for n, s in INPUT_SPECS.items()
              if len(s) >= 2 and int(np.prod(s)) >= 1024]


def build_program(layers, dbg_names=()):
    nc = bass.Bass("TRN2", target_bir_lowering=False, debug=False)
    inp = {}
    inp["x_in"] = nc.dram_tensor("x_in", [T, D], F32, kind="ExternalInput").ap()
    for name, shape in INPUT_SPECS.items():
        if any(name.startswith(f"L{i}_") for i in layers):
            en, es_ = eff_name(name), eff_shape(name)
            shp = list(es_) if es_ else [1]
            inp[name] = nc.dram_tensor(en, shp, F32, kind="ExternalInput").ap()
    cdram = {}
    for name, shape in CONST_SPECS.items():
        cdram[name] = nc.dram_tensor(name, list(shape), F32, kind="ExternalInput").ap()
    y = nc.dram_tensor("y", [T, D], F32, kind="ExternalOutput").ap()

    scratch = {}
    def dram(name, shape, dt=BF16):
        if name not in scratch:
            scratch[name] = nc.dram_tensor("scr_" + name, list(shape), dt, kind="Internal").ap()
        return scratch[name]

    es = ExitStack()
    with es:
        p = P(nc, es)
        cx = build_ctx(nc, es, p)
        cx.ghost = nc.dram_tensor("scr_ghost", [8192], F32, kind="Internal").ap()
        cx.gsrc = flat_view(cdram["c_ones"], 0, [[1, 1]])
        cx.gctr = [0]
        cx.dram = dram
        cx.inp = inp
        cx.cdram = cdram
        pp = ProjPools(cx)
        cx.pp = pp
        consts_sem = emit_load_consts(cx, pp, cdram)
        cx.consts_sem = consts_sem
        wb = {}
        pcsem = p.sem("precast")
        for name in BF_WEIGHTS:
            if name in inp:
                shape = eff_shape(name)
                n_el = int(np.prod(shape))
                wdst = dram("bf_" + name, [n_el], BF16)
                CH = 1 << 21
                off = 0
                while off < n_el:
                    ln = min(CH, n_el - off)
                    s = flat_view(inp[name], off, [[1, ln]])
                    d2 = flat_view(wdst, off, [[1, ln]])
                    p.dma("gpsimd", d2, s, pcsem)
                    off += ln
                wb[name] = wdst
        cx.wb = wb
        cx.pcsem = pcsem
        cx.pctarget = pcsem.n

        def wbf2(name, shape):
            return flat_view(wb[name], 0, [[shape[1], shape[0]], [1, shape[1]]])
        cx.wbf2 = wbf2

        xcur = dram("x0", [T, D], F32)
        xin_sem = p.sem()
        p.dma("sync", xcur[:, :], inp["x_in"][:, :], xin_sem)
        prev = [(xin_sem, 16)]
        for li, L in enumerate(layers):
            xnext = dram(f"x{li + 1}", [T, D], F32)
            is_last = (li == len(layers) - 1)
            fin = LAYER_EMITTERS[L](cx, pp, xcur, y if is_last else xnext, prev)
            prev = fin
            xcur = xnext
        dbgo = {}
        for nm in dbg_names:
            src = scratch[nm]
            shp = list(src.tensor.shape)
            dt = src.dtype
            o = nc.dram_tensor("dbg_" + nm, shp, dt, kind="ExternalOutput").ap()
            dsem = p.sem()
            p.dma("gpsimd", o[...] if False else o, src, dsem, waits=prev)
            prev = prev + [(dsem, 16)]
        p.wait("sync", prev)
        p.wait("gpsimd", prev)
        p.finalize_block()
    return nc


def shard_inputs(inputs):
    x = np.ascontiguousarray(inputs["L0_attn_x"], dtype=np.float32)
    return [np.ascontiguousarray(x[c]) for c in range(NCORES)]


def kernel(**inputs):
    groups = [[0, 1, 2, 3]]
    consts = host_constants()
    xs = shard_inputs(inputs)
    for g in groups:
        nc = build_program(g)
        wnames = [n for n in INPUT_SPECS if any(n.startswith(f"L{i}_") for i in g)]
        in_maps = []
        arrs = {}
        for n in wnames:
            a = np.asarray(inputs[n], dtype=np.float32)
            if n in TRANSPOSED:
                a = np.ascontiguousarray(a.transpose(TRANSPOSED[n][1]))
            else:
                a = np.ascontiguousarray(a)
            if a.ndim == 0:
                a = a.reshape(1)
            arrs[eff_name(n)] = a
        for c in range(NCORES):
            m = {"x_in": xs[c]}
            m.update(arrs)
            m.update(consts)
            in_maps.append(m)
        res = run_bass_kernel_spmd(nc, in_maps, core_ids=list(range(NCORES)))
        xs = [res.results[c]["y"] for c in range(NCORES)]
    return np.stack(xs, axis=0)


def emit_test_ln(cx, pp, x_dram, xout, prev):
    p = cx.p
    xn = cx.dram("xn_t", [D, T], BF16)
    st = emit_ln(cx, pp, x_dram, xn, 0, 16, cx.consts_sem, prev_done=prev)
    cp = p.sem()
    p.dma("gpsimd", xout[:, :], x_dram[:, :], cp, waits=st)
    return [(cp, 16)]


LAYER_EMITTERS["test_ln"] = emit_test_ln



def emit_router(cx, pp, *, src_tile, src_ready, wdram, wT, E, bias_row_dram, mode,
                coeff_col0, extra_waits=()):
    p = cx.p
    nk = D // 128
    woff = 21504
    wsem = pp.wsem
    wguard = pp.wguard()
    for dc in range(nk):
        w_strip_load(cx, wdram, dc, 0, E, cx.WRES[:, woff + dc * E: woff + (dc + 1) * E],
                     wsem, waits=wguard if dc == 0 else (), wT=wT)
    have_bias = bias_row_dram is not None
    brsem = p.sem("rt_br")
    if have_bias:
        p.dma("gpsimd", cx.ROWS[0:1, 3200:3200 + E],
              flat_view(bias_row_dram, 0, [[1, 1], [1, E]]), brsem)
    wtarget = wsem.n
    brtarget = brsem.n
    va = p.sem("rt_va")
    sa = p.sem("rt_sa")
    for tt in range(NT):
        ps, pw, prel = pp.pa1024.acquire()
        psum = ps[:, 0:E]
        waits = [(wsem, wtarget), (brsem, brtarget)] + pw + list(src_ready) + list(extra_waits)
        for dc in range(nk):
            p.emit("tensor", lambda e, dc=dc, psum=psum, tt=tt: e.matmul(
                       psum, src_tile(dc, tt),
                       cx.WRES[:, woff + dc * E: woff + (dc + 1) * E],
                       start=(dc == 0), stop=(dc == nk - 1 and not have_bias)),
                   waits=waits if dc == 0 else (),
                   incs=[(pp.mm, 1)] if (dc == nk - 1 and not have_bias) else ())
        if have_bias:
            p.emit("tensor", lambda e, psum=psum: e.matmul(
                       psum, cx.CBF[0:1, 384:512], cx.ROWS[0:1, 3200:3200 + E],
                       start=False, stop=True), incs=[(pp.mm, 1)])
        mmv = pp.mm.n
        lg = cx.SMALL[:, 3584:3584 + E]
        ex = cx.SMALL[:, 3648:3648 + E]
        m1 = cx.SMALL[:, 3712:3713]
        m2 = cx.SMALL[:, 3713:3714]
        sm = cx.SMALL[:, 3714:3715]
        co = cx.SMALL[:, coeff_col0 + tt * E: coeff_col0 + (tt + 1) * E]
        v0 = va.n
        p.emit("vector", lambda e, psum=psum, lg=lg: e.tensor_copy(lg, psum),
               waits=[(pp.mm, mmv)], incs=[prel])
        p.emit("vector", lambda e, lg=lg, m1=m1: e.reduce_max(m1, lg, axis=AX.X, negate=True),
               waits=[(prel[0], prel[0].n)], incs=[(va, 1)])
        p.emit("scalar", lambda e, lg=lg, ex=ex, m1=m1: e.activation(
                   out=ex, in_=lg, func=AF.Exp, bias=m1, accum_out=sm),
               waits=[(va, v0 + 1)], incs=[(sa, 1)])
        if mode in ("soft_raw", "soft_renorm"):
            p.emit("vector", lambda e, ex=ex, sm=sm: e.reciprocal(cx.SMALL[:, 3715:3716], sm),
                   waits=[(sa, sa.n)], incs=[(va, 1)])
            p.emit("vector", lambda e, ex=ex: e.tensor_scalar(
                       out=ex, in0=ex, scalar1=cx.SMALL[:, 3715:3716], scalar2=1.0,
                       op0=OP.mult, op1=OP.mult),
                   waits=[(va, v0 + 2)], incs=[(va, 1)])
        p.emit("vector", lambda e, ex=ex, m1=m1: e.reduce_max(m1, ex, axis=AX.X),
               waits=[(va, va.n), (sa, sa.n)], incs=[(va, 1)])
        p.emit("vector", lambda e, ex=ex, m1=m1: e.tensor_scalar(
                   out=cx.SMALL[:, 3716:3716 + E],
                   in0=ex, scalar1=m1, scalar2=-1e30, op0=OP.is_ge, op1=OP.mult),
               waits=[(va, va.n)], incs=[(va, 1)])
        p.emit("vector", lambda e, ex=ex: e.tensor_add(
                   cx.SMALL[:, 3716:3716 + E], cx.SMALL[:, 3716:3716 + E], ex),
               waits=[(va, va.n)], incs=[(va, 1)])
        p.emit("vector", lambda e: e.reduce_max(m2, cx.SMALL[:, 3716:3716 + E], axis=AX.X),
               waits=[(va, va.n)], incs=[(va, 1)])
        p.emit("vector", lambda e, ex=ex, co=co: e.tensor_scalar(
                   out=co, in0=ex, scalar1=m2, scalar2=1.0, op0=OP.is_ge, op1=OP.mult),
               waits=[(va, va.n)], incs=[(va, 1)])
        p.emit("vector", lambda e, ex=ex, co=co: e.tensor_mul(co, co, ex),
               waits=[(va, va.n)], incs=[(va, 1)])
        if mode in ("soft_renorm", "logit_soft"):
            p.emit("vector", lambda e, m1=m1, m2=m2: e.tensor_add(sm, m1, m2),
                   waits=[(va, va.n)], incs=[(va, 1)])
            p.emit("vector", lambda e, sm=sm: e.reciprocal(cx.SMALL[:, 3715:3716], sm),
                   waits=[(va, va.n)], incs=[(va, 1)])
            p.emit("vector", lambda e, co=co: e.tensor_scalar(
                       out=co, in0=co, scalar1=cx.SMALL[:, 3715:3716], scalar2=1.0,
                       op0=OP.mult, op1=OP.mult),
                   waits=[(va, va.n)], incs=[(va, 1)])
    return [(va, va.n)]



def emit_softmax_attn(cx, pp, *, q_dram, k_dram, vext_dram, out_dram, causal,
                      v8q_dram=None, v8t_dram=None, src_ready=()):
    p = cx.p
    bias = v8q_dram is not None
    QP, KP = cx.SBF[:, 0:2048], cx.SBF[:, 2048:4096]
    QE, KE = cx.SBF[:, 4096:6144], cx.SBF[:, 6144:8192]
    PS = [cx.SBF[:, 8192:10240], cx.SBF[:, 10240:12288]]
    VE = [cx.SBF[:, 12288:13328], cx.SBF[:, 13328:14368]]
    TQ = [cx.SBF[:, 14368:14880], cx.SBF[:, 14880:15392]]
    OUTB = [cx.ROWS[64:128, 0:512], cx.ROWS[64:128, 512:1024]]
    SFST = [cx.SF[0:64, 0:512], cx.SF[0:64, 512:1024]]
    MGB = cx.ROWS[:, 3584:3600]
    MRS = cx.ROWS[0:16, 3600:3728]
    DEN = cx.SMALL[0:1, 1024:3072]
    DENB = cx.ROWS[0:1, 0:2048]
    ONE1 = cx.ROWS[0:1, 2048:2176]
    GRID = cx.SMALL[:, 976:980]
    NEGM = cx.SMALL[:, 960:976]
    tri = cx.CONST[:, 0:128]
    triT = cx.CONST[:, 128:256]
    J = cx.CBF[:, 128:256]

    qk = p.sem("at_qk")
    ve2 = [p.sem("at_ve0"), p.sem("at_ve1")]
    tq2 = [p.sem("at_tq0"), p.sem("at_tq1")]
    st2 = [p.sem("at_st0"), p.sem("at_st1")]
    sk0 = [st2[0].n // 16, st2[1].n // 16]
    mq = p.sem("at_mq")
    va = p.sem("at_va")
    kslot = p.pool("at_kslot", lambda: Pool(p, [0, 1], "at_ks"))
    tqp = p.pool("at_tqp", lambda: Pool(p, [0, 1], "at_tqp"))
    tpool = p.pool("at_mT", lambda: Pool(p, [cx.PT[:, 0:128]], "at_mT"))

    p.emit("vector", lambda e: e.memset(KE[64:65, :], 1.0), incs=[(va, 1)])
    p.emit("vector", lambda e: e.memset(ONE1, 1.0), incs=[(va, 1)])

    out_slot = 0
    for h in range(16):
        hp, base = h // 2, (h % 2) * 64
        if h % 2 == 0:
            w = [(va, va.n)] + (list(src_ready) if h == 0 else [])
            p.dma("sync", QP, q_dram[hp * 128:(hp + 1) * 128, :], qk, waits=w)
            p.dma("sync", KP, k_dram[hp * 128:(hp + 1) * 128, :], qk)
        qktarget = qk.n
        for qt in range(NT):
            krange = (qt + 1) * 128 if causal else T
            nch = (krange + 511) // 512
            for kc in range(nch):
                k0 = kc * 512
                kw = min(512, krange - k0)
                ps, pw, prel = pp.pb512.acquire()
                psum = ps[:, 0:kw]
                p.emit("tensor", lambda e, psum=psum, qt=qt, k0=k0, kw=kw, base=base:
                           e.matmul(psum, QP[base:base + 64, qt * 128:(qt + 1) * 128],
                                    KP[base:base + 64, k0:k0 + kw],
                                    start=True, stop=not bias),
                       waits=[(qk, qktarget)] + pw,
                       incs=[(pp.mm, 1)] if not bias else ())
                if bias:
                    tsl, tw_, trel = None, None, None
                    sl, twv = tqp.acquire()
                    tq = TQ[sl][:, 0:kw]
                    b0 = 1920 - qt * 128 + k0
                    srcb = flat_view(v8q_dram, h * 4096 + b0, [[1, 128], [1, kw]])
                    p.dma("sync", tq, srcb, tq2[sl], waits=[(tqp.free, twv)])
                    p.emit("tensor", lambda e, psum=psum, tq=tq:
                               e.matmul(psum, J, tq, start=False, stop=True),
                           waits=[(tq2[sl], tq2[sl].n)], incs=[(pp.mm, 1)])
                mmv = pp.mm.n
                mwait = [(pp.mm, mmv)]
                if causal and k0 + kw == krange:
                    doff = kw - 128
                    p.emit("vector", lambda e, psum=psum, doff=doff:
                               e.tensor_add(psum[:, doff:doff + 128], psum[:, doff:doff + 128], tri),
                           waits=mwait, incs=[(va, 1)])
                    mwait = [(va, va.n)]
                p.emit("vector", lambda e, psum=psum, kc=kc:
                           e.reduce_max(GRID[:, kc:kc + 1], psum, axis=AX.X),
                       waits=mwait + ([(va, va.n)] if kc == 0 else []), incs=[prel])
                if bias:
                    p.emit("gpsimd", lambda e, tq=tq: e.tensor_copy(tq[0:1, 0:1], tq[0:1, 0:1]),
                           waits=[(pp.mm, mmv)], incs=[(tqp.free, 1)])
            p.emit("vector", lambda e, qt=qt, nch=nch: e.reduce_max(
                       NEGM[:, qt:qt + 1], GRID[:, 0:nch], axis=AX.X, negate=True),
                   waits=pp.pb512.all_done(), incs=[(va, 1)])
        p.emit("vector", lambda e: e.tensor_copy(MGB, NEGM), waits=[(va, va.n)], incs=[(va, 1)])
        reg, twv = tpool.acquire()
        p.emit("tensor", lambda e, reg=reg: e.transpose(reg[0:16, :], MGB, ident(cx)),
               waits=[(va, va.n), (tpool.free, twv)], incs=[(pp.mm, 1)])
        p.emit("vector", lambda e, reg=reg: e.tensor_copy(MRS, reg[0:16, :]),
               waits=[(pp.mm, pp.mm.n)], incs=[(tpool.free, 1)])
        p.dma("sync", QE[64:65, :], MRS, mq,
              waits=[(tpool.free, tpool.uses)] + pp.pb512.all_done())
        p.emit("vector", lambda e, base=base: e.tensor_copy(QE[0:64, :], QP[base:base + 64, :]),
               waits=pp.pb512.all_done() + [(qk, qktarget)], incs=[(va, 1)])
        p.emit("vector", lambda e, base=base: e.tensor_copy(KE[0:64, :], KP[base:base + 64, :]),
               incs=[(va, 1)])
        qetarget = va.n
        mqtarget = mq.n
        pa_ready = [(va, va.n)]
        for kt in range(NT):
            qlo = kt * 128 if causal else 0
            sl, kwv = kslot.acquire()
            Pslot, VEslot = PS[sl], VE[sl]
            p.dma("sync", VEslot, vext_dram[kt * 128:(kt + 1) * 128, :], ve2[sl],
                  waits=[(kslot.free, kwv)])
            vetarget = ve2[sl].n
            qcs = []
            for qc in range(qlo // 512, 4):
                q0 = max(qlo, qc * 512)
                qw = (qc + 1) * 512 - q0
                qcs.append((q0, qw, qc))
            first_exp = True
            for (q0, qw, qc) in qcs:
                ps, pw, prel = pp.pb512.acquire()
                psum = ps[:, 0:qw]
                p.emit("tensor", lambda e, psum=psum, kt=kt, q0=q0, qw=qw:
                           e.matmul(psum, KE[0:65, kt * 128:(kt + 1) * 128], QE[0:65, q0:q0 + qw],
                                    start=True, stop=not bias),
                       waits=[(va, qetarget), (mq, mqtarget)] + pw,
                       incs=[(pp.mm, 1)] if not bias else ())
                if bias:
                    sl2, twv2 = tqp.acquire()
                    tq = TQ[sl2][:, 0:qw]
                    b0 = 1920 + q0 - kt * 128
                    srcb = flat_view(v8t_dram, h * 4096 + b0, [[1, 128], [1, qw]])
                    p.dma("sync", tq, srcb, tq2[sl2], waits=[(tqp.free, twv2)])
                    p.emit("tensor", lambda e, psum=psum, tq=tq:
                               e.matmul(psum, J, tq, start=False, stop=True),
                           waits=[(tq2[sl2], tq2[sl2].n)], incs=[(pp.mm, 1)])
                mmv = pp.mm.n
                ewait = [(pp.mm, mmv)]
                if causal and q0 < kt * 128 + 128 and kt * 128 >= q0:
                    off = kt * 128 - q0
                    p.emit("vector", lambda e, psum=psum, off=off:
                               e.tensor_add(psum[:, off:off + 128], psum[:, off:off + 128], triT),
                           waits=ewait, incs=[(va, 1)])
                    ewait = [(va, va.n)]
                p.emit("scalar", lambda e, psum=psum, Pslot=Pslot, q0=q0, qw=qw:
                           e.activation(out=Pslot[:, q0:q0 + qw], in_=psum, func=AF.Exp),
                       waits=ewait + ([(kslot.free, kwv)] if first_exp else []),
                       incs=[prel])
                first_exp = False
                if bias:
                    p.emit("gpsimd", lambda e, tq=tq: e.tensor_copy(tq[0:1, 0:1], tq[0:1, 0:1]),
                           waits=[(pp.mm, mmv)], incs=[(tqp.free, 1)])
            exps_done = pp.pb512.all_done()
            for j, (q0, qw, qc) in enumerate(qcs):
                lastkt_for_qc = min(15, 4 * qc + 3) if causal else 15
                p.emit("tensor", lambda e, q0=q0, qw=qw, kt=kt, VEslot=VEslot, Pslot=Pslot, h=h,
                                  lk=lastkt_for_qc:
                           e.matmul(cx.PA[0:65, q0:q0 + qw], VEslot[:, h * 65:h * 65 + 65],
                                    Pslot[:, q0:q0 + qw],
                                    start=(kt == 0), stop=(kt == lk), skip_group_check=True),
                       waits=(exps_done + [(ve2[sl], vetarget)] + (pa_ready if kt == 0 else []))
                             if j == 0 else (),
                       incs=[(kslot.free, 1)] if j == len(qcs) - 1 else ())
        p.emit("vector", lambda e: e.tensor_copy(DEN, cx.PA[64:65, 0:2048]),
               waits=[(kslot.free, kslot.uses)], incs=[(va, 1)])
        p.emit("vector", lambda e: e.reciprocal(DEN, DEN), waits=[(va, va.n)], incs=[(va, 1)])
        p.emit("vector", lambda e: e.tensor_copy(DENB, DEN), waits=[(va, va.n)], incs=[(va, 1)])
        dentarget = va.n
        for qc in range(4):
            ps, pw, prel = pp.pb512.acquire()
            bc = ps[0:64, 0:512]
            p.emit("tensor", lambda e, bc=bc, qc=qc: e.matmul(
                       bc, ONE1[:, 0:64], DENB[:, qc * 512:(qc + 1) * 512],
                       start=True, stop=True),
                   waits=[(va, dentarget)] + pw, incs=[(pp.mm, 1)])
            mmv = pp.mm.n
            so = out_slot % 2
            out_slot += 1
            p.emit("vector", lambda e, so=so, qc=qc: e.tensor_copy(
                       SFST[so], cx.PA[0:64, qc * 512:(qc + 1) * 512]),
                   waits=([(st2[so], st2[so].n)] if st2[so].n > 0 else []) + [(pp.mm, mmv)],
                   incs=[(va, 1)])
            p.emit("vector", lambda e, so=so, bc=bc: e.tensor_mul(OUTB[so], SFST[so], bc),
                   waits=[(va, va.n)], incs=[prel])
            p.dma("gpsimd", out_dram[h * 64:(h + 1) * 64, qc * 512:(qc + 1) * 512], OUTB[so],
                  st2[so], waits=[(prel[0], prel[0].n)])
    return [(st2[0], st2[0].n), (st2[1], st2[1].n)]



def emit_L3(cx, pp, x_dram, xout, prev):
    p = cx.p
    dram, wbf2, inp = cx.dram, cx.wbf2, cx.inp
    xn = dram("l3_xn", [D, T], BF16)
    lnst = emit_ln(cx, pp, x_dram, xn, 0, 16, cx.consts_sem, prev_done=prev)
    xl = p.sem("lf")
    load_fm(cx, xn, D, T, cx.SRC, 0, xl, waits=lnst + [(cx.pcsem, cx.pctarget)])
    srcr = [(xl, xl.n)]
    srcB = lambda dc: cx.SRC[:, dc * T:(dc + 1) * T]
    srcA = lambda dc, tt: cx.SRC[:, dc * T + tt * 128: dc * T + tt * 128 + 128]
    vecs = [(inp["L3_attn_bq"], 0, 1024, 0.125), (inp["L3_attn_bk"], 0, 1024, 1.0)]
    for e in range(8):
        vecs.append((inp["L3_ffn_b1"], e * 4096, 4096, 1.0))
    colf, bdone, _ = vecs_to_cols(cx, pp, vecs, 64)
    qd = dram("l3_q", [D, T], BF16)
    kd = dram("l3_k", [D, T], BF16)
    d1 = projB(cx, pp, src_tile=srcB, K=D, Tn=T, wdram=wbf2("L3_attn_Wq", (1024, 1024)),
               wT=False, N=D, dst_dram=qd, src_ready=srcr + bdone,
               bias_col=lambda oc: colf(0, oc), scale=0.125)
    d2 = projB(cx, pp, src_tile=srcB, K=D, Tn=T, wdram=wbf2("L3_attn_Wk", (1024, 1024)),
               wT=False, N=D, dst_dram=kd, src_ready=srcr,
               bias_col=lambda oc: colf(1, oc))
    vext = dram("l3_vext", [T, 1040], BF16)
    bvb = emit_bias_bcast(cx, pp, inp["L3_attn_bv"], 0, 1024, cx.SF[:, 6144:7168])
    vsl = [cx.SF[:, 0:1040], cx.SF[:, 1040:2080]]
    onesem = p.sem("l3_ones")
    for i in range(2):
        ov = vsl[i].rearrange("p (a b) -> p a b", b=65)[:, :, 64:65]
        p.emit("vector", lambda e, ov=ov: e.memset(ov, 1.0), incs=[(onesem, 1)])
    onest = onesem.n
    vst2 = [p.sem("vx_st0"), p.sem("vx_st1")]
    def vext_cb(tt, psum, stage, waits, prel):
        i = tt % 2
        slot = vsl[i]
        sv = slot.rearrange("p (a b) -> p a b", b=65)[:, :, 0:64]
        p.emit("vector", lambda e, psum=psum, sv=sv: e.tensor_add(
                   sv, psum.rearrange("p (a b) -> p a b", b=64),
                   cx.SF[:, 6144:7168].rearrange("p (a b) -> p a b", b=64)),
               waits=list(waits) + bvb + [(onesem, onest)] +
                     ([(vst2[i], vst2[i].n)] if vst2[i].n > 0 else []),
               incs=[prel])
        p.dma("gpsimd", vext[tt * 128:(tt + 1) * 128, :], slot, vst2[i],
              waits=[(prel[0], prel[0].n)])
    d3 = projA(cx, pp, src_tile=srcA, K=D, Tn=T, wdram=wbf2("L3_attn_Wv", (1024, 1024)),
               wT=False, N=D, src_ready=srcr, evac_cb=vext_cb, store=False)
    attnd = dram("l3_attn", [D, T], BF16)
    adone = emit_softmax_attn(cx, pp, q_dram=qd, k_dram=kd, vext_dram=vext,
                              out_dram=attnd, causal=True,
                              src_ready=d1 + d2 + [(vst2[0], vst2[0].n), (vst2[1], vst2[1].n)])
    al = p.sem("lf")
    load_fm(cx, attnd, D, T, cx.SBF, 0, al, waits=adone)
    bob = emit_bias_bcast(cx, pp, inp["L3_attn_bo"], 0, 1024, cx.SF[:, 6144:7168])
    sd = dram("l3_s", [T, D], F32)
    sdone = projA(cx, pp,
                  src_tile=lambda dc, tt: cx.SBF[:, dc * T + tt * 128: dc * T + tt * 128 + 128],
                  K=D, Tn=T, wdram=wbf2("L3_attn_Wo", (1024, 1024)), wT=False, N=D,
                  dst_dram=sd, src_ready=[(al, al.n)] + bob, bias_bcast=cx.SF[:, 6144:7168])
    COEF = 416
    rdone = emit_router(cx, pp, src_tile=srcA, src_ready=srcr,
                        wdram=wbf2("L3_ffn_Wr", (1024, 8)), wT=False, E=8,
                        bias_row_dram=inp["L3_ffn_br"], mode="soft_raw",
                        coeff_col0=COEF, extra_waits=sdone)
    acc = dram("l3_acc", [T, D], F32)
    acld = [p.sem("mo_ld0"), p.sem("mo_ld1")]
    msem = [p.sem("mo_st0"), p.sem("mo_st1")]
    m0 = [msem[0].n, msem[1].n]
    a0 = [acld[0].n, acld[1].n]
    mst = [cx.SF[:, 4096:5120], cx.SF[:, 5120:6144]]
    b2sem = p.sem("l3_b2")
    for e in range(8):
        hd = dram(f"l3_h{e}", [4096, T], BF16)
        w1v = flat_view(cx.wb["L3_ffn_W1"], e * 4096 * 1024, [[4096, 1024], [1, 4096]])
        projB(cx, pp, src_tile=srcB, K=D, Tn=T, wdram=w1v, wT=False, N=4096,
              dst_dram=hd, src_ready=srcr + (rdone if e == 0 else []),
              bias_col=lambda oc, e=e: colf(2 + e, oc), act=AF.Gelu)
        p.dma("gpsimd", cx.ROWS[0:1, 2176:3200],
              flat_view(inp["L3_ffn_b2"], e * 1024, [[1, 1], [1, 1024]]), b2sem,
              waits=[(pp.mm, pp.mm.n)])
        b2t = b2sem.n
        for half in range(2):
            pas = e * 2 + half
            w2v = flat_view(cx.wb["L3_ffn_W2"],
                            e * 4096 * 1024 + half * 2048 * 1024, [[1024, 2048], [1, 1024]])
            hstream = (hd, cx.SBF[:, 8192:10240], cx.SBF[:, 10240:12288], half * 2048)
            def moe_cb(tt, psum, stage, waits, prel, pas=pas, e=e):
                i = tt % 2
                nst = pas * 8 + tt // 2
                ms = mst[i]
                w = list(waits) + ([(msem[i], m0[i] + 16 * nst)] if nst > 0 else [])
                if pas == 0:
                    p.emit("vector", lambda e2, psum=psum, ms=ms, tt=tt: e2.tensor_scalar(
                               out=ms, in0=psum,
                               scalar1=cx.SMALL[:, COEF + tt * 8 + e:COEF + tt * 8 + e + 1],
                               scalar2=1.0, op0=OP.mult, op1=OP.mult),
                           waits=w, incs=[prel])
                else:
                    slot = cx.SF[:, i * 1024: i * 1024 + 1024]
                    p.dma("sync", slot, acc[tt * 128:(tt + 1) * 128, :], acld[i],
                          waits=[(msem[i], m0[i] + 16 * (pas * 8 + tt // 2))])
                    p.emit("vector", lambda e2, psum=psum, ms=ms, slot=slot, tt=tt, e=e:
                               e2.scalar_tensor_tensor(
                                   out=ms, in0=psum,
                                   scalar=cx.SMALL[:, COEF + tt * 8 + e:COEF + tt * 8 + e + 1],
                                   in1=slot, op0=OP.mult, op1=OP.add),
                           waits=w + [(acld[i], acld[i].n)], incs=[prel])
                p.dma("gpsimd", acc[tt * 128:(tt + 1) * 128, :], ms, msem[i],
                      waits=[(prel[0], prel[0].n)])
            projA(cx, pp, src_stream=hstream, K=2048, Tn=T, wdram=w2v, wT=False, N=D,
                  src_ready=[(b2sem, b2t)], evac_cb=moe_cb, store=False,
                  extra_row=cx.ROWS[0:1, 2176:3200] if half == 0 else None)
    upd = emit_update(cx, pp, x_dram, sd, acc, xout,
                      pre_done=[(msem[0], msem[0].n), (msem[1], msem[1].n)] + sdone)
    return upd


LAYER_EMITTERS[3] = emit_L3



def emit_rms(cx, pp, x_dram, out_fm_dram, QC, RC, eps, g_bcast=None, pre_ln=None,
             prev_done=()):
    p = cx.p
    xs2 = [p.sem("rm_x0"), p.sem("rm_x1")]
    used_s = p.sem("rm_us")
    used_v = p.sem("rm_uv")
    us0, uv0 = used_s.n, used_v.n
    slots = [cx.SF[:, 0:1024], cx.SF[:, 1024:2048]]
    trash = cx.SF[:, 2048:3072]
    for tt in range(NT):
        i = tt % 2
        slot = slots[i]
        waits = (([(used_s, us0 + tt - 1)] + ([(used_v, uv0 + tt - 1)] if pre_ln else []))
                 if tt >= 2 else []) + list(prev_done)
        p.dma("sync", slot, x_dram[tt * 128:(tt + 1) * 128, :], xs2[i], waits=waits)
        wv = [(xs2[i], xs2[i].n)]
        if pre_ln is not None:
            MC0, RC0 = pre_ln
            p.emit("vector", lambda e, slot=slot, tt=tt: e.tensor_scalar(
                       out=slot, in0=slot, scalar1=cx.SMALL[:, MC0 + tt:MC0 + tt + 1],
                       scalar2=cx.SMALL[:, RC0 + tt:RC0 + tt + 1],
                       op0=OP.subtract, op1=OP.mult),
                   waits=wv, incs=[(used_v, 1)])
            wv = [(used_v, used_v.n)]
        p.emit("scalar", lambda e, slot=slot, tt=tt: e.activation(
                   out=trash, in_=slot, func=AF.Square,
                   accum_out=cx.SMALL[:, QC + tt:QC + tt + 1]),
               waits=wv + [(used_s, us0 + tt)], incs=[(used_s, 1)])
    sqc = cx.SMALL[:, QC:QC + 16]
    rstd = cx.SMALL[:, RC:RC + 16]
    vs = p.sem("rm_vs")
    v0 = vs.n
    p.emit("vector", lambda e: e.tensor_scalar(out=rstd, in0=sqc, scalar1=1.0 / D,
                                               scalar2=eps, op0=OP.mult, op1=OP.add),
           waits=[(used_s, us0 + 16)] + ([(used_v, uv0 + 16)] if pre_ln else []),
           incs=[(vs, 1)])
    p.emit("scalar", lambda e: e.activation(out=rstd, in_=rstd, func=AF.Sqrt),
           waits=[(vs, v0 + 1)], incs=[(vs, 1)])
    p.emit("vector", lambda e: e.reciprocal(rstd, rstd),
           waits=[(vs, v0 + 2)], incs=[(vs, 1)])
    tmslots = [cx.SBF[:, 0:1024], cx.SBF[:, 1024:2048]]
    fmstage = [cx.SBF[:, 2048:3072], cx.SBF[:, 3072:4096]]
    tmdone = p.sem("rm_tm")
    tpdone = p.sem("rm_tp")
    cpdone = p.sem("rm_cp")
    t0c, tp0, cp0 = tmdone.n, tpdone.n, cpdone.n
    st2 = [p.sem("rm_st0"), p.sem("rm_st1")]
    for tt in range(NT):
        i = tt % 2
        slot = slots[i]
        tm = tmslots[i]
        fs = fmstage[i]
        waits = [(used_s, us0 + 16), (vs, v0 + 3)] + ([(used_v, used_v.n)] if pre_ln else []) \
            if tt < 2 else [(tmdone, t0c + tt - 1)]
        p.dma("sync", slot, x_dram[tt * 128:(tt + 1) * 128, :], xs2[i], waits=waits)
        ch = p.sem("rm_ch")
        w0 = [(xs2[i], xs2[i].n)] + ([(cpdone, cp0 + tt - 1)] if tt >= 2 else [(cx.consts_sem, 112)])
        if pre_ln is not None:
            MC0, RC0 = pre_ln
            p.emit("vector", lambda e, slot=slot, tt=tt: e.tensor_scalar(
                       out=slot, in0=slot, scalar1=cx.SMALL[:, MC0 + tt:MC0 + tt + 1],
                       scalar2=cx.SMALL[:, RC0 + tt:RC0 + tt + 1],
                       op0=OP.subtract, op1=OP.mult),
                   waits=w0, incs=[(ch, 1)])
            w0 = [(ch, ch.n)]
        if g_bcast is not None:
            p.emit("vector", lambda e, slot=slot, tt=tt: e.tensor_scalar(
                       out=slot, in0=slot, scalar1=cx.SMALL[:, RC + tt:RC + tt + 1],
                       scalar2=1.0, op0=OP.mult, op1=OP.mult),
                   waits=w0, incs=[(ch, 1)])
            p.emit("vector", lambda e, slot=slot, tm=tm: e.tensor_mul(tm, slot, g_bcast),
                   waits=[(ch, ch.n)], incs=[(tmdone, 1)])
        else:
            p.emit("vector", lambda e, slot=slot, tm=tm, tt=tt: e.tensor_scalar(
                       out=tm, in0=slot, scalar1=cx.SMALL[:, RC + tt:RC + tt + 1],
                       scalar2=1.0, op0=OP.mult, op1=OP.mult),
                   waits=w0, incs=[(tmdone, 1)])
        for dc in range(ND):
            p.emit("tensor", lambda e, tm=tm, dc=dc: e.transpose(
                       cx.PT[:, dc * 128:(dc + 1) * 128], tm[:, dc * 128:(dc + 1) * 128], ident(cx)),
                   waits=([(tmdone, t0c + tt + 1)] if dc == 0 else []) + ([(cpdone, cp0 + tt)] if (dc == 0 and tt >= 1) else []),
                   incs=[(tpdone, 1)] if dc == ND - 1 else ())
        p.emit("vector", lambda e, fs=fs: e.tensor_copy(fs, cx.PT[:, :]),
               waits=[(tpdone, tp0 + tt + 1)] + ([(st2[i], st2[i].n)] if st2[i].n > 0 else []),
               incs=[(cpdone, 1)])
        dst = flat_view(out_fm_dram, tt * 128, [[T, 128], [128 * T, ND], [1, 128]])
        p.dma("gpsimd", dst, fs.rearrange("p (k t) -> p k t", k=ND), st2[i],
              waits=[(cpdone, cp0 + tt + 1)])
    return [(st2[0], st2[0].n), (st2[1], st2[1].n)]



def emit_l2_bias(cx, pp, relb_dram, v8_dram, v8r_dram, prev_done=()):
    p = cx.p
    ld = p.sem("bR_ld")
    p.dma("gpsimd", cx.SBF[0:32, 0:4096], cx.cdram["c_G"][:, :], ld, waits=list(prev_done))
    p.dma("gpsimd", cx.SBF[0:32, 4096:8192], cx.cdram["c_Gr"][:, :], ld)
    p.dma("gpsimd", cx.ROWS[0:32, 3200:3216], relb_dram[:, :], ld)
    ldt = ld.n
    tp = p.pool("at_mT", lambda: Pool(p, [cx.PT[:, 0:128]], "at_mT"))
    stg = [cx.ROWS[0:16, 3216:3344], cx.ROWS[0:16, 3344:3472]]
    stq = [p.sem("bR_st0"), p.sem("bR_st1")]
    bcol = cx.ROWS[:, 3472:3488]
    for m, (gcol0, dstd) in enumerate([(0, v8_dram), (4096, v8r_dram)]):
        for ch in range(32):
            ps, pw, prel = pp.pb512.acquire()
            psum = ps[:, 0:16]
            p.emit("tensor", lambda e, psum=psum, ch=ch, gcol0=gcol0: e.matmul(
                       psum, cx.SBF[0:32, gcol0 + ch * 128: gcol0 + (ch + 1) * 128],
                       cx.ROWS[0:32, 3200:3216], start=True, stop=True),
                   waits=[(ld, ldt)] + pw, incs=[(pp.mm, 1)])
            p.emit("vector", lambda e, psum=psum: e.tensor_copy(bcol, psum),
                   waits=[(pp.mm, pp.mm.n)], incs=[prel])
            reg, twv = tp.acquire()
            p.emit("tensor", lambda e, reg=reg: e.transpose(reg[0:16, :], bcol, ident(cx)),
                   waits=[(prel[0], prel[0].n), (tp.free, twv), (cx.consts_sem, 112)],
                   incs=[(pp.mm, 1)])
            i = ch % 2
            p.emit("vector", lambda e, reg=reg, i=i: e.tensor_copy(stg[i], reg[0:16, :]),
                   waits=[(pp.mm, pp.mm.n)] + ([(stq[i], stq[i].n)] if stq[i].n > 0 else []),
                   incs=[(tp.free, 1)])
            p.dma("gpsimd", dstd[:, ch * 128:(ch + 1) * 128], stg[i], stq[i],
                  waits=[(tp.free, tp.uses)])
    return [(stq[0], stq[0].n), (stq[1], stq[1].n), (pp.mm, pp.mm.n)]






def emit_L2(cx, pp, x_dram, xout, prev):
    p = cx.p
    dram, wbf2, inp = cx.dram, cx.wbf2, cx.inp
    xn = dram("l2_xn", [D, T], BF16)
    lnst = emit_ln(cx, pp, x_dram, xn, 0, 16, cx.consts_sem, prev_done=prev)
    v8 = dram("l2_v8", [16, 4096], BF16)
    v8r = dram("l2_v8r", [16, 4096], BF16)
    bdoneR = emit_l2_bias(cx, pp, inp["L2_attn_rel_bias"], v8, v8r, prev_done=lnst)
    xl = p.sem("lf")
    load_fm(cx, xn, D, T, cx.SRC, 0, xl, waits=lnst + [(cx.pcsem, cx.pctarget)] + bdoneR)
    srcr = [(xl, xl.n)]
    srcB = lambda dc: cx.SRC[:, dc * T:(dc + 1) * T]
    srcA = lambda dc, tt: cx.SRC[:, dc * T + tt * 128: dc * T + tt * 128 + 128]
    vecs = [(inp["L2_attn_bq"], 0, 1024, 0.125), (inp["L2_attn_bk"], 0, 1024, 1.0)]
    colf, bdone, _ = vecs_to_cols(cx, pp, vecs, 128)
    qd = dram("l2_q", [D, T], BF16)
    kd = dram("l2_k", [D, T], BF16)
    d1 = projB(cx, pp, src_tile=srcB, K=D, Tn=T, wdram=wbf2("L2_attn_Wq", (1024, 1024)),
               wT=False, N=D, dst_dram=qd, src_ready=srcr + bdone,
               bias_col=lambda oc: colf(0, oc), scale=0.125)
    d2 = projB(cx, pp, src_tile=srcB, K=D, Tn=T, wdram=wbf2("L2_attn_Wk", (1024, 1024)),
               wT=False, N=D, dst_dram=kd, src_ready=srcr,
               bias_col=lambda oc: colf(1, oc))
    vext = dram("l2_vext", [T, 1040], BF16)
    bvb = emit_bias_bcast(cx, pp, inp["L2_attn_bv"], 0, 1024, cx.SF[:, 6144:7168])
    vsl = [cx.SF[:, 0:1040], cx.SF[:, 1040:2080]]
    onesem = p.sem("l2_ones")
    for i in range(2):
        ov = vsl[i].rearrange("p (a b) -> p a b", b=65)[:, :, 64:65]
        p.emit("vector", lambda e, ov=ov: e.memset(ov, 1.0), incs=[(onesem, 1)])
    onest = onesem.n
    vst2 = [p.sem("vx_st0"), p.sem("vx_st1")]
    v0st = [vst2[0].n, vst2[1].n]
    def vext_cb(tt, psum, stage, waits, prel):
        i = tt % 2
        slot = vsl[i]
        sv = slot.rearrange("p (a b) -> p a b", b=65)[:, :, 0:64]
        p.emit("vector", lambda e, psum=psum, sv=sv: e.tensor_add(
                   sv, psum.rearrange("p (a b) -> p a b", b=64),
                   cx.SF[:, 6144:7168].rearrange("p (a b) -> p a b", b=64)),
               waits=list(waits) + bvb + [(onesem, onest)] +
                     ([(vst2[i], vst2[i].n)] if vst2[i].n > v0st[i] else []),
               incs=[prel])
        p.dma("gpsimd", vext[tt * 128:(tt + 1) * 128, :], slot, vst2[i],
              waits=[(prel[0], prel[0].n)])
    d3 = projA(cx, pp, src_tile=srcA, K=D, Tn=T, wdram=wbf2("L2_attn_Wv", (1024, 1024)),
               wT=False, N=D, src_ready=srcr, evac_cb=vext_cb, store=False)
    attnd = dram("l2_attn", [D, T], BF16)
    adone = emit_softmax_attn(cx, pp, q_dram=qd, k_dram=kd, vext_dram=vext,
                              out_dram=attnd, causal=False,
                              v8q_dram=v8r, v8t_dram=v8,
                              src_ready=d1 + d2 + [(vst2[0], vst2[0].n), (vst2[1], vst2[1].n)])
    al = p.sem("lf")
    load_fm(cx, attnd, D, T, cx.SBF, 0, al, waits=adone)
    bob = emit_bias_bcast(cx, pp, inp["L2_attn_bo"], 0, 1024, cx.SF[:, 6144:7168])
    sd = dram("l2_s", [T, D], F32)
    sdone = projA(cx, pp,
                  src_tile=lambda dc, tt: cx.SBF[:, dc * T + tt * 128: dc * T + tt * 128 + 128],
                  K=D, Tn=T, wdram=wbf2("L2_attn_Wo", (1024, 1024)), wT=False, N=D,
                  dst_dram=sd, src_ready=[(al, al.n)] + bob, bias_bcast=cx.SF[:, 6144:7168])
    g1b = emit_bias_bcast(cx, pp, inp["L2_ffn_g1"], 0, 1024, cx.SF[:, 7168:8192])
    h1 = dram("l2_h1", [D, T], BF16)
    r1done = emit_rms(cx, pp, x_dram, h1, 64, 80, 1e-6, g_bcast=cx.SF[:, 7168:8192],
                      pre_ln=(0, 16), prev_done=sdone + g1b)
    h1l = p.sem("lf")
    load_fm(cx, h1, D, T, cx.SRC, 0, h1l, waits=r1done)
    h1B = lambda dc: cx.SRC[:, dc * T:(dc + 1) * T]
    h1A = lambda dc, tt: cx.SRC[:, dc * T + tt * 128: dc * T + tt * 128 + 128]
    h1r = [(h1l, h1l.n)]
    mq = dram("l2_mq", [D, T], BF16)
    projB(cx, pp, src_tile=h1B, K=D, Tn=T, wdram=wbf2("L2_ffn_Wq", (1024, 1024)),
          wT=False, N=D, dst_dram=mq, src_ready=h1r, scale=0.125)
    cd = dram("l2_c", [512, T], BF16)
    cdone = projB(cx, pp, src_tile=h1B, K=D, Tn=T, wdram=wbf2("L2_ffn_Wdkv", (1024, 512)),
                  wT=False, N=512, dst_dram=cd, src_ready=h1r)
    cl = p.sem("lf")
    load_fm(cx, cd, 512, T, cx.SRC, 0, cl, waits=cdone)
    cB = lambda dc: cx.SRC[:, dc * T:(dc + 1) * T]
    cA = lambda dc, tt: cx.SRC[:, dc * T + tt * 128: dc * T + tt * 128 + 128]
    cr = [(cl, cl.n)]
    mk = dram("l2_mk", [D, T], BF16)
    d4 = projB(cx, pp, src_tile=cB, K=512, Tn=T, wdram=wbf2("L2_ffn_Wuk", (512, 1024)),
               wT=False, N=D, dst_dram=mk, src_ready=cr)
    mvext = dram("l2_mvext", [T, 1040], BF16)
    onesem2 = p.sem("l2_ones")
    for i in range(2):
        ov = vsl[i].rearrange("p (a b) -> p a b", b=65)[:, :, 64:65]
        p.emit("vector", lambda e, ov=ov: e.memset(ov, 1.0),
               waits=[(vst2[i], vst2[i].n)], incs=[(onesem2, 1)])
    onest2 = onesem2.n
    v1st = [vst2[0].n, vst2[1].n]
    def mvext_cb(tt, psum, stage, waits, prel):
        i = tt % 2
        slot = vsl[i]
        sv = slot.rearrange("p (a b) -> p a b", b=65)[:, :, 0:64]
        p.emit("vector", lambda e, psum=psum, sv=sv: e.tensor_copy(
                   sv, psum.rearrange("p (a b) -> p a b", b=64)),
               waits=list(waits) + [(onesem2, onest2)] +
                     ([(vst2[i], vst2[i].n)] if vst2[i].n > v1st[i] else []),
               incs=[prel])
        p.dma("gpsimd", mvext[tt * 128:(tt + 1) * 128, :], slot, vst2[i],
              waits=[(prel[0], prel[0].n)])
    d5 = projA(cx, pp, src_tile=cA, K=512, Tn=T, wdram=wbf2("L2_ffn_Wuv", (512, 1024)),
               wT=False, N=D, src_ready=cr, evac_cb=mvext_cb, store=False)
    mattn = dram("l2_mattn", [D, T], BF16)
    madone = emit_softmax_attn(cx, pp, q_dram=mq, k_dram=mk, vext_dram=mvext,
                               out_dram=mattn, causal=True,
                               src_ready=d4 + d5 + [(vst2[0], vst2[0].n), (vst2[1], vst2[1].n)])
    mal = p.sem("lf")
    load_fm(cx, mattn, D, T, cx.SBF, 0, mal, waits=madone)
    x2d = dram("l2_x2", [T, D], F32)
    x2ld = [p.sem("x2_ld0"), p.sem("x2_ld1")]
    xsl = [cx.SF[:, 0:1024], cx.SF[:, 1024:2048]]
    x2ch = p.sem("x2_ch")
    last_prel = [None, None]
    def x2_cb(tt, psum, stage, waits, prel):
        i = tt % 2
        slot = xsl[i]
        p.dma("sync", slot, x_dram[tt * 128:(tt + 1) * 128, :], x2ld[i],
              waits=[last_prel[i]] if last_prel[i] is not None else [])
        p.emit("vector", lambda e, slot=slot, tt=tt: e.tensor_scalar(
                   out=slot, in0=slot, scalar1=cx.SMALL[:, 0 + tt:0 + tt + 1],
                   scalar2=cx.SMALL[:, 16 + tt:16 + tt + 1],
                   op0=OP.subtract, op1=OP.mult),
               waits=[(x2ld[i], x2ld[i].n)], incs=[(x2ch, 1)])
        p.emit("vector", lambda e, psum=psum, stage=stage, slot=slot: e.tensor_add(
                   stage, psum, slot),
               waits=list(waits) + [(x2ch, x2ch.n)], incs=[prel])
        last_prel[i] = (prel[0], prel[0].n)
    x2done = projA(cx, pp,
                   src_tile=lambda dc, tt: cx.SBF[:, dc * T + tt * 128: dc * T + tt * 128 + 128],
                   K=D, Tn=T, wdram=wbf2("L2_ffn_Wo", (1024, 1024)), wT=False, N=D,
                   dst_dram=x2d, src_ready=[(mal, mal.n)], evac_cb=x2_cb)
    g2b = emit_bias_bcast(cx, pp, inp["L2_ffn_g2"], 0, 1024, cx.SF[:, 7168:8192])
    h2 = dram("l2_h2", [D, T], BF16)
    r2done = emit_rms(cx, pp, x2d, h2, 64, 96, 1e-6, g_bcast=cx.SF[:, 7168:8192],
                      prev_done=x2done + g2b)
    h2l = p.sem("lf")
    load_fm(cx, h2, D, T, cx.SRC, 0, h2l, waits=r2done)
    h2B = lambda dc: cx.SRC[:, dc * T:(dc + 1) * T]
    h2A = lambda dc, tt: cx.SRC[:, dc * T + tt * 128: dc * T + tt * 128 + 128]
    h2r = [(h2l, h2l.n)]
    COEF = 416
    rdone = emit_router(cx, pp, src_tile=h2A, src_ready=h2r,
                        wdram=wbf2("L2_ffn_gate_w", (1024, 8)), wT=False, E=8,
                        bias_row_dram=None, mode="logit_soft", coeff_col0=COEF)
    acc = dram("l2_acc", [T, D], F32)
    acld = [p.sem("mo_ld0"), p.sem("mo_ld1")]
    msem = [p.sem("mo_st0"), p.sem("mo_st1")]
    m0 = [msem[0].n, msem[1].n]
    mst = [cx.SF[:, 4096:5120], cx.SF[:, 5120:6144]]
    for e in range(8):
        hd = dram(f"l2_he{e}", [2048, T], BF16)
        w1v = flat_view(cx.wb["L2_ffn_w1"], e * 1024 * 2048, [[2048, 1024], [1, 2048]])
        projB(cx, pp, src_tile=h2B, K=D, Tn=T, wdram=w1v, wT=False, N=2048,
              dst_dram=hd, src_ready=h2r + (rdone if e == 0 else []), act=AF.Silu)
        w2v = flat_view(cx.wb["L2_ffn_w2"], e * 2048 * 1024, [[1024, 2048], [1, 1024]])
        hstream = (hd, cx.SBF[:, 8192:10240], cx.SBF[:, 10240:12288], 0)
        def moe_cb(tt, psum, stage, waits, prel, pas=e):
            i = tt % 2
            nst = pas * 8 + tt // 2
            ms = mst[i]
            w = list(waits) + ([(msem[i], m0[i] + 16 * nst)] if nst > 0 else [])
            if pas == 0:
                p.emit("vector", lambda e2, psum=psum, ms=ms, tt=tt, pas=pas: e2.tensor_scalar(
                           out=ms, in0=psum,
                           scalar1=cx.SMALL[:, COEF + tt * 8 + pas:COEF + tt * 8 + pas + 1],
                           scalar2=1.0, op0=OP.mult, op1=OP.mult),
                       waits=w, incs=[prel])
            else:
                slot = cx.SF[:, i * 1024: i * 1024 + 1024]
                p.dma("sync", slot, acc[tt * 128:(tt + 1) * 128, :], acld[i],
                      waits=[(msem[i], m0[i] + 16 * (pas * 8 + tt // 2))])
                p.emit("vector", lambda e2, psum=psum, ms=ms, slot=slot, tt=tt, pas=pas:
                           e2.scalar_tensor_tensor(
                               out=ms, in0=psum,
                               scalar=cx.SMALL[:, COEF + tt * 8 + pas:COEF + tt * 8 + pas + 1],
                               in1=slot, op0=OP.mult, op1=OP.add),
                       waits=w + [(acld[i], acld[i].n)], incs=[prel])
            p.dma("gpsimd", acc[tt * 128:(tt + 1) * 128, :], ms, msem[i],
                  waits=[(prel[0], prel[0].n)])
        projA(cx, pp, src_stream=hstream, K=2048, Tn=T, wdram=w2v, wT=False, N=D,
              src_ready=[], evac_cb=moe_cb, store=False)
    upd = emit_update(cx, pp, x_dram, sd, acc, xout,
                      pre_done=[(msem[0], msem[0].n), (msem[1], msem[1].n)] + sdone,
                      m2_dram=x2d)
    return upd


LAYER_EMITTERS[2] = emit_L2



def emit_lnG(cx, pp, x_dram, out_fm_dram, MC, RC, g_bcast, b_bcast, prev_done=(),
             tm_dram=None, ext_dram=None, skip_fm=False):
    p = cx.p
    xs2 = [p.sem("lg_x0"), p.sem("lg_x1")]
    used_s = p.sem("lg_us")
    used_v = p.sem("lg_uv")
    us0, uv0 = used_s.n, used_v.n
    slots = [cx.SF[:, 0:1024], cx.SF[:, 1024:2048]]
    trash = cx.SF[:, 2048:3072]
    for tt in range(NT):
        i = tt % 2
        slot = slots[i]
        waits = ([(used_s, us0 + tt - 1), (used_v, uv0 + tt - 1)] if tt >= 2 else []) + list(prev_done)
        p.dma("sync", slot, x_dram[tt * 128:(tt + 1) * 128, :], xs2[i], waits=waits)
        wv = [(xs2[i], xs2[i].n)]
        p.emit("scalar", lambda e, slot=slot, tt=tt: e.activation(
                   out=trash, in_=slot, func=AF.Square,
                   accum_out=cx.SMALL[:, RC + 32 + tt:RC + 32 + tt + 1]),
               waits=wv + [(used_s, us0 + tt)], incs=[(used_s, 1)])
        p.emit("vector", lambda e, slot=slot, tt=tt: e.reduce_sum(
                   cx.SMALL[:, RC + 16 + tt:RC + 16 + tt + 1], slot, axis=AX.X),
               waits=wv, incs=[(used_v, 1)])
    mean = cx.SMALL[:, MC:MC + 16]
    rstd = cx.SMALL[:, RC:RC + 16]
    sumc = cx.SMALL[:, RC + 16:RC + 32]
    sqc = cx.SMALL[:, RC + 32:RC + 48]
    vs = p.sem("lg_vs")
    v0 = vs.n
    p.emit("vector", lambda e: e.tensor_scalar_mul(mean, sumc, 1.0 / D),
           waits=[(used_s, us0 + 16), (used_v, uv0 + 16)], incs=[(vs, 1)])
    p.emit("vector", lambda e: e.tensor_scalar_mul(sqc, sqc, 1.0 / D),
           waits=[(vs, v0 + 1)], incs=[(vs, 1)])
    p.emit("vector", lambda e: e.tensor_mul(sumc, mean, mean),
           waits=[(vs, v0 + 2)], incs=[(vs, 1)])
    p.emit("vector", lambda e: e.tensor_sub(sqc, sqc, sumc),
           waits=[(vs, v0 + 3)], incs=[(vs, 1)])
    p.emit("vector", lambda e: e.tensor_scalar_add(sqc, sqc, 1e-5),
           waits=[(vs, v0 + 4)], incs=[(vs, 1)])
    p.emit("scalar", lambda e: e.activation(out=sqc, in_=sqc, func=AF.Sqrt),
           waits=[(vs, v0 + 5)], incs=[(vs, 1)])
    p.emit("vector", lambda e: e.reciprocal(rstd, sqc),
           waits=[(vs, v0 + 6)], incs=[(vs, 1)])
    tmslots = [cx.SBF[:, 0:1024], cx.SBF[:, 1024:2048]]
    fmstage = [cx.SBF[:, 2048:3072], cx.SBF[:, 3072:4096]]
    tmdone = p.sem("lg_tm")
    tpdone = p.sem("lg_tp")
    cpdone = p.sem("lg_cp")
    t0c, tp0, cp0 = tmdone.n, tpdone.n, cpdone.n
    st2 = [p.sem("lg_st0"), p.sem("lg_st1")]
    ch = p.sem("lg_ch")
    exsl = [cx.SF[:, 4096:5136], cx.SF[:, 5136:6176]]
    exst = [p.sem("lg_ex0"), p.sem("lg_ex1")]
    onesem = p.sem("lg_on")
    if ext_dram is not None:
        for i in range(2):
            ov = exsl[i].rearrange("p (a b) -> p a b", b=65)[:, :, 64:65]
            p.emit("vector", lambda e, ov=ov: e.memset(ov, 1.0),
                   waits=[(exst[i], exst[i].n)], incs=[(onesem, 1)])
    onest = onesem.n
    for tt in range(NT):
        i = tt % 2
        slot = slots[i]
        tm = tmslots[i]
        fs = fmstage[i]
        waits = [(used_s, us0 + 16), (used_v, uv0 + 16), (vs, v0 + 7)] if tt < 2 else [(tmdone, t0c + tt - 1)]
        p.dma("sync", slot, x_dram[tt * 128:(tt + 1) * 128, :], xs2[i], waits=waits)
        p.emit("vector", lambda e, slot=slot, tt=tt: e.tensor_scalar(
                   out=slot, in0=slot, scalar1=cx.SMALL[:, MC + tt:MC + tt + 1],
                   scalar2=cx.SMALL[:, RC + tt:RC + tt + 1],
                   op0=OP.subtract, op1=OP.mult),
               waits=[(xs2[i], xs2[i].n)] + ([(cpdone, cp0 + tt - 1)] if tt >= 2 else [(cx.consts_sem, 112)])
                     + ([(st2[i], st2[i].n)] if tm_dram is not None else []),
               incs=[(ch, 1)])
        p.emit("vector", lambda e, slot=slot: e.tensor_mul(slot, slot, g_bcast),
               waits=[(ch, ch.n)], incs=[(ch, 1)])
        p.emit("vector", lambda e, slot=slot, tm=tm: e.tensor_add(tm, slot, b_bcast),
               waits=[(ch, ch.n)], incs=[(tmdone, 1)])
        nuse = 0
        if tm_dram is not None:
            p.dma("gpsimd", tm_dram[tt * 128:(tt + 1) * 128, :], tm, st2[i],
                  waits=[(tmdone, tmdone.n)])
        if ext_dram is not None:
            ex = exsl[i]
            sv = ex.rearrange("p (a b) -> p a b", b=65)[:, :, 0:64]
            p.emit("vector", lambda e, sv=sv, tm=tm: e.tensor_copy(
                       sv, tm.rearrange("p (a b) -> p a b", b=64)),
                   waits=[(tmdone, tmdone.n), (onesem, onest)] +
                         ([(exst[i], exst[i].n)] if exst[i].n > 0 else []),
                   incs=[(ch, 1)])
            p.dma("gpsimd", ext_dram[tt * 128:(tt + 1) * 128, :], ex, exst[i],
                  waits=[(ch, ch.n)])
        if not skip_fm:
            for dc in range(ND):
                p.emit("tensor", lambda e, tm=tm, dc=dc: e.transpose(
                           cx.PT[:, dc * 128:(dc + 1) * 128], tm[:, dc * 128:(dc + 1) * 128], ident(cx)),
                       waits=([(tmdone, t0c + tt + 1)] if dc == 0 else []) + ([(cpdone, cp0 + tt)] if (dc == 0 and tt >= 1) else []),
                       incs=[(tpdone, 1)] if dc == ND - 1 else ())
            p.emit("vector", lambda e, fs=fs: e.tensor_copy(fs, cx.PT[:, :]),
                   waits=[(tpdone, tp0 + tt + 1)] + ([(st2[i], st2[i].n)] if st2[i].n > 0 else []),
                   incs=[(cpdone, 1)])
            dst = flat_view(out_fm_dram, tt * 128, [[T, 128], [128 * T, ND], [1, 128]])
            p.dma("gpsimd", dst, fs.rearrange("p (k t) -> p k t", k=ND), st2[i],
                  waits=[(cpdone, cp0 + tt + 1)])
        else:
            p.emit("vector", lambda e, tm=tm: e.tensor_copy(trash[:, 0:1], tm[:, 0:1]),
                   waits=[(tmdone, t0c + tt + 1), (ch, ch.n)], incs=[(cpdone, 1)])
    done = [(st2[0], st2[0].n), (st2[1], st2[1].n), (cpdone, cpdone.n)]
    if ext_dram is not None:
        done += [(exst[0], exst[0].n), (exst[1], exst[1].n)]
    return done


def emit_tm2fm(cx, pp, src_dram, dst_fm_dram, col_off=0, row_stride=T, prev_done=(),
               cast=False):
    p = cx.p
    eng = "gpsimd" if cast else "sync"
    ld2 = [p.sem("t2f_l0" + eng), p.sem("t2f_l1" + eng)]
    st2 = [p.sem("t2f_s0"), p.sem("t2f_s1")]
    tpdone = p.sem("t2f_tp")
    cpdone = p.sem("t2f_cp")
    tp0, cp0 = tpdone.n, cpdone.n
    tmslots = [cx.SBF[:, 0:1024], cx.SBF[:, 1024:2048]]
    fmstage = [cx.SBF[:, 2048:3072], cx.SBF[:, 3072:4096]]
    for tt in range(NT):
        i = tt % 2
        tm = tmslots[i]
        fs = fmstage[i]
        waits = ([(cpdone, cp0 + tt - 1)] if tt >= 2 else list(prev_done)) \
            + [(cx.consts_sem, 112)]
        p.dma(eng, tm, src_dram[tt * 128:(tt + 1) * 128, :], ld2[i], waits=waits)
        for dc in range(ND):
            p.emit("tensor", lambda e, tm=tm, dc=dc: e.transpose(
                       cx.PT[:, dc * 128:(dc + 1) * 128], tm[:, dc * 128:(dc + 1) * 128], ident(cx)),
                   waits=([(ld2[i], ld2[i].n)] if dc == 0 else []) + ([(cpdone, cp0 + tt)] if (dc == 0 and tt >= 1) else []),
                   incs=[(tpdone, 1)] if dc == ND - 1 else ())
        p.emit("vector", lambda e, fs=fs: e.tensor_copy(fs, cx.PT[:, :]),
               waits=[(tpdone, tp0 + tt + 1)] + ([(st2[i], st2[i].n)] if st2[i].n > 0 else []),
               incs=[(cpdone, 1)])
        dst = flat_view(dst_fm_dram, col_off + tt * 128,
                        [[row_stride, 128], [128 * row_stride, ND], [1, 128]])
        p.dma("gpsimd", dst, fs.rearrange("p (k t) -> p k t", k=ND), st2[i],
              waits=[(cpdone, cp0 + tt + 1)])
    return [(st2[0], st2[0].n), (st2[1], st2[1].n)]



def emit_L0(cx, pp, x_dram, xout, prev):
    p = cx.p
    dram, wbf2, inp = cx.dram, cx.wbf2, cx.inp
    xn = dram("l0_xn", [D, T], BF16)
    lnst = emit_ln(cx, pp, x_dram, xn, 0, 16, cx.consts_sem, prev_done=prev)
    xl = p.sem("lf")
    load_fm(cx, xn, D, T, cx.SRC, 0, xl, waits=lnst + [(cx.pcsem, cx.pctarget)])
    srcr = [(xl, xl.n)]
    srcB = lambda dc: cx.SRC[:, dc * T:(dc + 1) * T]
    srcA = lambda dc, tt: cx.SRC[:, dc * T + tt * 128: dc * T + tt * 128 + 128]
    GB, BB = cx.SF[:, 6144:7168], cx.SF[:, 7168:8192]
    qraw = dram("l0_qraw", [T, D], F32)
    kraw = dram("l0_kraw", [T, D], F32)
    vraw = dram("l0_vraw", [T, D], F32)
    qfm = dram("l0_qfm", [D, T], BF16)
    kfm = dram("l0_kfm", [D, T], BF16)
    vext = dram("l0_vext", [T, 1040], BF16)
    for (W, bvec, raw) in [("L0_attn_Wq", "L0_attn_bq", qraw),
                           ("L0_attn_Wk", "L0_attn_bk", kraw),
                           ("L0_attn_Wv", "L0_attn_bv", vraw)]:
        bb = emit_bias_bcast(cx, pp, inp[bvec], 0, 1024, GB)
        projA(cx, pp, src_tile=srcA, K=D, Tn=T, wdram=wbf2(W, (1024, 1024)),
              wT=False, N=D, dst_dram=raw, src_ready=srcr + bb, bias_bcast=GB)
    rawdone = pp.stage1024.all_done()
    vdone = None
    for (raw, gv, bv, outfm, extd) in [
            (qraw, "L0_attn_lnq_g", "L0_attn_lnq_b", qfm, None),
            (kraw, "L0_attn_lnk_g", "L0_attn_lnk_b", kfm, None),
            (vraw, "L0_attn_lnv_g", "L0_attn_lnv_b", None, vext)]:
        gb = emit_bias_bcast(cx, pp, inp[gv], 0, 1024, GB)
        bb = emit_bias_bcast(cx, pp, inp[bv], 0, 1024, BB)
        vdone = emit_lnG(cx, pp, raw, outfm, 64, 80, GB, BB, prev_done=gb + bb + rawdone,
                         ext_dram=extd, skip_fm=(outfm is None))
    ql = p.sem("lf")
    load_fm(cx, qfm, D, T, cx.SRC, 0, ql, waits=[(p.sem("lg_st0"), p.sem("lg_st0").n),
                                                 (p.sem("lg_st1"), p.sem("lg_st1").n)])
    qr = [(ql, ql.n)]
    vq = p.sem("l0_vq")
    for dc in range(ND):
        p.emit("vector", lambda e, dc=dc: e.reduce_sum(
                   cx.SMALL[:, 144 + dc:144 + dc + 1], cx.SRC[:, dc * T:(dc + 1) * T], axis=AX.X),
               waits=qr, incs=[(vq, 1)])
    p.emit("vector", lambda e: e.tensor_scalar_mul(cx.SMALL[:, 144:152],
                                                   cx.SMALL[:, 144:152], 1.0 / (T * 8.0)),
           waits=[(vq, vq.n)], incs=[(vq, 1)])
    qgdone = vq.n
    vecsP = [(inp["L0_attn_bphi_q"], 0, 1024, 1.0)]
    colf, bpdone, _ = vecs_to_cols(cx, pp, vecsP, 720)
    phiq = dram("l0_phiq", [D, T], BF16)
    def plus1_cb(ocg, t0, tw, psum, stage, waits, prel):
        p.emit("scalar", lambda e, psum=psum, stage=stage, ocg=ocg: e.activation(
                   out=stage, in_=psum, func=AF.Tanh, bias=colf(0, ocg)),
               waits=waits, incs=[(pp.evs, 1)])
        p.emit("vector", lambda e, stage=stage: e.tensor_scalar_add(stage, stage, 1.0),
               waits=[(pp.evs, pp.evs.n)], incs=[prel])
    d1 = projB(cx, pp, src_tile=srcB, K=D, Tn=T, wdram=wbf2("L0_attn_Wphi_q", (1024, 1024)),
               wT=False, N=D, dst_dram=phiq, src_ready=qr + bpdone, evac_cb=plus1_cb)
    kl = p.sem("lf")
    load_fm(cx, kfm, D, T, cx.SRC, 0, kl, waits=d1)
    kr = [(kl, kl.n)]
    vqg = p.sem("l0_qgf")
    for dc in range(ND):
        reg = cx.ROWS[:, 2048 + dc * 16: 2048 + (dc + 1) * 16]
        p.emit("vector", lambda e, reg=reg: e.memset(reg, 0.0),
               waits=[(vq, qgdone)], incs=[(vqg, 1)])
        p.emit("vector", lambda e, reg=reg, dc=dc: e.tensor_copy(
                   reg[0:64, 2 * dc:2 * dc + 1], cx.SMALL[0:64, 144 + dc:144 + dc + 1]),
               waits=[(vqg, vqg.n)], incs=[(vqg, 1)])
        p.emit("vector", lambda e, reg=reg, dc=dc: e.tensor_copy(
                   reg[64:128, 2 * dc + 1:2 * dc + 2], cx.SMALL[64:128, 144 + dc:144 + dc + 1]),
               waits=[(vqg, vqg.n)], incs=[(vqg, 1)])
    qgf = vqg.n
    sc = p.sem("l0_sc")
    for dc in range(ND):
        for ncc in range(4):
            p.emit("tensor", lambda e, dc=dc, ncc=ncc: e.matmul(
                       cx.PA[0:16, ncc * 512:(ncc + 1) * 512],
                       cx.ROWS[:, 2048 + dc * 16: 2048 + (dc + 1) * 16],
                       cx.SRC[:, dc * T + ncc * 512: dc * T + ncc * 512 + 512],
                       start=(dc == 0), stop=(dc == ND - 1)),
                   waits=([(vqg, qgf)] + kr + pp.pa1024.all_done() + pp.pb512.all_done())
                         if (dc == 0 and ncc == 0) else (),
                   incs=[(pp.mm, 1)] if (dc == ND - 1 and ncc == 3) else ())
    vs = p.sem("l0_sm")
    ALC = 160
    m1 = cx.SMALL[:, 672:673]
    nm = cx.SMALL[:, 673:674]
    den = cx.SMALL[:, 674:675]
    enm = cx.SMALL[:, 675:676]
    A0 = cx.SMALL[:, 676:677]
    exps = cx.SBF[0:16, 4096:6144]
    p.emit("vector", lambda e: e.reduce_max(m1[0:16, :], cx.PA[0:16, 0:2048], axis=AX.X),
           waits=[(pp.mm, pp.mm.n)], incs=[(vs, 1)])
    p.emit("vector", lambda e: e.tensor_scalar_max(m1[0:16, :], m1[0:16, :], 0.0),
           waits=[(vs, vs.n)], incs=[(vs, 1)])
    p.emit("vector", lambda e: e.tensor_scalar_mul(nm[0:16, :], m1[0:16, :], -1.0),
           waits=[(vs, vs.n)], incs=[(vs, 1)])
    vsm = vs.n
    p.emit("scalar", lambda e: e.activation(out=exps, in_=cx.PA[0:16, 0:2048], func=AF.Exp,
                                            bias=nm[0:16, :], accum_out=den[0:16, :]),
           waits=[(vs, vsm)], incs=[(vs, 1)])
    p.emit("scalar", lambda e: e.activation(out=enm[0:16, :], in_=nm[0:16, :], func=AF.Exp),
           waits=[(vs, vs.n)], incs=[(vs, 1)])
    p.emit("vector", lambda e: e.tensor_add(den[0:16, :], den[0:16, :], enm[0:16, :]),
           waits=[(vs, vs.n)], incs=[(vs, 1)])
    p.emit("vector", lambda e: e.reciprocal(den[0:16, :], den[0:16, :]),
           waits=[(vs, vs.n)], incs=[(vs, 1)])
    p.emit("vector", lambda e: e.tensor_scalar(out=den[0:16, :], in0=den[0:16, :],
                                               scalar1=float(T), scalar2=1.0,
                                               op0=OP.mult, op1=OP.mult),
           waits=[(vs, vs.n)], incs=[(vs, 1)])
    p.emit("vector", lambda e: e.tensor_scalar(out=exps, in0=exps, scalar1=den[0:16, :],
                                               scalar2=1.0, op0=OP.mult, op1=OP.mult),
           waits=[(vs, vs.n)], incs=[(vs, 1)])
    p.emit("vector", lambda e: e.tensor_mul(A0[0:16, :], enm[0:16, :], den[0:16, :]),
           waits=[(vs, vs.n)], incs=[(vs, 1)])
    aldone = vs.n
    tp = p.pool("at_mT", lambda: Pool(p, [cx.PT[:, 0:128]], "at_mT"))
    for lt in range(NT):
        reg, twv = tp.acquire()
        p.emit("tensor", lambda e, reg=reg, lt=lt: e.transpose(
                   reg[:, 0:16], exps[:, lt * 128:(lt + 1) * 128], ident(cx)[0:16, 0:16]),
               waits=[(vs, aldone), (tp.free, twv), (cx.consts_sem, 112)], incs=[(pp.mm, 1)])
        p.emit("vector", lambda e, reg=reg, lt=lt: e.tensor_copy(
                   cx.SMALL[:, ALC + lt * 16: ALC + (lt + 1) * 16], reg[0:16, :].transpose() if False else reg[0:16, :]),
               waits=[(pp.mm, pp.mm.n)], incs=[(tp.free, 1)]) if False else None
        p.emit("vector", lambda e, reg=reg, lt=lt: e.tensor_copy(
                   cx.SMALL[:, ALC + lt * 16: ALC + (lt + 1) * 16], reg[:, 0:16]),
               waits=[(pp.mm, pp.mm.n)], incs=[(tp.free, 1)])
    alT = [(tp.free, tp.uses)]
    return _emit_L0_part2(cx, pp, x_dram, xout, prev, locals())


def _emit_L0_part2(cx, pp, x_dram, xout, prev, env):
    p = cx.p
    dram, wbf2, inp = cx.dram, cx.wbf2, cx.inp
    srcB, srcA = env["srcB"], env["srcA"]
    vext, kfm, phiq = env["vext"], env["kfm"], env["phiq"]
    kr, alT, vs, aldone = env["kr"], env["alT"], env["vs"], env["aldone"]
    A0 = env["A0"]
    GB, BB = env["GB"], env["BB"]
    xn = env["xn"]
    ALC = 160
    bkb = emit_bias_bcast(cx, pp, inp["L0_attn_bphi_k"], 0, 1024, BB)
    pks = dram("l0_pks", [T, D], BF16)
    pkch = p.sem("l0_pk")
    def pks_cb(tt, psum, stage, waits, prel):
        p.emit("vector", lambda e, psum=psum, stage=stage: e.tensor_add(stage, psum, BB),
               waits=list(waits) + bkb + [(vs, aldone)], incs=[(pkch, 1)])
        p.emit("scalar", lambda e, stage=stage: e.activation(out=stage, in_=stage, func=AF.Tanh),
               waits=[(pkch, pkch.n)], incs=[(pp.evs, 1)])
        p.emit("vector", lambda e, stage=stage: e.tensor_scalar_add(stage, stage, 1.0),
               waits=[(pp.evs, pp.evs.n)], incs=[(pkch, 1)])
        for h in range(16):
            p.emit("vector", lambda e, stage=stage, h=h, tt=tt: e.tensor_scalar(
                       out=stage[:, h * 64:(h + 1) * 64], in0=stage[:, h * 64:(h + 1) * 64],
                       scalar1=cx.SMALL[:, ALC + tt * 16 + h: ALC + tt * 16 + h + 1],
                       scalar2=1.0, op0=OP.mult, op1=OP.mult),
                   waits=[(pkch, pkch.n)] + (alT if h == 0 else []),
                   incs=[prel] if h == 15 else ())
    dpk = projA(cx, pp, src_tile=lambda dc, tt: cx.SRC[:, dc * T + tt * 128: dc * T + tt * 128 + 128],
                K=D, Tn=T, wdram=wbf2("L0_attn_Wphi_k", (1024, 1024)),
                wT=False, N=D, dst_dram=pks, src_ready=kr + [(vs, aldone)], evac_cb=pks_cb)
    r0 = p.sem("l0_r0")
    p.dma("gpsimd", cx.ROWS[0:1, 2080:3104],
          flat_view(inp["L0_attn_bphi_k"], 0, [[1, 1], [1, 1024]]), r0,
          waits=[(pp.mm, pp.mm.n)])
    p.emit("scalar", lambda e: e.activation(out=cx.ROWS[0:1, 2080:3104],
                                            in_=cx.ROWS[0:1, 2080:3104], func=AF.Tanh),
           waits=[(r0, r0.n)], incs=[(pp.evs, 1)])
    p.emit("vector", lambda e: e.tensor_scalar_add(cx.ROWS[0:1, 2080:3104],
                                                   cx.ROWS[0:1, 2080:3104], 1.0),
           waits=[(pp.evs, pp.evs.n)], incs=[(pkch, 1)])
    tp = p.pool("at_mT", lambda: Pool(p, [cx.PT[:, 0:128]], "at_mT"))
    a0b = cx.ROWS[:, 3480:3481]
    p.emit("vector", lambda e: e.tensor_copy(a0b[0:16, :], A0[0:16, :]),
           waits=[(vs, aldone)], incs=[(pkch, 1)])
    reg, twv = tp.acquire()
    p.emit("tensor", lambda e, reg=reg: e.transpose(reg[0:1, 0:16], a0b[0:16, :],
                                                    ident(cx)[0:16, 0:16]),
           waits=[(pkch, pkch.n), (tp.free, twv), (cx.consts_sem, 112)], incs=[(pp.mm, 1)])
    p.emit("vector", lambda e, reg=reg: e.tensor_copy(cx.SMALL[0:1, 744:760], reg[0:1, 0:16]),
           waits=[(pp.mm, pp.mm.n)], incs=[(tp.free, 1)])
    a0t = [(tp.free, tp.uses)]
    for h in range(16):
        p.emit("vector", lambda e, h=h: e.tensor_scalar(
                   out=cx.ROWS[0:1, 2080 + h * 64:2080 + (h + 1) * 64],
                   in0=cx.ROWS[0:1, 2080 + h * 64:2080 + (h + 1) * 64],
                   scalar1=cx.SMALL[0:1, 744 + h:744 + h + 1],
                   scalar2=1.0, op0=OP.mult, op1=OP.mult),
               waits=[(pkch, pkch.n)] + (a0t if h == 0 else []), incs=[(pkch, 1)])
    pks0done = [(pkch, pkch.n)]
    zv = p.sem("l0_zv")
    p.emit("vector", lambda e: e.memset(cx.ROWS[0:1, 3392:3457], 0.0), incs=[(zv, 1)])
    p.emit("vector", lambda e: e.memset(cx.ROWS[0:1, 3456:3457], 1.0),
           waits=[(zv, zv.n)], incs=[(zv, 1)])
    zvdone = zv.n
    return _emit_L0_part3(cx, pp, x_dram, xout, prev, env, locals())


def _emit_L0_part3(cx, pp, x_dram, xout, prev, env, env2):
    p = cx.p
    dram, wbf2, inp = cx.dram, cx.wbf2, cx.inp
    vext, phiq, xn = env["vext"], env["phiq"], env["xn"]
    pks = env2["pks"]
    dpk, pks0done, zvdone, zv = env2["dpk"], env2["pks0done"], env2["zvdone"], env2["zv"]
    srcB, srcA = env["srcB"], env["srcA"]
    pl = p.sem("lf")
    load_fm(cx, flat_view(pks, 0, [[1024, 2048], [1, 1024]]) if False else pks,
            0, 0, None, 0, pl) if False else None
    pkl = p.sem("l0_pkl")
    src = flat_view(pks, 0, [[1024, 128], [128 * 1024, 16], [1, 1024]])
    p.dma("sync", cx.SBF[:, 0:16384].rearrange("p (a b) -> p a b", b=1024), src, pkl,
          waits=dpk + pp.pb512.all_done() + pp.pa1024.all_done() + [(pkl, pkl.n)])
    pklt = pkl.n
    ve2 = [p.sem("l0_ve0"), p.sem("l0_ve1")]
    VE = [cx.ROWS[:, 0:1040], cx.ROWS[:, 1040:2080]]
    kvavail = p.sem("l0_kva")
    def kvp(h):
        return cx.PA[0:64, 512 * (h // 7) + (h % 7) * 65: 512 * (h // 7) + (h % 7) * 65 + 65]
    vdone = env["vdone"]
    for lt in range(NT):
        sl = lt % 2
        p.dma("sync", VE[sl], vext[lt * 128:(lt + 1) * 128, :], ve2[sl],
              waits=([(pp.mm, pp.mm.n)] + list(vdone)) if lt < 2 else [(kvavail, kvavail.n)])
        vet = ve2[sl].n
        for h in range(16):
            p.emit("tensor", lambda e, h=h, lt=lt, sl=sl: e.matmul(
                       kvp(h), cx.SBF[:, lt * 1024 + h * 64: lt * 1024 + h * 64 + 64],
                       VE[sl][:, h * 65:(h + 1) * 65],
                       start=(lt == 0), stop=False, skip_group_check=True),
                   waits=([(pkl, pklt), (ve2[sl], vet)] if h == 0 else ()),
                   incs=[(kvavail, 1)] if h == 15 else ())
    kve = p.sem("l0_kve")
    for h in range(16):
        p.emit("tensor", lambda e, h=h: e.matmul(
                   kvp(h), cx.ROWS[0:1, 2080 + h * 64:2080 + (h + 1) * 64],
                   cx.ROWS[0:1, 3392:3457], start=False, stop=True, skip_group_check=True),
               waits=(pks0done + [(zv, zvdone), (kvavail, kvavail.n)]) if h == 0 else (),
               incs=[(pp.mm, 1)] if h == 15 else ())
    kvm = pp.mm.n
    def kvsb(h):
        b = (h % 2) * 64
        c = 14336 + (h // 2) * 65
        return cx.SBF[b:b + 64, c:c + 65]
    for h in range(16):
        p.emit("vector", lambda e, h=h: e.tensor_copy(kvsb(h), kvp(h)),
               waits=[(pp.mm, kvm)] if h == 0 else (), incs=[(kve, 1)])
    kved = kve.n
    phl = p.sem("lf")
    load_fm(cx, phiq, D, T, cx.SRC, 0, phl, waits=[(kve, kved)])
    phr = [(phl, phl.n)]
    attn = dram("l0_attn", [T, D], BF16)
    nd = p.sem("l0_nd")
    ast2 = [p.sem("l0_as0"), p.sem("l0_as1")]
    lastp = [None]
    for h in range(16):
        for tt in range(NT):
            ps, pw, prel = pp.pb512.acquire()
            psum = ps[:, 0:65]
            p.emit("tensor", lambda e, h=h, tt=tt, psum=psum: e.matmul(
                       psum, cx.SRC[(h % 2) * 64:(h % 2) * 64 + 64,
                                    (h // 2) * T + tt * 128: (h // 2) * T + tt * 128 + 128],
                       kvsb(h), start=True, stop=True),
                   waits=phr + pw + ([(kve, kved)] if (h == 0 and tt == 0) else []),
                   incs=[(pp.mm, 1)])
            mmv = pp.mm.n
            dcol = cx.SMALL[:, 728:729]
            p.emit("vector", lambda e, psum=psum: e.tensor_scalar_add(dcol, psum[:, 64:65], 1e-6),
                   waits=[(pp.mm, mmv), (nd, nd.n)] + ([lastp[0]] if lastp[0] else []),
                   incs=[(nd, 1)])
            p.emit("vector", lambda e: e.reciprocal(dcol, dcol),
                   waits=[(nd, nd.n)], incs=[(nd, 1)])
            i = (h * NT + tt) % 2
            stg = cx.SF[0:128, 3072 + i * 64: 3072 + i * 64 + 64]
            p.emit("vector", lambda e, psum=psum, stg=stg: e.tensor_scalar(
                       out=stg, in0=psum[:, 0:64], scalar1=dcol, scalar2=1.0,
                       op0=OP.mult, op1=OP.mult),
                   waits=[(nd, nd.n)] + ([(ast2[i], ast2[i].n)] if ast2[i].n > 0 else []),
                   incs=[prel])
            p.dma("gpsimd", attn[tt * 128:(tt + 1) * 128, h * 64:(h + 1) * 64], stg,
                  ast2[i], waits=[(prel[0], prel[0].n)])
            lastp[0] = (prel[0], prel[0].n)
    attn_done = [(ast2[0], ast2[0].n), (ast2[1], ast2[1].n)]
    return _emit_L0_part4(cx, pp, x_dram, xout, prev, env, env2, locals())


def _emit_L0_part4(cx, pp, x_dram, xout, prev, env, env2, env3):
    p = cx.p
    dram, wbf2, inp = cx.dram, cx.wbf2, cx.inp
    attn, attn_done = env3["attn"], env3["attn_done"]
    srcB, srcA, xn = env["srcB"], env["srcA"], env["xn"]
    GB, BB = env["GB"], env["BB"]
    TP = T + 4
    afp = dram("l0_afp", [D, TP], BF16)
    zb = p.sem("l0_zb")
    zbd = p.sem("l0_zbd")
    zb0 = zb.n
    p.emit("vector", lambda e: e.memset(cx.ROWS[0:128, 3488:3490], 0.0), incs=[(zb, 1)])
    for dc in range(ND):
        for side0 in (0, T + 2):
            p.dma("gpsimd", afp[dc * 128:(dc + 1) * 128, side0:side0 + 2],
                  cx.ROWS[0:128, 3488:3490], zbd, waits=[(zb, zb0 + 1)])
    zbt = zbd.n
    afb = emit_tm2fm(cx, pp, attn, afp, col_off=2, row_stride=TP, prev_done=attn_done)
    gsem = p.sem("l0_g")
    gsemd = p.sem("l0_gd")
    p.dma("gpsimd", cx.ROWS[0:1, 3496:3497],
          flat_view(inp["L0_attn_gamma"], 0, [[1, 1], [1, 1]]), gsemd)
    ps0, pw0, prel0 = pp.pb512.acquire()
    p.emit("tensor", lambda e, ps0=ps0: e.matmul(ps0[:, 0:1], cx.CBF[0:1, 384:512],
                                                 cx.ROWS[0:1, 3496:3497], start=True, stop=True),
           waits=[(gsemd, gsemd.n), (cx.consts_sem, 112)] + pw0, incs=[(pp.mm, 1)])
    p.emit("vector", lambda e, ps0=ps0: e.tensor_copy(cx.SMALL[:, 740:741], ps0[:, 0:1]),
           waits=[(pp.mm, pp.mm.n)], incs=[prel0])
    gcol = [(prel0[0], prel0[0].n)]
    cbb = emit_bias_bcast(cx, pp, inp["L0_attn_conv_b"], 0, 1024, GB)
    p.emit("vector", lambda e: e.tensor_scalar(out=GB, in0=GB, scalar1=cx.SMALL[:, 740:741],
                                               scalar2=1.0, op0=OP.mult, op1=OP.mult),
           waits=cbb + gcol, incs=[(gsem, 1)])
    gcbd = gsem.n
    cl = p.sem("l0_cl")
    srcv = flat_view(afp, 0, [[TP, 128], [128 * TP, 7], [1, TP]])
    p.dma("sync", cx.SRC[:, 0:7 * TP].rearrange("p (k t) -> p k t", k=7), srcv, cl,
          waits=afb + [(zbd, zbt), (cl, cl.n)])
    srcv7 = flat_view(afp, 7 * 128 * TP, [[TP, 128], [1, TP]])
    p.dma("sync", cx.SRC2[:, 0:TP], srcv7, cl, waits=[(cl, cl.n)])
    clt = cl.n
    def afp_sb(dc):
        return cx.SRC[:, dc * TP:(dc + 1) * TP] if dc < 7 else cx.SRC2[:, 0:TP]
    twv = cx.wb["L0_attn_conv_w"]
    cw = p.sem("l0_cw")
    p.dma("sync", cx.WRES[0:64, 0:5120], flat_view(twv, 0, [[5120, 64], [1, 5120]]), cw,
          waits=pp.wguard() + [(cw, cw.n)])
    p.dma("sync", cx.WRES[64:128, 0:5120], flat_view(twv, 0, [[5120, 64], [1, 5120]]), cw,
          waits=[(cw, cw.n)])
    cwt = cw.n
    a2 = dram("l0_a2", [T, D], F32)
    a2st = [p.sem("l0_a2s0"), p.sem("l0_a2s1")]
    avl = [p.sem("l0_avl0"), p.sem("l0_avl1")]
    avs = [cx.SF[:, 0:1024], cx.SF[:, 1024:2048]]
    cch = p.sem("l0_cch")
    for tt in range(NT):
        i = tt % 2
        p.dma("gpsimd", avs[i], attn[tt * 128:(tt + 1) * 128, :], avl[i],
              waits=([(a2st[i], a2st[i].n)] if a2st[i].n > 0 else list(attn_done)) + [(gsem, gcbd)])
        stg = cx.SF[:, 2048 + i * 1024:2048 + i * 1024 + 1024]
        for g in range(16):
            ps, pw, prel = pp.pb512.acquire()
            psum = ps[:, 0:64]
            dc, half = g // 2, (g % 2) * 64
            for k in range(5):
                p.emit("tensor", lambda e, psum=psum, dc=dc, half=half, g=g, k=k, tt=tt: e.matmul(
                           psum,
                           afp_sb(dc)[half:half + 64, tt * 128 + k: tt * 128 + k + 128],
                           cx.WRES[half:half + 64, k * 1024 + g * 64: k * 1024 + g * 64 + 64],
                           start=(k == 0), stop=(k == 4)),
                       waits=([(cl, clt), (cw, cwt)] + pw) if k == 0 else (),
                       incs=[(pp.mm, 1)] if k == 4 else ())
            mmv = pp.mm.n
            p.emit("vector", lambda e, psum=psum, stg=stg, g=g: e.tensor_scalar(
                       out=stg[:, g * 64:(g + 1) * 64], in0=psum,
                       scalar1=cx.SMALL[:, 740:741], scalar2=1.0, op0=OP.mult, op1=OP.mult),
                   waits=[(pp.mm, mmv), (avl[i], avl[i].n)], incs=[prel])
        p.emit("vector", lambda e, stg=stg: e.tensor_add(stg, stg, GB),
               waits=[(s, s.n) for s in pp.pb512.sems], incs=[(cch, 1)])
        p.emit("vector", lambda e, stg=stg, i=i: e.tensor_add(stg, stg, avs[i]),
               waits=[(cch, cch.n)], incs=[(cch, 1)])
        p.dma("gpsimd", a2[tt * 128:(tt + 1) * 128, :], stg, a2st[i],
              waits=[(cch, cch.n)])
    a2done = [(a2st[0], a2st[0].n), (a2st[1], a2st[1].n)]
    return _emit_L0_part5(cx, pp, x_dram, xout, prev, env, env2, env3, locals())


def _emit_L0_part5(cx, pp, x_dram, xout, prev, env, env2, env3, env4):
    p = cx.p
    dram, wbf2, inp = cx.dram, cx.wbf2, cx.inp
    a2, a2done = env4["a2"], env4["a2done"]
    srcB, srcA, xn = env["srcB"], env["srcA"], env["xn"]
    GB, BB = env["GB"], env["BB"]
    gb = emit_bias_bcast(cx, pp, inp["L0_attn_lna_g"], 0, 1024, GB)
    bb = emit_bias_bcast(cx, pp, inp["L0_attn_lna_b"], 0, 1024, BB)
    a2f = dram("l0_a2f", [D, T], BF16)
    a2t = dram("l0_a2t", [T, D], BF16)
    lnd = emit_lnG(cx, pp, a2, a2f, 64, 80, GB, BB, prev_done=a2done + gb + bb,
                   tm_dram=a2t)
    al = p.sem("lf")
    load_fm(cx, a2f, D, T, cx.SRC, 0, al, waits=lnd)
    ar = [(al, al.n)]
    aB = lambda dc: cx.SRC[:, dc * T:(dc + 1) * T]
    colf, b1done, _ = vecs_to_cols(cx, pp, [(inp["L0_attn_b1"], 0, 4096, 1.0)], 688)
    hd = dram("l0_h", [4096, T], BF16)
    projB(cx, pp, src_tile=aB, K=D, Tn=T, wdram=wbf2("L0_attn_W1", (1024, 4096)),
          wT=False, N=4096, dst_dram=hd, src_ready=ar + b1done,
          bias_col=lambda oc: colf(0, oc), act=AF.Gelu)
    b2b = p.sem("l0_b2")
    p.dma("gpsimd", cx.ROWS[0:1, 2176:3200],
          flat_view(inp["L0_attn_b2"], 0, [[1, 1], [1, 1024]]), b2b,
          waits=[(pp.mm, pp.mm.n)])
    b2t = b2b.n
    s0 = dram("l0_s", [T, D], F32)
    sst = [p.sem("l0_ss0"), p.sem("l0_ss1")]
    sld = [p.sem("l0_sl0"), p.sem("l0_sl1")]
    s_sl = [cx.SF[:, 0:1024], cx.SF[:, 1024:2048]]
    x_sl = [cx.SF[:, 2048:3072], cx.SF[:, 3072:4096]]
    sch = p.sem("l0_sch")
    al2 = [p.sem("l0_al0"), p.sem("l0_al1")]
    xl2 = [p.sem("l0_xl0"), p.sem("l0_xl1")]
    a_sl = [cx.SF[:, 4096:5120], cx.SF[:, 5120:6144]]
    for half in range(2):
        w2v = flat_view(cx.wb["L0_attn_W2"], half * 2048 * 1024, [[1024, 2048], [1, 1024]])
        hstream = (hd, cx.SBF[:, 8192:10240], cx.SBF[:, 10240:12288], half * 2048)
        def s0_cb(tt, psum, stage, waits, prel, half=half):
            i = tt % 2
            if half == 0:
                p.emit("vector", lambda e, psum=psum, stage=stage: e.tensor_copy(stage, psum),
                       waits=list(waits) + ([(sst[i], sst[i].n)] if sst[i].n > 0 else []),
                       incs=[prel])
            else:
                swp = [(sst[i], 16 * (8 + tt // 2))]
                p.dma("sync", s_sl[i], s0[tt * 128:(tt + 1) * 128, :], sld[i], waits=swp)
                p.dma("gpsimd", a_sl[i], a2t[tt * 128:(tt + 1) * 128, :], al2[i], waits=swp)
                p.dma("sync", x_sl[i], x_dram[tt * 128:(tt + 1) * 128, :], xl2[i])
                p.emit("vector", lambda e, psum=psum, stage=stage, i=i: e.tensor_add(
                           stage, psum, s_sl[i]),
                       waits=list(waits) + [(sld[i], sld[i].n)], incs=[(sch, 1)])
                p.emit("vector", lambda e, stage=stage, i=i, tt=tt: e.tensor_scalar(
                           out=x_sl[i], in0=x_sl[i], scalar1=cx.SMALL[:, 0 + tt:0 + tt + 1],
                           scalar2=cx.SMALL[:, 16 + tt:16 + tt + 1],
                           op0=OP.subtract, op1=OP.mult),
                       waits=[(xl2[i], xl2[i].n), (sch, sch.n)], incs=[(sch, 1)])
                p.emit("vector", lambda e, stage=stage, i=i: e.tensor_add(stage, stage, x_sl[i]),
                       waits=[(sch, sch.n)], incs=[(sch, 1)])
                p.emit("vector", lambda e, stage=stage, i=i: e.tensor_add(stage, stage, a_sl[i]),
                       waits=[(al2[i], al2[i].n), (sch, sch.n)], incs=[prel])
            p.dma("gpsimd", s0[tt * 128:(tt + 1) * 128, :],
                  stage, sst[i], waits=[(prel[0], prel[0].n)])
        projA(cx, pp, src_stream=hstream, K=2048, Tn=T, wdram=w2v, wT=False, N=D,
              src_ready=[(b2b, b2t)], evac_cb=s0_cb, store=False,
              extra_row=cx.ROWS[0:1, 2176:3200] if half == 0 else None)
    s0done = [(sst[0], sst[0].n), (sst[1], sst[1].n)]
    return _emit_L0_part6(cx, pp, x_dram, xout, prev, env, s0, s0done)


def _emit_L0_part6(cx, pp, x_dram, xout, prev, env, s0, s0done):
    p = cx.p
    dram, wbf2, inp = cx.dram, cx.wbf2, cx.inp
    srcA, srcB, xn = env["srcA"], env["srcB"], env["xn"]
    GB = env["GB"]
    xl = p.sem("lf")
    load_fm(cx, xn, D, T, cx.SRC, 0, xl, waits=s0done)
    xr = [(xl, xl.n)]
    xA = lambda dc, tt: cx.SRC[:, dc * T + tt * 128: dc * T + tt * 128 + 128]
    COEF = 416
    rdone = emit_router(cx, pp, src_tile=xA, src_ready=xr,
                        wdram=wbf2("L0_ffn_gate_w", (1024, 16)), wT=False, E=16,
                        bias_row_dram=None, mode="soft_raw", coeff_col0=COEF)
    acc = dram("l0_acc", [T, D], F32)
    acld = [p.sem("mo_ld0"), p.sem("mo_ld1")]
    msem = [p.sem("mo_st0"), p.sem("mo_st1")]
    m0 = [msem[0].n, msem[1].n]
    a0 = [acld[0].n, acld[1].n]
    mst = [cx.SF[:, 4096:5120], cx.SF[:, 5120:6144]]
    ech = p.sem("l0_ech")
    for e in range(16):
        bb = emit_bias_bcast(cx, pp, inp["L0_ffn_expert_b"], e * 1024, 1024, GB)
        wv = flat_view(cx.wb["L0_ffn_expert_w"], e * 1024 * 1024, [[1024, 1024], [1, 1024]])
        def moe_cb(tt, psum, stage, waits, prel, pas=e):
            i = tt % 2
            nst = pas * 8 + tt // 2
            ms = mst[i]
            w = list(waits) + bb + ([(msem[i], m0[i] + 16 * nst)] if nst > 0 else [])
            p.emit("vector", lambda e2, psum=psum, stage=stage: e2.tensor_add(stage, psum, GB),
                   waits=w, incs=[(ech, 1)])
            if pas == 0:
                p.emit("vector", lambda e2, stage=stage, ms=ms, tt=tt, pas=pas: e2.tensor_scalar(
                           out=ms, in0=stage, scalar1=0.0,
                           scalar2=cx.SMALL[:, COEF + tt * 16 + pas:COEF + tt * 16 + pas + 1],
                           op0=OP.max, op1=OP.mult),
                       waits=[(ech, ech.n)], incs=[prel])
            else:
                slot = cx.SF[:, i * 1024: i * 1024 + 1024]
                p.dma("sync", slot, acc[tt * 128:(tt + 1) * 128, :], acld[i],
                      waits=[(msem[i], m0[i] + 16 * (pas * 8 + tt // 2))])
                p.emit("vector", lambda e2, stage=stage, tt=tt, pas=pas: e2.tensor_scalar(
                           out=stage, in0=stage, scalar1=0.0,
                           scalar2=cx.SMALL[:, COEF + tt * 16 + pas:COEF + tt * 16 + pas + 1],
                           op0=OP.max, op1=OP.mult),
                       waits=[(ech, ech.n)], incs=[(ech, 1)])
                p.emit("vector", lambda e2, stage=stage, ms=ms, slot=slot: e2.tensor_add(
                           ms, stage, slot),
                       waits=[(ech, ech.n), (acld[i], acld[i].n)], incs=[prel])
            p.dma("gpsimd", acc[tt * 128:(tt + 1) * 128, :], ms, msem[i],
                  waits=[(prel[0], prel[0].n)])
        projA(cx, pp, src_tile=xA, K=D, Tn=T, wdram=wv, wT=False, N=D,
              src_ready=xr + (rdone if e == 0 else []), evac_cb=moe_cb, store=False)
    upd = emit_update(cx, pp, x_dram, s0, acc, xout,
                      pre_done=[(msem[0], msem[0].n), (msem[1], msem[1].n)] + s0done)
    return upd


LAYER_EMITTERS[0] = emit_L0



def emit_fp4(cx, pp, src_dram, dst_fm, R, C, pre_ln=None, src_bf=False, prev_done=(),
             tmul=None):
    p = cx.p
    nb = C // 16
    nr = R // 128
    nc_ = C // 128
    eng = "gpsimd" if src_bf else "sync"
    ld2 = [p.sem("q_l0" + eng), p.sem("q_l1" + eng)]
    st2 = [p.sem("q_s0"), p.sem("q_s1")]
    ch = p.sem("chv")
    tpd = p.sem("q_tp")
    cpd = p.sem("q_cp")
    tp0, cp0 = tpd.n, cpd.n
    slots = [cx.SF[:, 0:1024], cx.SF[:, 1024:2048]]
    big = C > 1024
    if big:
        slots = [cx.SF[:, 0:2816], cx.SF[:, 2816:5632]]
    scs = cx.SF[:, 5632:5632 + 352]
    sgn = cx.SBF[:, 2048:2048 + 2816]
    acc = cx.SF[:, 5888:5888 + 2816] if False else None
    qtm = [cx.SBF[:, 7680:7680 + 2816], cx.SBF[:, 10496:10496 + 2816]]
    fms = [cx.SBF[:, 13312:14336], cx.SBF[:, 14336:15360]]
    for rt in range(nr):
        i = rt % 2
        slot = slots[i]
        w0 = ([(ch, ch.n)] if rt >= 2 else list(prev_done)) + [(ld2[i], ld2[i].n)]
        p.dma(eng, slot, src_dram[rt * 128:(rt + 1) * 128, :], ld2[i], waits=w0)
        wv = [(ld2[i], ld2[i].n)]
        if pre_ln is not None:
            MC0, RC0 = pre_ln
            p.emit("vector", lambda e, slot=slot, rt=rt: e.tensor_scalar(
                       out=slot, in0=slot, scalar1=cx.SMALL[:, pre_ln[0] + rt:pre_ln[0] + rt + 1],
                       scalar2=cx.SMALL[:, pre_ln[1] + rt:pre_ln[1] + rt + 1],
                       op0=OP.subtract, op1=OP.mult),
                   waits=wv, incs=[(ch, 1)])
            wv = [(ch, ch.n)]
        s3 = slot.rearrange("p (b k) -> p b k", k=16)
        p.emit("vector", lambda e, s3=s3: e.reduce_max(
                   scs[:, 0:nb], s3, axis=AX.X, apply_absolute_value=True),
               waits=wv, incs=[(ch, 1)])
        p.emit("vector", lambda e: e.tensor_scalar(out=scs[:, 0:nb], in0=scs[:, 0:nb],
                                                   scalar1=1.0 / 6.0, scalar2=1e-12,
                                                   op0=OP.mult, op1=OP.max),
               waits=[(ch, ch.n)], incs=[(ch, 1)])
        p.emit("vector", lambda e: e.reciprocal(scs[:, nb:2 * nb], scs[:, 0:nb]),
               waits=[(ch, ch.n)], incs=[(ch, 1)])
        r3 = scs[:, nb:2 * nb].rearrange("p b -> p b 1" if False else "p (b o) -> p b o", o=1)
        p.emit("vector", lambda e, s3=s3, r3=r3: e.tensor_tensor(
                   s3, s3, r3.broadcast_to((128, nb, 16)), op=OP.mult),
               waits=[(ch, ch.n)], incs=[(ch, 1)])
        p.emit("scalar", lambda e, slot=slot: e.activation(out=sgn[:, 0:C], in_=slot, func=AF.Sign),
               waits=[(ch, ch.n)], incs=[(pp.evs, 1)])
        p.emit("vector", lambda e, slot=slot: e.tensor_mul(slot, slot, sgn[:, 0:C]),
               waits=[(pp.evs, pp.evs.n)], incs=[(ch, 1)])
        qacc = qtm[i][:, 0:C]
        first = True
        for thr, stepv in [(0.25, 0.5), (0.75, 0.5), (1.25, 0.5), (1.75, 0.5),
                           (2.5, 1.0), (3.5, 1.0), (5.0, 2.0)]:
            if first:
                p.emit("vector", lambda e, slot=slot, qacc=qacc, thr=thr, stepv=stepv:
                           e.tensor_scalar(out=qacc, in0=slot, scalar1=thr, scalar2=stepv,
                                           op0=OP.is_gt, op1=OP.mult),
                       waits=[(ch, ch.n)], incs=[(ch, 1)])
                first = False
            else:
                p.emit("vector", lambda e, slot=slot, thr=thr, stepv=stepv:
                           e.tensor_scalar(out=sgn[:, 2816:2816 + C] if False else cx.SBF[:, 4864:4864 + C],
                                           in0=slot, scalar1=thr, scalar2=stepv,
                                           op0=OP.is_gt, op1=OP.mult),
                       waits=[(ch, ch.n)], incs=[(ch, 1)])
                p.emit("vector", lambda e, qacc=qacc: e.tensor_add(
                           qacc, qacc, cx.SBF[:, 4864:4864 + C]),
                       waits=[(ch, ch.n)], incs=[(ch, 1)])
        p.emit("vector", lambda e, qacc=qacc: e.tensor_mul(qacc, qacc, sgn[:, 0:C]),
               waits=[(ch, ch.n)], incs=[(ch, 1)])
        q3 = qacc.rearrange("p (b k) -> p b k", k=16)
        sc3 = scs[:, 0:nb].rearrange("p (b o) -> p b o", o=1)
        p.emit("vector", lambda e, q3=q3, sc3=sc3: e.tensor_tensor(
                   q3, q3, sc3.broadcast_to((128, nb, 16)), op=OP.mult),
               waits=[(ch, ch.n)], incs=[(ch, 1)])
        qdone = ch.n
        for dc in range(nc_):
            p.emit("tensor", lambda e, qacc=qacc, dc=dc: e.transpose(
                       cx.PT[:, (dc % 8) * 128:(dc % 8) * 128 + 128],
                       qacc[:, dc * 128:(dc + 1) * 128], ident(cx)),
                   waits=([(ch, qdone)] if dc == 0 else []) + ([(cpd, cpd.n)] if dc % 8 == 0 else []),
                   incs=[(tpd, 1)] if (dc % 8 == 7 or dc == nc_ - 1) else ())
            if dc % 8 == 7 or dc == nc_ - 1:
                lo = (dc // 8) * 8
                n8 = dc - lo + 1
                fs = fms[i]
                p.emit("vector", lambda e, fs=fs, n8=n8: e.tensor_copy(
                           fs[:, 0:n8 * 128], cx.PT[:, 0:n8 * 128]),
                       waits=[(tpd, tpd.n)] + ([(st2[i], st2[i].n)] if st2[i].n > 0 else []),
                       incs=[(cpd, 1)])
                dst = flat_view(dst_fm, lo * 128 * R + rt * 128,
                                [[R, 128], [128 * R, n8], [1, 128]])
                p.dma("gpsimd", dst, fs[:, 0:n8 * 128].rearrange("p (k t) -> p k t", k=n8),
                      st2[i], waits=[(cpd, cpd.n)])
    return [(st2[0], st2[0].n), (st2[1], st2[1].n)]


def emit_fm2tm(cx, pp, src_fm, dst_tm, prev_done=()):
    p = cx.p
    ld2 = [p.sem("f2t_l0"), p.sem("f2t_l1")]
    st2 = [p.sem("ln_st0"), p.sem("ln_st1")]
    tpdone = p.sem("ln_tp")
    cpdone = p.sem("ln_cp")
    tp0, cp0 = tpdone.n, cpdone.n
    ins = [cx.SBF[:, 0:1024], cx.SBF[:, 1024:2048]]
    outs = [cx.SBF[:, 2048:3072], cx.SBF[:, 3072:4096]]
    for tt in range(NT):
        i = tt % 2
        blk = ins[i]
        fs = outs[i]
        waits = ([(cpdone, cp0 + tt - 1)] if tt >= 2 else list(prev_done)) \
            + [(cx.consts_sem, 112), (ld2[i], ld2[i].n)]
        srcv = flat_view(src_fm, tt * 128, [[T, 128], [128 * T, ND], [1, 128]])
        p.dma("sync", blk.rearrange("p (k t) -> p k t", k=ND), srcv, ld2[i], waits=waits)
        for dc in range(ND):
            p.emit("tensor", lambda e, blk=blk, dc=dc: e.transpose(
                       cx.PT[:, dc * 128:(dc + 1) * 128], blk[:, dc * 128:(dc + 1) * 128], ident(cx)),
                   waits=([(ld2[i], ld2[i].n)] if dc == 0 else []) + ([(cpdone, cp0 + tt)] if (dc == 0 and tt >= 1) else []),
                   incs=[(tpdone, 1)] if dc == ND - 1 else ())
        p.emit("vector", lambda e, fs=fs: e.tensor_copy(fs, cx.PT[:, :]),
               waits=[(tpdone, tp0 + tt + 1)] + ([(st2[i], st2[i].n)] if st2[i].n > 0 else []),
               incs=[(cpdone, 1)])
        p.dma("gpsimd", dst_tm[tt * 128:(tt + 1) * 128, :], fs, st2[i],
              waits=[(cpdone, cp0 + tt + 1)])
    return [(st2[0], st2[0].n), (st2[1], st2[1].n)]


def emit_ropephi(cx, pp, src_fm, dst_fm, do_phi, prev_done=()):
    p = cx.p
    ld2 = [p.sem("rp_l0"), p.sem("rp_l1")]
    st2 = [p.sem("ln_st0"), p.sem("ln_st1")]
    ch = p.sem("chv")
    COS, SIN = cx.SF[:, 4096:6144], cx.SF[:, 6144:8192]
    ins = [cx.SBF[:, 0:2048], cx.SBF[:, 2048:4096]]
    stg = [cx.SBF[:, 4096:6144], cx.SBF[:, 6144:8192]]
    tmp = cx.SBF[:, 8192:10240]
    tmp2 = cx.SBF[:, 10240:12288]
    for dc in range(ND):
        i = dc % 2
        blk = ins[i]
        out = stg[i]
        w0 = ([(ch, ch.n)] if dc >= 2 else list(prev_done)) + [(ld2[i], ld2[i].n)]
        p.dma("sync", blk, src_fm[dc * 128:(dc + 1) * 128, :], ld2[i], waits=w0)
        for ncc in range(4):
            ps, pw, prel = pp.pb512.acquire()
            psum = ps[:, 0:512]
            sl = slice(ncc * 512, (ncc + 1) * 512)
            p.emit("tensor", lambda e, psum=psum, blk=blk, sl=sl: e.matmul(
                       psum, cx.CBF[:, 256:384], blk[:, sl], start=True, stop=True),
                   waits=[(ld2[i], ld2[i].n)] + pw, incs=[(pp.mm, 1)])
            mmv = pp.mm.n
            p.emit("vector", lambda e, psum=psum, sl=sl: e.tensor_mul(
                       tmp[:, sl], psum, SIN[:, sl]),
                   waits=[(pp.mm, mmv)], incs=[prel])
            p.emit("vector", lambda e, blk=blk, sl=sl: e.tensor_mul(
                       tmp2[:, sl], blk[:, sl], COS[:, sl]),
                   waits=[(prel[0], prel[0].n)], incs=[(ch, 1)])
            p.emit("vector", lambda e, out=out, sl=sl: e.tensor_add(
                       tmp2[:, sl], tmp2[:, sl], tmp[:, sl]),
                   waits=[(ch, ch.n)], incs=[(ch, 1)])
            if do_phi:
                p.emit("vector", lambda e, sl=sl: e.tensor_scalar_min(
                           tmp[:, sl], tmp2[:, sl], 0.0),
                       waits=[(ch, ch.n)] + ([(st2[i], st2[i].n)] if ncc == 0 else []),
                       incs=[(ch, 1)])
                p.emit("scalar", lambda e, sl=sl: e.activation(
                           out=tmp[:, sl], in_=tmp[:, sl], func=AF.Exp),
                       waits=[(ch, ch.n)], incs=[(pp.evs, 1)])
                p.emit("vector", lambda e, sl=sl: e.tensor_scalar_max(
                           tmp2[:, sl], tmp2[:, sl], 0.0),
                       waits=[(pp.evs, pp.evs.n), (ch, ch.n)], incs=[(ch, 1)])
                p.emit("vector", lambda e, out=out, sl=sl: e.tensor_add(
                           out[:, sl], tmp2[:, sl], tmp[:, sl]),
                       waits=[(ch, ch.n)], incs=[(ch, 1)])
            else:
                p.emit("vector", lambda e, out=out, sl=sl: e.tensor_copy(
                           out[:, sl], tmp2[:, sl]),
                       waits=[(ch, ch.n)] + ([(st2[i], st2[i].n)] if ncc == 0 else []),
                       incs=[(ch, 1)])
        p.dma("gpsimd", dst_fm[dc * 128:(dc + 1) * 128, :], out, st2[i],
              waits=[(ch, ch.n)])
    return [(st2[0], st2[0].n), (st2[1], st2[1].n)]



def emit_L1(cx, pp, x_dram, xout, prev):
    p = cx.p
    dram, wbf2, inp = cx.dram, cx.wbf2, cx.inp
    xn = dram("l1_xn", [D, T], BF16)
    lnst = emit_ln(cx, pp, x_dram, xn, 0, 16, cx.consts_sem, prev_done=prev)
    GB, BB = cx.SF[:, 6144:7168], cx.SF[:, 7168:8192]
    xq = dram("l1_xq", [D, T], BF16)
    f1 = emit_fp4(cx, pp, x_dram, xq, T, D, pre_ln=(0, 16), prev_done=lnst)
    wgq = dram("l1_wgq", [D, 2816], BF16)
    wuq = dram("l1_wuq", [D, 2816], BF16)
    wdq = dram("l1_wdq", [2816, D], BF16)
    f2 = emit_fp4(cx, pp, flat_view(cx.wb["L1_ffn_w_gate"], 0, [[1024, 2816], [1, 1024]]),
                  wgq, 2816, 1024, src_bf=True, prev_done=[(cx.pcsem, cx.pctarget)] + f1)
    f3 = emit_fp4(cx, pp, flat_view(cx.wb["L1_ffn_w_up"], 0, [[1024, 2816], [1, 1024]]),
                  wuq, 2816, 1024, src_bf=True, prev_done=f2)
    f4 = emit_fp4(cx, pp, flat_view(cx.wb["L1_ffn_w_down"], 0, [[2816, 1024], [1, 2816]]),
                  wdq, 1024, 2816, src_bf=True, prev_done=f3)
    xl = p.sem("lf")
    load_fm(cx, xq, D, T, cx.SRC, 0, xl, waits=f4)
    xqr = [(xl, xl.n)]
    xqA = lambda dc, tt: cx.SRC[:, dc * T + tt * 128: dc * T + tt * 128 + 128]
    gated = dram("l1_gate", [T, 2816], F32)
    upd_ = dram("l1_up", [T, 2816], F32)
    gsl = [cx.SF[:, 0:1024], cx.SF[:, 1024:2048]]
    gvst = [p.sem("vx_st0"), p.sem("vx_st1")]
    for nb0 in range(3):
        n0 = nb0 * 1024
        nw = min(1024, 2816 - n0)
        for wsrc, dst in [(wgq, gated), (wuq, upd_)]:
            wv = flat_view(wsrc, n0, [[2816, 1024], [1, 2816]])
            def bandcb(tt, psum, stage, waits, prel, dst=dst, n0=n0, nw=nw):
                i = tt % 2
                ms = gsl[i]
                p.emit("vector", lambda e, psum=psum, ms=ms, nw=nw: e.tensor_copy(
                           ms[:, 0:nw], psum[:, 0:nw]),
                       waits=list(waits) + [(gvst[i], gvst[i].n)], incs=[prel])
                p.dma("gpsimd", dst[tt * 128:(tt + 1) * 128, n0:n0 + nw], ms[:, 0:nw],
                      gvst[i], waits=[(prel[0], prel[0].n)])
            projA(cx, pp, src_tile=xqA, K=D, Tn=T, wdram=wv, wT=False, N=nw,
                  src_ready=xqr, evac_cb=bandcb, store=False)
    gdone = [(gvst[0], gvst[0].n), (gvst[1], gvst[1].n)]
    hq = dram("l1_hq", [2816, T], BF16)
    hch = p.sem("chv")
    hst = [p.sem("vx_st0"), p.sem("vx_st1")]
    htm = dram("l1_htm", [T, 2816], F32)
    hl2 = [p.sem("up_ld0"), p.sem("up_ld1")]
    for tt in range(NT):
        i = tt % 2
        ga = cx.SF[:, 0:2816]
        ua = cx.SF[:, 2816:5632]
        w0 = (gdone if tt == 0 else []) + [(hch, hch.n), (hst[0], hst[0].n),
                                           (hst[1], hst[1].n), (hl2[i], hl2[i].n)]
        p.dma("sync", ga, gated[tt * 128:(tt + 1) * 128, :], hl2[i], waits=w0)
        p.dma("sync", ua, upd_[tt * 128:(tt + 1) * 128, :], hl2[i])
        p.emit("scalar", lambda e, ga=ga: e.activation(out=ga, in_=ga, func=AF.Silu),
               waits=[(hl2[i], hl2[i].n)], incs=[(pp.evs, 1)])
        p.emit("vector", lambda e, ga=ga, ua=ua: e.tensor_mul(ga, ga, ua),
               waits=[(pp.evs, pp.evs.n)], incs=[(hch, 1)])
        p.dma("gpsimd", htm[tt * 128:(tt + 1) * 128, :], ga, hst[i],
              waits=[(hch, hch.n)])
    f5 = emit_fp4(cx, pp, htm, hq, T, 2816,
                  prev_done=[(hst[0], hst[0].n), (hst[1], hst[1].n)])
    md = dram("l1_m", [T, D], F32)
    hstream = (hq, cx.SBF[:, 0:2816], cx.SBF[:, 2816:5632], 0)
    wdv = flat_view(wdq, 0, [[1024, 2816], [1, 1024]])
    mdone = projA(cx, pp, src_stream=hstream, K=2816, Tn=T, wdram=wdv, wT=False, N=D,
                  dst_dram=md, src_ready=f5)
    return _emit_L1_part2(cx, pp, x_dram, xout, prev, locals())


def _emit_L1_part2(cx, pp, x_dram, xout, prev, env):
    p = cx.p
    dram, wbf2, inp = cx.dram, cx.wbf2, cx.inp
    xn, lnst, mdone = env["xn"], env["lnst"], env["mdone"]
    GB, BB = env["GB"], env["BB"]
    n1b = emit_bias_bcast(cx, pp, inp["L1_attn_norm1_scale"], 0, 1024, GB)
    r1 = dram("l1_r1", [D, T], BF16)
    r1d = emit_rms(cx, pp, x_dram, r1, 64, 80, 0.0, g_bcast=GB, pre_ln=(0, 16),
                   prev_done=mdone + n1b)
    rl = p.sem("lf")
    load_fm(cx, r1, D, T, cx.SRC, 0, rl, waits=r1d)
    rr = [(rl, rl.n)]
    rB = lambda dc: cx.SRC[:, dc * T:(dc + 1) * T]
    rA = lambda dc, tt: cx.SRC[:, dc * T + tt * 128: dc * T + tt * 128 + 128]
    vecs = [(inp["L1_attn_bq"], 0, 1024, 1.0), (inp["L1_attn_bk"], 0, 1024, 1.0)]
    colf, bdone, _ = vecs_to_cols(cx, pp, vecs, 128)
    q0 = dram("l1_q0", [D, T], BF16)
    k0 = dram("l1_k0", [D, T], BF16)
    projB(cx, pp, src_tile=rB, K=D, Tn=T, wdram=wbf2("L1_attn_Wq", (1024, 1024)),
          wT=False, N=D, dst_dram=q0, src_ready=rr + bdone, bias_col=lambda oc: colf(0, oc))
    projB(cx, pp, src_tile=rB, K=D, Tn=T, wdram=wbf2("L1_attn_Wk", (1024, 1024)),
          wT=False, N=D, dst_dram=k0, src_ready=rr, bias_col=lambda oc: colf(1, oc))
    bvb = emit_bias_bcast(cx, pp, inp["L1_attn_bv"], 0, 1024, BB)
    vext = dram("l1_vext", [T, 1040], BF16)
    vsl = [cx.SF[:, 0:1040], cx.SF[:, 1040:2080]]
    onesem = p.sem("vx_on")
    for i in range(2):
        ov = vsl[i].rearrange("p (a b) -> p a b", b=65)[:, :, 64:65]
        p.emit("vector", lambda e, ov=ov: e.memset(ov, 1.0),
               waits=[(p.sem("vx_st0" if i == 0 else "vx_st1"),
                       p.sem("vx_st0" if i == 0 else "vx_st1").n)], incs=[(onesem, 1)])
    onest = onesem.n
    vst2 = [p.sem("vx_st0"), p.sem("vx_st1")]
    vp = [None, None]
    def vext_cb(tt, psum, stage, waits, prel):
        i = tt % 2
        slot = vsl[i]
        sv = slot.rearrange("p (a b) -> p a b", b=65)[:, :, 0:64]
        p.emit("vector", lambda e, psum=psum, sv=sv: e.tensor_add(
                   sv, psum.rearrange("p (a b) -> p a b", b=64),
                   BB.rearrange("p (a b) -> p a b", b=64)),
               waits=list(waits) + bvb + [(onesem, onest)] +
                     ([vp[i]] if vp[i] else []),
               incs=[prel])
        p.dma("gpsimd", vext[tt * 128:(tt + 1) * 128, :], slot, vst2[i],
              waits=[(prel[0], prel[0].n)])
        vp[i] = (vst2[i], vst2[i].n)
    projA(cx, pp, src_tile=rA, K=D, Tn=T, wdram=wbf2("L1_attn_Wv", (1024, 1024)),
          wT=False, N=D, src_ready=rr, evac_cb=vext_cb, store=False)
    vdone = [(vst2[0], vst2[0].n), (vst2[1], vst2[1].n)]
    bgb = emit_bias_bcast(cx, pp, inp["L1_attn_bg"], 0, 1024, BB)
    gd = dram("l1_g", [T, D], BF16)
    gch = p.sem("chv")
    def g_cb(tt, psum, stage, waits, prel):
        p.emit("vector", lambda e, psum=psum, stage=stage: e.tensor_add(stage, psum, BB),
               waits=list(waits) + bgb, incs=[(gch, 1)])
        p.emit("scalar", lambda e, stage=stage: e.activation(out=stage, in_=stage, func=AF.Sigmoid),
               waits=[(gch, gch.n)], incs=[prel])
    gdone = projA(cx, pp, src_tile=rA, K=D, Tn=T, wdram=wbf2("L1_attn_Wg", (1024, 1024)),
                  wT=False, N=D, dst_dram=gd, src_ready=rr, evac_cb=g_cb)
    rc = p.sem("l1_rc")
    p.dma("sync", cx.SF[:, 4096:6144], cx.cdram["c_ropecos"][:, :], rc,
          waits=list(gdone) + vdone + [(rc, rc.n)])
    p.dma("sync", cx.SF[:, 6144:8192], cx.cdram["c_ropesin"][:, :], rc,
          waits=[(rc, rc.n)])
    rct = rc.n
    q1 = dram("l1_q1", [D, T], BF16)
    k1 = dram("l1_k1", [D, T], BF16)
    rp1 = emit_ropephi(cx, pp, q0, q1, True,
                       prev_done=[(rc, rct)] + pp.stage1024.all_done())
    rp2 = emit_ropephi(cx, pp, k0, k1, True, prev_done=rp1)
    ktm = dram("l1_ktm", [T, D], BF16)
    ktd = emit_fm2tm(cx, pp, k1, ktm, prev_done=rp2)
    return _emit_L1_part3(cx, pp, x_dram, xout, prev, env, locals())


def _emit_L1_part3(cx, pp, x_dram, xout, prev, env, env2):
    p = cx.p
    dram, wbf2, inp = cx.dram, cx.wbf2, cx.inp
    xn = env["xn"]
    q1, ktm, vext, gd = env2["q1"], env2["ktm"], env2["vext"], env2["gd"]
    ktd, gdone, vdone = env2["ktd"], env2["gdone"], env2["vdone"]
    GB, BB = env["GB"], env["BB"]
    ql = p.sem("lf")
    load_fm(cx, q1, D, T, cx.SRC, 0, ql, waits=ktd)
    qr = [(ql, ql.n)]
    def svp(h):
        return cx.PA[0:64, 512 * (h // 7) + (h % 7) * 65: 512 * (h // 7) + (h % 7) * 65 + 65]
    kld = [p.sem("l1_kl0"), p.sem("l1_kl1")]
    vld = [p.sem("l1_vl0"), p.sem("l1_vl1")]
    gld = [p.sem("l1_gl0"), p.sem("l1_gl1")]
    KC = [cx.SBF[:, 0:512], cx.SBF[:, 512:1024]]
    KCs = [cx.SBF[0:64, 0:1024], cx.SBF[0:64, 1024:2048]]
    VCs = [cx.SBF[0:64, 2048:3088], cx.SBF[0:64, 3088:4128]]
    GCs = [cx.SBF[0:64, 4128:5152], cx.SBF[0:64, 5152:6176]]
    def svsb(h):
        b = (h % 2) * 64
        return cx.SBF[b:b + 64, 6176 + (h // 2) * 65: 6176 + (h // 2) * 65 + 65]
    sevac = p.sem("chv")
    avs = p.sem("l1_av")
    ost = [p.sem("vx_st0"), p.sem("vx_st1")]
    odone = p.sem("l1_od")
    attn = dram("l1_attn", [T, D], BF16)
    lastp = [None]
    dch = p.sem("l1_dch")
    p.emit("vector", lambda e: e.memset(cx.SMALL[0:1, 730:731], 0.0))
    for c in range(32):
        i = c % 2
        cw = ([(avs, avs.n)] if c >= 2 else (qr + list(vdone) + list(gdone)))
        p.dma("sync", KCs[i], ktm[c * 64:(c + 1) * 64, :], kld[i],
              waits=cw + [(kld[i], kld[i].n)])
        p.dma("sync", VCs[i], vext[c * 64:(c + 1) * 64, :], vld[i],
              waits=[(vld[i], vld[i].n)])
        p.dma("sync", GCs[i], gd[c * 64:(c + 1) * 64, :], gld[i],
              waits=[(gld[i], gld[i].n)])
        lt = [(kld[i], kld[i].n), (vld[i], vld[i].n), (gld[i], gld[i].n)]
        for h in range(16):
            p.emit("tensor", lambda e, h=h, i=i, c=c: e.matmul(
                       svp(h), KCs[i][:, h * 64:(h + 1) * 64], VCs[i][:, h * 65:(h + 1) * 65],
                       start=(c == 0), stop=(c == 31), skip_group_check=True),
                   waits=(lt + ([(sevac, sevac.n)] if c > 0 else [])) if h == 0 else (),
                   incs=[(pp.mm, 1)] if h == 15 else ())
        mmv = pp.mm.n
        for h in range(16):
            p.emit("vector", lambda e, h=h: e.tensor_copy(svsb(h), svp(h)),
                   waits=[(pp.mm, mmv)] if h == 0 else (), incs=[(sevac, 1)])
        sevt = sevac.n
        for h in range(16):
            ps, pw, prel = pp.pb512.acquire()
            psum = ps[0:64, 0:65]
            hb = (h % 2) * 64
            p.emit("tensor", lambda e, h=h, c=c, psum=psum, hb=hb: e.matmul(
                       psum, cx.SRC[hb:hb + 64, (h // 2) * T + c * 64: (h // 2) * T + c * 64 + 64],
                       svsb(h), start=True, stop=True),
                   waits=[(sevac, sevt)] + pw, incs=[(pp.mm, 1)])
            mv = pp.mm.n
            dcol = cx.SMALL[:, 729:730]
            p.emit("vector", lambda e, psum=psum: e.tensor_scalar_add(
                       dcol[0:64, :], psum[:, 64:65], 1e-6),
                   waits=[(pp.mm, mv), (dch, dch.n)] + ([lastp[0]] if lastp[0] else []),
                   incs=[(dch, 1)])
            p.emit("vector", lambda e: e.reciprocal(dcol[0:64, :], dcol[0:64, :]),
                   waits=[(dch, dch.n)], incs=[(dch, 1)])
            oi = (c * 16 + h) % 2
            stg = cx.SF[0:64, 5632 + oi * 64: 5632 + oi * 64 + 64]
            p.emit("vector", lambda e, psum=psum, stg=stg, h=h, i=i: e.tensor_scalar(
                       out=stg, in0=psum[:, 0:64], scalar1=dcol[0:64, :], scalar2=1.0,
                       op0=OP.mult, op1=OP.mult),
                   waits=[(dch, dch.n)] + ([(ost[oi], ost[oi].n)] if ost[oi].n > 0 else []),
                   incs=[(dch, 1)])
            p.emit("vector", lambda e, stg=stg, h=h, i=i: e.tensor_mul(
                       stg, stg, GCs[i][:, h * 64:(h + 1) * 64]),
                   waits=[(dch, dch.n)], incs=[prel])
            p.dma("gpsimd", attn[c * 64:(c + 1) * 64, h * 64:(h + 1) * 64], stg, ost[oi],
                  waits=[(prel[0], prel[0].n)])
            lastp[0] = (prel[0], prel[0].n)
        p.emit("vector", lambda e: e.tensor_copy(cx.SMALL[0:1, 730:731], cx.SMALL[0:1, 730:731]),
               waits=[lastp[0]], incs=[(avs, 1)])
    adone = [(ost[0], ost[0].n), (ost[1], ost[1].n)]
    return _emit_L1_part4(cx, pp, x_dram, xout, prev, env, env2, locals())


def _emit_L1_part4(cx, pp, x_dram, xout, prev, env, env2, env3):
    p = cx.p
    dram, wbf2, inp = cx.dram, cx.wbf2, cx.inp
    xn = env["xn"]
    attn, adone = env3["attn"], env3["adone"]
    GB, BB = env["GB"], env["BB"]
    md, mdone = env["md"], env["mdone"]
    al = p.sem("lf")
    ad = dram("l1_attnf", [D, T], BF16)
    at = emit_tm2fm(cx, pp, attn, ad, prev_done=adone)
    load_fm(cx, ad, D, T, cx.SBF, 0, al, waits=at)
    bob = emit_bias_bcast(cx, pp, inp["L1_attn_bo"], 0, 1024, BB)
    ix = dram("l1_ix", [T, D], F32)
    x2ld = [p.sem("x2_ld0"), p.sem("x2_ld1")]
    xsl = [cx.SF[:, 0:1024], cx.SF[:, 1024:2048]]
    x2ch = p.sem("chv")
    last_prel = [None, None]
    def ix_cb(tt, psum, stage, waits, prel):
        i = tt % 2
        slot = xsl[i]
        p.dma("sync", slot, x_dram[tt * 128:(tt + 1) * 128, :], x2ld[i],
              waits=([last_prel[i]] if last_prel[i] is not None else []) + [(x2ld[i], x2ld[i].n)])
        p.emit("vector", lambda e, slot=slot, tt=tt: e.tensor_scalar(
                   out=slot, in0=slot, scalar1=cx.SMALL[:, 0 + tt:0 + tt + 1],
                   scalar2=cx.SMALL[:, 16 + tt:16 + tt + 1],
                   op0=OP.subtract, op1=OP.mult),
               waits=[(x2ld[i], x2ld[i].n)], incs=[(x2ch, 1)])
        p.emit("vector", lambda e, psum=psum, stage=stage, slot=slot: e.tensor_add(
                   stage, psum, slot),
               waits=list(waits) + [(x2ch, x2ch.n)], incs=[(x2ch, 1)])
        p.emit("vector", lambda e, stage=stage: e.tensor_add(stage, stage, BB),
               waits=[(x2ch, x2ch.n)] + bob, incs=[prel])
        last_prel[i] = (prel[0], prel[0].n)
    ixd = projA(cx, pp, src_tile=lambda dc, tt: cx.SBF[:, dc * T + tt * 128: dc * T + tt * 128 + 128],
                K=D, Tn=T, wdram=wbf2("L1_attn_Wo", (1024, 1024)), wT=False, N=D,
                dst_dram=ix, src_ready=[(al, al.n)], evac_cb=ix_cb)
    n2b = emit_bias_bcast(cx, pp, inp["L1_attn_norm2_scale"], 0, 1024, GB)
    r2 = dram("l1_r2", [D, T], BF16)
    r2d = emit_rms(cx, pp, ix, r2, 96, 112, 0.0, g_bcast=GB, prev_done=ixd + n2b)
    r2l = p.sem("lf")
    load_fm(cx, r2, D, T, cx.SRC, 0, r2l, waits=r2d)
    r2r = [(r2l, r2l.n)]
    r2B = lambda dc: cx.SRC[:, dc * T:(dc + 1) * T]
    r2A = lambda dc, tt: cx.SRC[:, dc * T + tt * 128: dc * T + tt * 128 + 128]
    COEF = 416
    rdone = emit_router(cx, pp, src_tile=r2A, src_ready=r2r,
                        wdram=wbf2("L1_attn_router_W", (1024, 8)), wT=False, E=8,
                        bias_row_dram=inp["L1_attn_router_b"], mode="soft_renorm",
                        coeff_col0=COEF)
    vecs = []
    for e in range(8):
        vecs.append((inp["L1_attn_b1"], e * 4096, 4096, 1.0))
    colf, b1done, _ = vecs_to_cols(cx, pp, vecs, 688)
    acc = dram("l1_acc", [T, D], F32)
    acld = [p.sem("mo_ld0"), p.sem("mo_ld1")]
    msem = [p.sem("mo_st0"), p.sem("mo_st1")]
    m0 = [msem[0].n, msem[1].n]
    mst = [cx.SF[:, 4096:5120], cx.SF[:, 5120:6144]]
    b2sem = p.sem("gb2")
    for e in range(8):
        hd = dram(f"l1_he{e}", [4096, T], BF16)
        w1v = flat_view(cx.wb["L1_attn_W1"], e * 1024 * 4096, [[4096, 1024], [1, 4096]])
        projB(cx, pp, src_tile=r2B, K=D, Tn=T, wdram=w1v, wT=False, N=4096,
              dst_dram=hd, src_ready=r2r + (rdone if e == 0 else []) + b1done,
              bias_col=lambda oc, e=e: colf(e, oc), act=AF.Gelu)
        p.dma("gpsimd", cx.ROWS[0:1, 2176:3200],
              flat_view(inp["L1_attn_b2"], e * 1024, [[1, 1], [1, 1024]]), b2sem,
              waits=[(pp.mm, pp.mm.n)])
        b2t = b2sem.n
        for half in range(2):
            pas = e * 2 + half
            w2v = flat_view(cx.wb["L1_attn_W2"],
                            e * 4096 * 1024 + half * 2048 * 1024, [[1024, 2048], [1, 1024]])
            hstream = (hd, cx.SBF[:, 8192:10240], cx.SBF[:, 10240:12288], half * 2048)
            def moe_cb(tt, psum, stage, waits, prel, pas=pas, e=e):
                i = tt % 2
                nst = pas * 8 + tt // 2
                ms = mst[i]
                w = list(waits) + ([(msem[i], m0[i] + 16 * nst)] if nst > 0 else [])
                slot = cx.SF[:, i * 1024 + 2048: i * 1024 + 2048 + 1024]
                if pas == 0:
                    p.dma("sync", slot, ix[tt * 128:(tt + 1) * 128, :], acld[i],
                          waits=[(acld[i], acld[i].n)] +
                                ([(msem[i], m0[i] + 16 * (tt // 2))] if tt >= 2 else []))
                    p.emit("vector", lambda e2, psum=psum, ms=ms, tt=tt, slot=slot, e=e:
                               e2.scalar_tensor_tensor(
                                   out=ms, in0=psum,
                                   scalar=cx.SMALL[:, COEF + tt * 8 + e:COEF + tt * 8 + e + 1],
                                   in1=slot, op0=OP.mult, op1=OP.add),
                           waits=w + [(acld[i], acld[i].n)], incs=[prel])
                else:
                    p.dma("sync", slot, acc[tt * 128:(tt + 1) * 128, :], acld[i],
                          waits=[(msem[i], m0[i] + 16 * (pas * 8 + tt // 2)), (acld[i], acld[i].n)])
                    p.emit("vector", lambda e2, psum=psum, ms=ms, slot=slot, tt=tt, e=e:
                               e2.scalar_tensor_tensor(
                                   out=ms, in0=psum,
                                   scalar=cx.SMALL[:, COEF + tt * 8 + e:COEF + tt * 8 + e + 1],
                                   in1=slot, op0=OP.mult, op1=OP.add),
                           waits=w + [(acld[i], acld[i].n)], incs=[prel])
                p.dma("gpsimd", acc[tt * 128:(tt + 1) * 128, :], ms, msem[i],
                      waits=[(prel[0], prel[0].n)])
            projA(cx, pp, src_stream=hstream, K=2048, Tn=T, wdram=w2v, wT=False, N=D,
                  src_ready=[(b2sem, b2t)] + (ixd if pas == 0 else []), evac_cb=moe_cb,
                  store=False, extra_row=cx.ROWS[0:1, 2176:3200] if half == 0 else None)
    upd = emit_update(cx, pp, x_dram, acc, md, xout,
                      pre_done=[(msem[0], msem[0].n), (msem[1], msem[1].n)] + mdone)
    return upd


LAYER_EMITTERS[1] = emit_L1


# version 03 (final)
# speedup vs baseline: 1.0136x; 1.0136x over previous
import math
import numpy as np
from contextlib import ExitStack

import concourse.bass as bass
import concourse.mybir as mybir
from concourse.bass_utils import run_bass_kernel_spmd

F32 = mybir.dt.float32
BF16 = mybir.dt.bfloat16
AF = mybir.ActivationFunctionType
AX = mybir.AxisListType
OP = mybir.AluOpType

T = 2048
D = 1024
NT = T // 128
ND = D // 128
NCORES = 8


def _rel_bucket(delta):
    NUM_BUCKETS, MAX_DISTANCE = 32, 64
    rp = max(delta, 0)
    max_exact = NUM_BUCKETS // 2
    if rp < max_exact:
        return rp
    rp_f = max(rp, 1)
    large = max_exact + int(math.log(rp_f / max_exact) / math.log(MAX_DISTANCE / max_exact) * (NUM_BUCKETS - max_exact))
    return min(large, NUM_BUCKETS - 1)


def host_constants():
    c = {}
    c["c_ident"] = np.eye(128, dtype=np.float32)
    c["c_ones"] = np.ones((128, 128), dtype=np.float32)
    c["c_J"] = np.eye(128, dtype=np.float32)[::-1].copy()
    R = np.zeros((128, 128), dtype=np.float32)
    for h in range(2):
        b = h * 64
        R[b:b+32, b+32:b+64] = np.eye(32)
        R[b+32:b+64, b:b+32] = np.eye(32)
    c["c_R"] = R
    half = 32
    inv = 1.0 / (10000.0 ** (np.arange(half, dtype=np.float64) / half))
    ang = np.arange(T, dtype=np.float64)[:, None] * inv[None, :]
    sin, cos = np.sin(ang).T, np.cos(ang).T
    cos_ext = np.zeros((128, T), dtype=np.float32)
    sin_ext = np.zeros((128, T), dtype=np.float32)
    for h in range(2):
        b = h * 64
        cos_ext[b:b+32] = cos; cos_ext[b+32:b+64] = cos
        sin_ext[b:b+32] = -sin; sin_ext[b+32:b+64] = sin
    cos_rep = np.zeros((128, T), dtype=np.float32)
    for h in range(2):
        b = h * 64
        cos_rep[b:b+64] = np.vstack([cos, cos])
    c["c_ropecos"] = cos_rep
    c["c_ropesin"] = sin_ext
    tri = np.where(np.arange(128)[None, :] > np.arange(128)[:, None], -1e9, 0.0)
    c["c_tri"] = tri.astype(np.float32)
    c["c_triT"] = np.ascontiguousarray(c["c_tri"].T)
    G = np.zeros((32, 4096), dtype=np.float32)
    for d in range(4095):
        delta = d - 2047
        G[_rel_bucket(delta), d] = 8.0
    c["c_G"] = G
    c["c_Gr"] = np.ascontiguousarray(G[:, ::-1][:, 1:].copy())
    Gr = np.zeros((32, 4096), dtype=np.float32)
    Gr[:, 0:4095] = G[:, 0:4095][:, ::-1]
    c["c_Gr"] = Gr
    return c



class Sem:
    __slots__ = ("h", "n", "name")
    def __init__(self, h, name):
        self.h = h; self.n = 0; self.name = name


class P:
    ENGINES = ("sync", "scalar", "vector", "tensor", "gpsimd")

    def __init__(self, nc, es):
        self.nc = nc
        self.es = es
        self.items = []
        self._semid = 0
        self._reg = {}
        self._pools = {}

    SEM_ALIAS = {
        "l0_vq": "chv", "l0_qgf": "chv", "l0_pk": "chv", "l0_kve": "chv",
        "l0_nd": "chv", "l0_cch": "chv", "l0_sch": "chv", "l0_ech": "chv",
        "x2_ch": "chv", "lg_ch": "chv", "rm_ch": "chv", "l0_zv": "chv",
        "l0_sm": "chm", "lg_vs": "chm", "rm_vs": "chm", "ln_vs": "chm",
        "l3_ones": "vx_on", "l2_ones": "vx_on", "lg_on": "vx_on",
        "l3_b2": "gb2", "l0_b2": "gb2", "l0_r0": "gb2", "bR_ld": "gb2",
        "bb_ld": "gb2", "vc_ld": "gb2",
        "l0_cl": "lf", "l0_cw": "lf", "l0_pkl": "lf",
        "l0_ve0": "at_ve0", "l0_ve1": "at_ve1",
        "l0_as0": "vx_st0", "l0_as1": "vx_st1",
        "l0_a2s0": "vx_st0", "l0_a2s1": "vx_st1",
        "bR_st0": "vx_st0", "bR_st1": "vx_st1",
        "t2f_l0gpsimd": "gld0", "t2f_l1gpsimd": "gld1",
        "l0_sl0": "up_ld0", "l0_sl1": "up_ld1",
        "l0_xl0": "x2_ld0", "l0_xl1": "x2_ld1",
        "t2f_l0sync": "up_ld0", "t2f_l1sync": "up_ld1",
        "t2f_s0": "ln_st0", "t2f_s1": "ln_st1",
        "rm_x0": "ln_x0", "rm_x1": "ln_x1", "lg_x0": "ln_x0", "lg_x1": "ln_x1",
        "rm_us": "ln_us", "rm_uv": "ln_uv", "lg_us": "ln_us", "lg_uv": "ln_uv",
        "rm_tm": "ln_tm", "rm_tp": "ln_tp", "rm_cp": "ln_cp",
        "lg_tm": "ln_tm", "lg_tp": "ln_tp", "lg_cp": "ln_cp",
        "t2f_tp": "ln_tp", "t2f_cp": "ln_cp",
        "rm_st0": "ln_st0", "rm_st1": "ln_st1",
        "lg_st0": "ln_st0", "lg_st1": "ln_st1",
        "lg_ex0": "vx_st0", "lg_ex1": "vx_st1",
        "l0_g": "chv", "l0_gd": "gb2", "l0_zb": "chv", "l0_zbd": "gld0",
    }

    def sem(self, name=None):
        if name is None:
            import inspect
            fr = inspect.currentframe().f_back
            name = f"{fr.f_code.co_name}_{fr.f_lineno}"
        name = self.SEM_ALIAS.get(name, name)
        if name in self._reg:
            return self._reg[name]
        self._semid += 1
        s = Sem(self.es.enter_context(self.nc.semaphore(name)), name)
        self._reg[name] = s
        return s

    def pool(self, name, factory):
        if name not in self._pools:
            self._pools[name] = factory()
        return self._pools[name]

    def emit(self, engine, build, waits=(), incs=()):
        wcap = [(s.h, int(v)) for (s, v) in waits]
        icap = []
        for s, a in incs:
            s.n += a
            icap.append((s.h, a))
        def thunk(eng, _w=wcap, _i=icap, _b=build):
            for h, v in _w:
                eng.wait_ge(h, v)
            ins = _b(eng)
            for h, a in _i:
                if ins is None:
                    raise RuntimeError("inc on build that returned None")
                ins.then_inc(h, a)
        self.items.append((engine, thunk))

    def wait(self, engine, waits):
        wcap = [(s.h, int(v)) for (s, v) in waits]
        def thunk(eng, _w=wcap):
            for h, v in _w:
                eng.wait_ge(h, v)
        self.items.append((engine, thunk))

    def dma(self, engine, out, in_, sem, waits=(), **kw):
        self.emit(engine, lambda e, o=out, i=in_: e.dma_start(out=o, in_=i, **kw),
                  waits=waits, incs=[(sem, 16)])

    def finalize_block(self):
        with self.nc.Block() as block:
            for name in self.ENGINES:
                def body(eng, _nm=name):
                    for e, fn in self.items:
                        if e == _nm:
                            fn(eng)
                getattr(block, name)(body)



class Pool:
    def __init__(self, p, regions, name):
        self.p = p
        self.regions = regions
        self.free = p.sem(name)
        self.uses = 0

    def acquire(self):
        i = self.uses
        self.uses += 1
        r = self.regions[i % len(self.regions)]
        wv = i - len(self.regions) + 1
        return r, max(0, wv)


class SlotPool:
    def __init__(self, p, regions, name, inc=1):
        self.p = p
        self.regions = regions
        self.inc = inc
        self.sems = [p.sem(f"{name}{i}") for i in range(len(regions))]
        self.kuse = [0] * len(regions)
        self.uses = 0

    def acquire(self):
        n = len(self.regions)
        slot = self.uses % n
        self.uses += 1
        k = self.kuse[slot]
        self.kuse[slot] += 1
        waits = [(self.sems[slot], self.inc * k)] if k > 0 else []
        return self.regions[slot], waits, (self.sems[slot], self.inc)

    def all_done(self):
        return [(self.sems[i], self.inc * self.kuse[i]) for i in range(len(self.regions))
                if self.kuse[i] > 0]



class Ctx:
    pass


def make_inputs_dram(nc, input_specs):
    aps = {}
    for name, shape in input_specs:
        shp = list(shape) if len(shape) > 0 else [1]
        aps[name] = nc.dram_tensor(name, shp, F32, kind="ExternalInput").ap()
    return aps


def build_ctx(nc, es, p):
    cx = Ctx()
    cx.nc, cx.es, cx.p = nc, es, p
    sb = lambda nm, shp, dt: es.enter_context(nc.sbuf_tensor(nm, shp, dt))
    ps = lambda nm, shp, dt=F32: es.enter_context(nc.psum_tensor(nm, shp, dt))
    cx.SRC = sb("SRC", [128, 16384], BF16)
    cx.SRC2 = sb("SRC2", [128, 11264], BF16)
    cx.WRES = sb("WRES", [128, 22528], BF16)
    cx.OUTS = sb("OUTS", [128, 4096], F32)
    cx.SBF = sb("SBF", [128, 16384], BF16)
    cx.SF = sb("SF", [128, 8192], F32)
    cx.SMALL = sb("SMALL", [128, 4096], F32)
    cx.CONST = sb("CONST", [128, 512], F32)
    cx.CBF = sb("CBF", [128, 512], BF16)
    cx.ROWS = sb("ROWS", [128, 4096], BF16)
    cx.PA = ps("PA", [128, 2048])
    cx.PB = ps("PB", [128, 1536])
    cx.PT = ps("PT", [128, 1024], BF16)
    return cx



def precast_weights(p, nc, inp, names_shapes, wbf):
    sem = p.sem("precast")
    for name, shape in names_shapes:
        src = inp[name]
        dst = wbf[name]
        n_el = int(np.prod(shape)) if shape else 1
        flat_src = src
        CH = 1 << 21
        off = 0
        sflat = src.tensor.ap()
        dflat = dst.tensor.ap()
        while off < n_el:
            ln = min(CH, n_el - off)
            s = sflat.copy(); s.offset = off; s.ap = _vec([[1, ln]])
            d = dflat.copy(); d.offset = off; d.ap = _vec([[1, ln]])
            p.dma("gpsimd", d, s, sem)
            off += ln
    return sem


import bass_rust
def _vec(lst):
    return bass_rust.VecI64Pair(lst)


def flat_view(ap, offset, pattern):
    v = ap.tensor.ap().copy() if hasattr(ap, "tensor") else ap.copy()
    v.offset = int(offset)
    v.ap = _vec(pattern)
    return v


def load_fm(cx, dram, K, Tn, dst_sb, dst_col0, sem, waits=()):
    p = cx.p
    nk = K // 128
    src = flat_view(dram, 0, [[Tn, 128], [128 * Tn, nk], [1, Tn]])
    dst = dst_sb[:, dst_col0:dst_col0 + nk * Tn].rearrange("p (k t) -> p k t", k=nk)
    p.dma("sync", dst, src, sem, waits=list(waits) + [(sem, sem.n)])


def w_strip_load(cx, wdram, dc, n0, ncols, dst_ap, sem, waits=(), wT=False):
    p = cx.p
    if not wT:
        p.dma("sync", dst_ap, wdram[dc * 128:(dc + 1) * 128, n0:n0 + ncols], sem, waits=waits)
    else:
        p.dma("sync", dst_ap, wdram[n0:n0 + ncols, dc * 128:(dc + 1) * 128], sem,
              waits=waits, transpose=True)


class ProjPools:
    def __init__(self, cx):
        p = cx.p
        self.pb512 = SlotPool(p, [cx.PB[:, i * 512:(i + 1) * 512] for i in range(3)], "pb", 1)
        self.pa1024 = SlotPool(p, [cx.PA[:, 0:1024], cx.PA[:, 1024:2048]], "pa", 1)
        self.stage1024 = SlotPool(p, [cx.OUTS[:, i * 1024:(i + 1) * 1024] for i in range(4)],
                                  "stg", 16)
        self.wsem = p.sem("wsem")
        self.ss = [p.sem("ss0"), p.sem("ss1")]
        self.ssk = [0, 0]
        self.mm = p.sem("mm")
        self.ev = p.sem("ev")
        self.evs = p.sem("evs")

    def wguard(self):
        return self.pb512.all_done() + self.pa1024.all_done()


def projB(cx, pp, *, src_tile, K, Tn, wdram, wT, N, dst_dram=None, src_ready=(),
          bias_col=None, act=None, scale=1.0, evac_cb=None, store=True, sb_out=None,
          wres_off=0):
    p = cx.p
    nk = K // 128
    wguard = pp.wguard()
    NG = min(N, max(128, ((22528 - wres_off) // nk) // 128 * 128))
    n_groups = (N + NG - 1) // NG
    TC = 512
    ntc = (Tn + TC - 1) // TC
    done = []
    for g in range(n_groups):
        n0g = g * NG
        ng = min(NG, N - n0g)
        w0 = pp.wsem.n
        for dc in range(nk):
            wts = wguard if dc == 0 else ()
            w_strip_load(cx, wdram, dc, n0g, ng,
                         cx.WRES[:, wres_off + dc * ng: wres_off + dc * ng + ng],
                         pp.wsem, waits=wts, wT=wT)
        wtarget = pp.wsem.n
        wguard = None
        for oc in range(ng // 128):
            for tc in range(ntc):
                t0 = tc * TC
                tw = min(TC, Tn - t0)
                ps, pw, prel = pp.pb512.acquire()
                psum = ps[:, 0:tw]
                waits = [(pp.wsem, wtarget)] + pw + list(src_ready)
                for dc in range(nk):
                    def b(e, dc=dc, oc=oc, t0=t0, tw=tw, psum=psum, ng=ng):
                        return e.matmul(psum,
                                        cx.WRES[:, wres_off + dc * ng + oc * 128:
                                                wres_off + dc * ng + oc * 128 + 128],
                                        src_tile(dc)[:, t0:t0 + tw],
                                        start=(dc == 0), stop=(dc == nk - 1))
                    p.emit("tensor", b, waits=waits if dc == 0 else (),
                           incs=[(pp.mm, 1)] if dc == nk - 1 else ())
                mmv = pp.mm.n
                st, sw, srel = pp.stage1024.acquire()
                stage = st[:, 0:tw]
                ocg = (n0g + oc * 128) // 128
                ewaits = [(pp.mm, mmv)] + sw
                if evac_cb is not None:
                    evac_cb(ocg, t0, tw, psum, stage, ewaits, prel)
                else:
                    bc = bias_col(ocg) if bias_col is not None else 0.0
                    fn = act if act is not None else AF.Identity
                    p.emit("scalar",
                           lambda e, psum=psum, stage=stage, bc=bc, fn=fn:
                               e.activation(out=stage, in_=psum, func=fn, bias=bc, scale=scale),
                           waits=ewaits, incs=[prel])
                sdone = [(prel[0], prel[0].n)]
                if sb_out is not None:
                    sbuf, col0 = sb_out
                    dst = sbuf[:, col0 + ocg * Tn + t0: col0 + ocg * Tn + t0 + tw]
                    p.emit("vector", lambda e, d=dst, s=stage: e.tensor_copy(d, s),
                           waits=sdone, incs=[(pp.ev, 1)])
                    sdone = [(pp.ev, pp.ev.n)]
                if store and dst_dram is not None:
                    p.dma("gpsimd", dst_dram[n0g + oc * 128:n0g + oc * 128 + 128, t0:t0 + tw],
                          stage, srel[0], waits=sdone)
                else:
                    gi = cx.gctr[0] % 8192
                    cx.gctr[0] += 1
                    p.dma("gpsimd", cx.ghost[gi:gi + 1], cx.gsrc, srel[0], waits=sdone)
        wguard = pp.wguard()
    return pp.stage1024.all_done()


def projA(cx, pp, *, src_tile=None, src_stream=None, K, Tn=T, wdram, wT=False, N,
          dst_dram=None, src_ready=(), evac_cb=None, store=True, act=None,
          bias_bcast=None, extra_row=None, wres_off=0):
    p = cx.p
    nk = K // 128
    wguard = pp.wguard()
    for dc in range(nk):
        wts = wguard if dc == 0 else ()
        w_strip_load(cx, wdram, dc, 0, N,
                     cx.WRES[:, wres_off + dc * N: wres_off + dc * N + N],
                     pp.wsem, waits=wts, wT=wT)
    wtarget = pp.wsem.n
    nnch = (N + 511) // 512
    smm = []
    stdone0 = pp.stage1024.all_done()
    for tt in range(Tn // 128):
        if src_stream is not None:
            sdram, slotA, slotB, rowoff = src_stream
            si = tt % 2
            slot = (slotA, slotB)[si]
            sw = ([(pp.mm, smm[tt - 2])] if tt >= 2 else list(wguard) + list(src_ready)) + stdone0
            srcv = flat_view(sdram, rowoff * Tn + tt * 128, [[Tn, 128], [128 * Tn, nk], [1, 128]])
            p.dma("sync", slot.rearrange("p (k t) -> p k t", k=nk), srcv, pp.ss[si], waits=sw)
            pp.ssk[si] += 1
            tile = lambda dc, slot=slot: slot[:, dc * 128:(dc + 1) * 128]
            sready = [(pp.ss[si], 16 * pp.ssk[si])] + list(src_ready)
        else:
            tile = lambda dc, tt=tt: src_tile(dc, tt)
            sready = list(src_ready)
        ps, pw, prel = pp.pa1024.acquire()
        psum = ps[:, 0:N]
        waits = [(pp.wsem, wtarget)] + pw + sready
        first = True
        for nch in range(nnch):
            n0 = nch * 512
            nw = min(512, N - n0)
            lastch = (nch == nnch - 1)
            for dc in range(nk):
                def b(e, dc=dc, n0=n0, nw=nw, psum=psum, tile=tile):
                    return e.matmul(psum[:, n0:n0 + nw], tile(dc),
                                    cx.WRES[:, wres_off + dc * N + n0: wres_off + dc * N + n0 + nw],
                                    start=(dc == 0), stop=(dc == nk - 1 and extra_row is None))
                lastmm = lastch and (dc == nk - 1) and extra_row is None
                p.emit("tensor", b, waits=waits if first else (),
                       incs=[(pp.mm, 1)] if lastmm else ())
                first = False
            if extra_row is not None:
                p.emit("tensor", lambda e, n0=n0, nw=nw, psum=psum: e.matmul(
                           psum[:, n0:n0 + nw], cx.CBF[0:1, 384:512], extra_row[:, n0:n0 + nw],
                           start=False, stop=True),
                       incs=[(pp.mm, 1)] if lastch else ())
        mmv = pp.mm.n
        smm.append(mmv)
        st, sw2, srel = pp.stage1024.acquire()
        stage = st[:, 0:N]
        ewaits = [(pp.mm, mmv)] + sw2
        if evac_cb is not None:
            evac_cb(tt, psum, stage, ewaits, prel)
            sdone = [(prel[0], prel[0].n)]
        else:
            if bias_bcast is not None:
                p.emit("vector", lambda e, psum=psum, stage=stage: e.tensor_add(
                           stage, psum, bias_bcast[:, 0:N]),
                       waits=ewaits, incs=[prel])
                sdone = [(prel[0], prel[0].n)]
                if act is not None:
                    p.emit("scalar", lambda e, stage=stage, fn=act: e.activation(
                               out=stage, in_=stage, func=fn),
                           waits=sdone, incs=[(pp.evs, 1)])
                    sdone = [(pp.evs, pp.evs.n)]
            elif act is not None:
                p.emit("scalar", lambda e, psum=psum, stage=stage, fn=act:
                           e.activation(out=stage, in_=psum, func=fn),
                       waits=ewaits, incs=[prel])
                sdone = [(prel[0], prel[0].n)]
            else:
                p.emit("vector", lambda e, psum=psum, stage=stage: e.tensor_copy(stage, psum),
                       waits=ewaits, incs=[prel])
                sdone = [(prel[0], prel[0].n)]
        if store and dst_dram is not None:
            p.dma("gpsimd", dst_dram[tt * 128:(tt + 1) * 128, 0:N], stage, srel[0],
                  waits=sdone)
        else:
            gi = cx.gctr[0] % 8192
            cx.gctr[0] += 1
            p.dma("gpsimd", cx.ghost[gi:gi + 1], cx.gsrc, srel[0], waits=sdone)
    return pp.stage1024.all_done()


def vecs_to_cols(cx, pp, vecs, small_col0):
    p = cx.p
    nv = len(vecs)
    assert nv <= 120
    rows = cx.ROWS
    ld = p.sem("vc_ld")
    ld0 = ld.n
    for v, (dr, off, ln, sc) in enumerate(vecs):
        src = flat_view(dr, off, [[1, 1], [1, ln]])
        p.dma("gpsimd", rows[v:v + 1, 0:ln], src, ld, waits=[(cx.pp.mm, cx.pp.mm.n)] if v == 0 else ())
    offs = []
    off = 0
    for v, (_, _, ln, _) in enumerate(vecs):
        offs.append(off)
        off += (ln + 127) // 128
    nch_max = max((ln + 127) // 128 for _, _, ln, _ in vecs)
    tp = p.pool("at_mT", lambda: Pool(p, [cx.PT[:, 0:128]], "at_mT"))
    tpf0 = None
    for ch in range(nch_max):
        reg, wv = tp.acquire()
        p.emit("tensor", lambda e, ch=ch, reg=reg: e.transpose(
                   reg, rows[:, ch * 128:(ch + 1) * 128], ident(cx)),
               waits=[(ld, ld0 + 16 * nv), (tp.free, wv), (cx.consts_sem, 112)], incs=[(pp.mm, 1)])
        mmv = pp.mm.n
        movers = [v for v in range(nv) if ch < (vecs[v][2] + 127) // 128]
        for j, v in enumerate(movers):
            dst = cx.SMALL[:, small_col0 + offs[v] + ch: small_col0 + offs[v] + ch + 1]
            sc = vecs[v][3]
            p.emit("vector", lambda e, dst=dst, reg=reg, v=v, sc=sc: e.tensor_scalar_mul(
                       dst, reg[:, v:v + 1], sc),
                   waits=[(pp.mm, mmv)] if j == 0 else (),
                   incs=[(tp.free, 1)] if j == len(movers) - 1 else ())
    def col_fn(v, ch=0):
        cbase = small_col0 + offs[v] + ch
        return cx.SMALL[:, cbase:cbase + 1]
    return col_fn, [(tp.free, tp.uses)], off


def emit_bias_bcast(cx, pp, vec_dram, off, n, dst_ap):
    p = cx.p
    ld = p.sem("bb_ld")
    p.dma("gpsimd", cx.ROWS[0:1, 2176:2176 + n],
          flat_view(vec_dram, off, [[1, 1], [1, n]]), ld,
          waits=[(pp.mm, pp.mm.n)])
    ldt = ld.n
    done = []
    for c0 in range(0, n, 512):
        cw = min(512, n - c0)
        ps, pw, prel = pp.pb512.acquire()
        p.emit("tensor", lambda e, ps=ps, c0=c0, cw=cw: e.matmul(
                   ps[:, 0:cw], cx.CBF[0:1, 384:512],
                   cx.ROWS[0:1, 2176 + c0:2176 + c0 + cw], start=True, stop=True),
               waits=[(ld, ldt), (cx.consts_sem, 112)] + pw, incs=[(pp.mm, 1)])
        p.emit("vector", lambda e, ps=ps, c0=c0, cw=cw: e.tensor_copy(
                   dst_ap[:, c0:c0 + cw], ps[:, 0:cw]),
               waits=[(pp.mm, pp.mm.n)], incs=[prel])
        done = [(prel[0], prel[0].n)]
    return done



def emit_load_consts(cx, pp, cdram):
    p = cx.p
    sem = p.sem("consts")
    p.dma("gpsimd", cx.CBF[:, 0:128], cdram["c_ident"][:, :], sem)
    p.dma("gpsimd", cx.CBF[:, 128:256], cdram["c_J"][:, :], sem)
    p.dma("gpsimd", cx.CBF[:, 256:384], cdram["c_R"][:, :], sem)
    p.dma("gpsimd", cx.CBF[:, 384:512], cdram["c_ones"][:, :], sem)
    p.dma("gpsimd", cx.CONST[:, 0:128], cdram["c_tri"][:, :], sem)
    p.dma("gpsimd", cx.CONST[:, 128:256], cdram["c_triT"][:, :], sem)
    p.dma("gpsimd", cx.CONST[:, 256:272], cdram["c_ones"][:, 0:16], sem)
    return sem


def ident(cx):
    return cx.CBF[:, 0:128]



def emit_ln(cx, pp, x_dram, xn_dram, MC, RC, consts_sem, prev_done=()):
    p = cx.p
    SC, QC = MC + 32, MC + 48
    xs2 = [p.sem("ln_x0"), p.sem("ln_x1")]
    us0, uv0 = None, None
    used_s = p.sem("ln_us")
    used_v = p.sem("ln_uv")
    us0, uv0 = used_s.n, used_v.n
    slots = [cx.SF[:, 0:1024], cx.SF[:, 1024:2048]]
    trash = cx.SF[:, 2048:3072]
    for tt in range(NT):
        i = tt % 2
        slot = slots[i]
        waits = ([(used_s, us0 + tt - 1), (used_v, uv0 + tt - 1)] if tt >= 2 else []) + list(prev_done)
        p.dma("sync", slot, x_dram[tt * 128:(tt + 1) * 128, :], xs2[i], waits=waits)
        wv = [(xs2[i], xs2[i].n)]
        p.emit("scalar", lambda e, slot=slot, tt=tt: e.activation(
                   out=trash, in_=slot, func=AF.Square,
                   accum_out=cx.SMALL[:, QC + tt:QC + tt + 1]),
               waits=wv + [(used_s, us0 + tt)], incs=[(used_s, 1)])
        p.emit("vector", lambda e, slot=slot, tt=tt: e.reduce_sum(
                   cx.SMALL[:, SC + tt:SC + tt + 1], slot, axis=AX.X),
               waits=wv, incs=[(used_v, 1)])
    mean = cx.SMALL[:, MC:MC + 16]
    rstd = cx.SMALL[:, RC:RC + 16]
    sumc = cx.SMALL[:, SC:SC + 16]
    sqc = cx.SMALL[:, QC:QC + 16]
    vs = p.sem("ln_vs")
    v0 = vs.n
    p.emit("vector", lambda e: e.tensor_scalar_mul(mean, sumc, 1.0 / D),
           waits=[(used_s, us0 + 16), (used_v, uv0 + 16)], incs=[(vs, 1)])
    p.emit("vector", lambda e: e.tensor_scalar_mul(sqc, sqc, 1.0 / D),
           waits=[(vs, v0 + 1)], incs=[(vs, 1)])
    p.emit("vector", lambda e: e.tensor_mul(sumc, mean, mean),
           waits=[(vs, v0 + 2)], incs=[(vs, 1)])
    p.emit("vector", lambda e: e.tensor_sub(sqc, sqc, sumc),
           waits=[(vs, v0 + 3)], incs=[(vs, 1)])
    p.emit("vector", lambda e: e.tensor_scalar_add(sqc, sqc, 1e-5),
           waits=[(vs, v0 + 4)], incs=[(vs, 1)])
    p.emit("scalar", lambda e: e.activation(out=sumc, in_=sqc, func=AF.Sqrt),
           waits=[(vs, v0 + 5)], incs=[(vs, 1)])
    p.emit("vector", lambda e: e.reciprocal(rstd, sumc),
           waits=[(vs, v0 + 6)], incs=[(vs, 1)])
    tmslots = [cx.SBF[:, 0:1024], cx.SBF[:, 1024:2048]]
    fmstage = [cx.SBF[:, 2048:3072], cx.SBF[:, 3072:4096]]
    tmdone = p.sem("ln_tm")
    tpdone = p.sem("ln_tp")
    cpdone = p.sem("ln_cp")
    t0c, tp0, cp0 = tmdone.n, tpdone.n, cpdone.n
    st2 = [p.sem("ln_st0"), p.sem("ln_st1")]
    for tt in range(NT):
        i = tt % 2
        slot = slots[i]
        tm = tmslots[i]
        fs = fmstage[i]
        waits = [(used_s, us0 + 16), (used_v, uv0 + 16), (vs, v0 + 7)] if tt < 2 else [(tmdone, t0c + tt - 1)]
        p.dma("sync", slot, x_dram[tt * 128:(tt + 1) * 128, :], xs2[i], waits=waits)
        p.emit("vector", lambda e, slot=slot, tm=tm, tt=tt: e.tensor_scalar(
                   out=tm, in0=slot, scalar1=cx.SMALL[:, MC + tt:MC + tt + 1],
                   scalar2=cx.SMALL[:, RC + tt:RC + tt + 1],
                   op0=OP.subtract, op1=OP.mult),
               waits=[(xs2[i], xs2[i].n)] + ([(cpdone, cp0 + tt - 1)] if tt >= 2 else [(consts_sem, 112)]),
               incs=[(tmdone, 1)])
        for dc in range(ND):
            p.emit("tensor", lambda e, tm=tm, dc=dc: e.transpose(
                       cx.PT[:, dc * 128:(dc + 1) * 128], tm[:, dc * 128:(dc + 1) * 128], ident(cx)),
                   waits=([(tmdone, t0c + tt + 1)] if dc == 0 else []) + ([(cpdone, cp0 + tt)] if (dc == 0 and tt >= 1) else []),
                   incs=[(tpdone, 1)] if dc == ND - 1 else ())
        p.emit("vector", lambda e, fs=fs: e.tensor_copy(fs, cx.PT[:, :]),
               waits=[(tpdone, tp0 + tt + 1)] + ([(st2[i], st2[i].n)] if st2[i].n > 0 else []),
               incs=[(cpdone, 1)])
        dst = flat_view(xn_dram, tt * 128, [[T, 128], [128 * T, ND], [1, 128]])
        p.dma("gpsimd", dst, fs.rearrange("p (k t) -> p k t", k=ND), st2[i],
              waits=[(cpdone, cp0 + tt + 1)])
    return [(st2[0], st2[0].n), (st2[1], st2[1].n)]



def emit_update(cx, pp, x_dram, s_dram, m_dram, xout_dram, pre_done=(), m2_dram=None):
    p = cx.p
    ld2 = [p.sem("up_ld0"), p.sem("up_ld1")]
    st2 = [p.sem("up_st0"), p.sem("up_st1")]
    sx = [cx.SF[:, 0:1024], cx.SF[:, 3072:4096]]
    ss = [cx.SF[:, 1024:2048], cx.SF[:, 4096:5120]]
    sm = [cx.SF[:, 2048:3072], cx.SF[:, 5120:6144]]
    sm2 = [cx.SF[:, 6144:7168], cx.SF[:, 7168:8192]]
    evs_ = p.sem("up_evs")
    evv_ = p.sem("up_evv")
    evm_ = p.sem("up_evm")
    eva_ = p.sem("up_eva")
    e0, v0, em0, a0 = evs_.n, evv_.n, evm_.n, eva_.n
    for tt in range(NT):
        i = tt % 2
        waits = ([(evv_, v0 + tt - 1), (st2[i], st2[i].n)] if tt >= 2 else []) \
            + (list(pre_done) if tt == 0 else [])
        r0, r1 = tt * 128, (tt + 1) * 128
        p.dma("sync", sx[i], x_dram[r0:r1, :], ld2[i], waits=waits)
        p.dma("sync", ss[i], s_dram[r0:r1, :], ld2[i])
        p.dma("sync", sm[i], m_dram[r0:r1, :], ld2[i])
        if m2_dram is not None:
            p.dma("sync", sm2[i], m2_dram[r0:r1, :], ld2[i])
            p.emit("vector", lambda e, i=i: e.tensor_add(sm[i], sm[i], sm2[i]),
                   waits=[(ld2[i], ld2[i].n)], incs=[(eva_, 1)])
            swait = [(eva_, a0 + tt + 1)]
        else:
            swait = [(ld2[i], ld2[i].n)]
        p.emit("scalar", lambda e, i=i: e.activation(out=sm[i], in_=sm[i], func=AF.Silu),
               waits=swait, incs=[(evs_, 1)])
        p.emit("vector", lambda e, i=i: e.tensor_mul(ss[i], ss[i], sm[i]),
               waits=[(evs_, e0 + tt + 1)], incs=[(evm_, 1)])
        p.emit("vector", lambda e, i=i: e.tensor_add(ss[i], ss[i], sx[i]),
               waits=[(evm_, em0 + tt + 1)], incs=[(evv_, 1)])
        p.dma("gpsimd", xout_dram[r0:r1, :], ss[i], st2[i], waits=[(evv_, v0 + tt + 1)])
    return [(st2[0], st2[0].n), (st2[1], st2[1].n)]



LAYER_EMITTERS = {}

INPUT_SPECS = {
    "L0_attn_Wq": (1024, 1024), "L0_attn_bq": (1024,), "L0_attn_lnq_g": (1024,),
    "L0_attn_lnq_b": (1024,), "L0_attn_Wk": (1024, 1024), "L0_attn_bk": (1024,),
    "L0_attn_lnk_g": (1024,), "L0_attn_lnk_b": (1024,), "L0_attn_Wv": (1024, 1024),
    "L0_attn_bv": (1024,), "L0_attn_lnv_g": (1024,), "L0_attn_lnv_b": (1024,),
    "L0_attn_Wphi_q": (1024, 1024), "L0_attn_bphi_q": (1024,),
    "L0_attn_Wphi_k": (1024, 1024), "L0_attn_bphi_k": (1024,),
    "L0_attn_lna_g": (1024,), "L0_attn_lna_b": (1024,),
    "L0_attn_W1": (1024, 4096), "L0_attn_b1": (4096,), "L0_attn_W2": (4096, 1024),
    "L0_attn_b2": (1024,), "L0_attn_gamma": (), "L0_attn_conv_w": (1024, 64, 5),
    "L0_attn_conv_b": (1024,), "L0_ffn_gate_w": (16, 1024),
    "L0_ffn_expert_w": (16, 1024, 1024), "L0_ffn_expert_b": (16, 1024),
    "L1_attn_norm1_scale": (1024,), "L1_attn_norm2_scale": (1024,),
    "L1_attn_Wq": (1024, 1024), "L1_attn_bq": (1024,), "L1_attn_Wk": (1024, 1024),
    "L1_attn_bk": (1024,), "L1_attn_Wv": (1024, 1024), "L1_attn_bv": (1024,),
    "L1_attn_Wg": (1024, 1024), "L1_attn_bg": (1024,), "L1_attn_Wo": (1024, 1024),
    "L1_attn_bo": (1024,), "L1_attn_router_W": (1024, 8), "L1_attn_router_b": (8,),
    "L1_attn_W1": (8, 1024, 4096), "L1_attn_b1": (8, 4096),
    "L1_attn_W2": (8, 4096, 1024), "L1_attn_b2": (8, 1024),
    "L1_ffn_w_gate": (2816, 1024), "L1_ffn_w_up": (2816, 1024), "L1_ffn_w_down": (1024, 2816),
    "L2_attn_Wq": (1024, 1024), "L2_attn_bq": (1024,), "L2_attn_Wk": (1024, 1024),
    "L2_attn_bk": (1024,), "L2_attn_Wv": (1024, 1024), "L2_attn_bv": (1024,),
    "L2_attn_Wo": (1024, 1024), "L2_attn_bo": (1024,), "L2_attn_rel_bias": (32, 16),
    "L2_ffn_g1": (1024,), "L2_ffn_Wq": (1024, 1024), "L2_ffn_Wdkv": (1024, 512),
    "L2_ffn_Wuk": (512, 1024), "L2_ffn_Wuv": (512, 1024), "L2_ffn_Wo": (1024, 1024),
    "L2_ffn_g2": (1024,), "L2_ffn_gate_w": (1024, 8),
    "L2_ffn_w1": (8, 1024, 2048), "L2_ffn_w2": (8, 2048, 1024),
    "L3_attn_Wq": (1024, 1024), "L3_attn_bq": (1024,), "L3_attn_Wk": (1024, 1024),
    "L3_attn_bk": (1024,), "L3_attn_Wv": (1024, 1024), "L3_attn_bv": (1024,),
    "L3_attn_Wo": (1024, 1024), "L3_attn_bo": (1024,),
    "L3_ffn_Wr": (8, 1024), "L3_ffn_br": (8,), "L3_ffn_W1": (8, 4096, 1024),
    "L3_ffn_b1": (8, 4096), "L3_ffn_W2": (8, 1024, 4096), "L3_ffn_b2": (8, 1024),
}

CONST_SPECS = {
    "c_ident": (128, 128), "c_ones": (128, 128), "c_J": (128, 128), "c_R": (128, 128),
    "c_ropecos": (128, 2048), "c_ropesin": (128, 2048), "c_tri": (128, 128),
    "c_triT": (128, 128), "c_G": (32, 4096), "c_Gr": (32, 4096),
}

TRANSPOSED = {
    "L3_attn_Wq": ((1024, 1024), (1, 0)),
    "L3_attn_Wk": ((1024, 1024), (1, 0)),
    "L3_attn_Wv": ((1024, 1024), (1, 0)),
    "L3_attn_Wo": ((1024, 1024), (1, 0)),
    "L3_ffn_Wr": ((1024, 8), (1, 0)),
    "L3_ffn_W1": ((8, 1024, 4096), (0, 2, 1)),
    "L3_ffn_W2": ((8, 4096, 1024), (0, 2, 1)),
    "L0_ffn_expert_w": ((16, 1024, 1024), (0, 2, 1)),
    "L0_ffn_gate_w": ((1024, 16), (1, 0)),
    "L0_attn_conv_w": ((64, 5, 1024), (1, 2, 0)),
}


def eff_name(n):
    return ("t" + n) if n in TRANSPOSED else n


def eff_shape(n):
    return TRANSPOSED[n][0] if n in TRANSPOSED else INPUT_SPECS[n]


BF_WEIGHTS = [n for n, s in INPUT_SPECS.items()
              if len(s) >= 2 and int(np.prod(s)) >= 1024]


def build_program(layers, dbg_names=()):
    nc = bass.Bass("TRN2", target_bir_lowering=False, debug=False)
    inp = {}
    inp["x_in"] = nc.dram_tensor("x_in", [T, D], F32, kind="ExternalInput").ap()
    for name, shape in INPUT_SPECS.items():
        if any(name.startswith(f"L{i}_") for i in layers):
            en, es_ = eff_name(name), eff_shape(name)
            shp = list(es_) if es_ else [1]
            inp[name] = nc.dram_tensor(en, shp, F32, kind="ExternalInput").ap()
    cdram = {}
    for name, shape in CONST_SPECS.items():
        cdram[name] = nc.dram_tensor(name, list(shape), F32, kind="ExternalInput").ap()
    y = nc.dram_tensor("y", [T, D], F32, kind="ExternalOutput").ap()

    scratch = {}
    def dram(name, shape, dt=BF16):
        if name not in scratch:
            scratch[name] = nc.dram_tensor("scr_" + name, list(shape), dt, kind="Internal").ap()
        return scratch[name]

    es = ExitStack()
    with es:
        p = P(nc, es)
        cx = build_ctx(nc, es, p)
        cx.ghost = nc.dram_tensor("scr_ghost", [8192], F32, kind="Internal").ap()
        cx.gsrc = flat_view(cdram["c_ones"], 0, [[1, 1]])
        cx.gctr = [0]
        cx.dram = dram
        cx.inp = inp
        cx.cdram = cdram
        pp = ProjPools(cx)
        cx.pp = pp
        consts_sem = emit_load_consts(cx, pp, cdram)
        cx.consts_sem = consts_sem
        wb = {}
        pcsem = p.sem("precast")
        for name in BF_WEIGHTS:
            if name in inp:
                shape = eff_shape(name)
                n_el = int(np.prod(shape))
                wdst = dram("bf_" + name, [n_el], BF16)
                CH = 1 << 21
                off = 0
                while off < n_el:
                    ln = min(CH, n_el - off)
                    s = flat_view(inp[name], off, [[1, ln]])
                    d2 = flat_view(wdst, off, [[1, ln]])
                    p.dma("gpsimd", d2, s, pcsem)
                    off += ln
                wb[name] = wdst
        cx.wb = wb
        cx.pcsem = pcsem
        cx.pctarget = pcsem.n

        def wbf2(name, shape):
            return flat_view(wb[name], 0, [[shape[1], shape[0]], [1, shape[1]]])
        cx.wbf2 = wbf2

        xcur = dram("x0", [T, D], F32)
        xin_sem = p.sem()
        p.dma("sync", xcur[:, :], inp["x_in"][:, :], xin_sem)
        prev = [(xin_sem, 16)]
        for li, L in enumerate(layers):
            xnext = dram(f"x{li + 1}", [T, D], F32)
            is_last = (li == len(layers) - 1)
            fin = LAYER_EMITTERS[L](cx, pp, xcur, y if is_last else xnext, prev)
            prev = fin
            xcur = xnext
        dbgo = {}
        for nm in dbg_names:
            src = scratch[nm]
            shp = list(src.tensor.shape)
            dt = src.dtype
            o = nc.dram_tensor("dbg_" + nm, shp, dt, kind="ExternalOutput").ap()
            dsem = p.sem()
            p.dma("gpsimd", o[...] if False else o, src, dsem, waits=prev)
            prev = prev + [(dsem, 16)]
        p.wait("sync", prev)
        p.wait("gpsimd", prev)
        p.finalize_block()
    return nc


def shard_inputs(inputs):
    x = np.ascontiguousarray(inputs["L0_attn_x"], dtype=np.float32)
    return [np.ascontiguousarray(x[c]) for c in range(NCORES)]


def kernel(**inputs):
    groups = [[0, 1, 2, 3]]
    consts = host_constants()
    xs = shard_inputs(inputs)
    for g in groups:
        nc = build_program(g)
        wnames = [n for n in INPUT_SPECS if any(n.startswith(f"L{i}_") for i in g)]
        in_maps = []
        arrs = {}
        for n in wnames:
            a = np.asarray(inputs[n], dtype=np.float32)
            if n in TRANSPOSED:
                a = np.ascontiguousarray(a.transpose(TRANSPOSED[n][1]))
            else:
                a = np.ascontiguousarray(a)
            if a.ndim == 0:
                a = a.reshape(1)
            arrs[eff_name(n)] = a
        for c in range(NCORES):
            m = {"x_in": xs[c]}
            m.update(arrs)
            m.update(consts)
            in_maps.append(m)
        res = run_bass_kernel_spmd(nc, in_maps, core_ids=list(range(NCORES)))
        xs = [res.results[c]["y"] for c in range(NCORES)]
    return np.stack(xs, axis=0)


def emit_test_ln(cx, pp, x_dram, xout, prev):
    p = cx.p
    xn = cx.dram("xn_t", [D, T], BF16)
    st = emit_ln(cx, pp, x_dram, xn, 0, 16, cx.consts_sem, prev_done=prev)
    cp = p.sem()
    p.dma("gpsimd", xout[:, :], x_dram[:, :], cp, waits=st)
    return [(cp, 16)]


LAYER_EMITTERS["test_ln"] = emit_test_ln



def emit_router(cx, pp, *, src_tile, src_ready, wdram, wT, E, bias_row_dram, mode,
                coeff_col0, extra_waits=()):
    p = cx.p
    nk = D // 128
    woff = 21504
    wsem = pp.wsem
    wguard = pp.wguard()
    for dc in range(nk):
        w_strip_load(cx, wdram, dc, 0, E, cx.WRES[:, woff + dc * E: woff + (dc + 1) * E],
                     wsem, waits=wguard if dc == 0 else (), wT=wT)
    have_bias = bias_row_dram is not None
    brsem = p.sem("rt_br")
    if have_bias:
        p.dma("gpsimd", cx.ROWS[0:1, 3200:3200 + E],
              flat_view(bias_row_dram, 0, [[1, 1], [1, E]]), brsem)
    wtarget = wsem.n
    brtarget = brsem.n
    va = p.sem("rt_va")
    sa = p.sem("rt_sa")
    for tt in range(NT):
        ps, pw, prel = pp.pa1024.acquire()
        psum = ps[:, 0:E]
        waits = [(wsem, wtarget), (brsem, brtarget)] + pw + list(src_ready) + list(extra_waits)
        for dc in range(nk):
            p.emit("tensor", lambda e, dc=dc, psum=psum, tt=tt: e.matmul(
                       psum, src_tile(dc, tt),
                       cx.WRES[:, woff + dc * E: woff + (dc + 1) * E],
                       start=(dc == 0), stop=(dc == nk - 1 and not have_bias)),
                   waits=waits if dc == 0 else (),
                   incs=[(pp.mm, 1)] if (dc == nk - 1 and not have_bias) else ())
        if have_bias:
            p.emit("tensor", lambda e, psum=psum: e.matmul(
                       psum, cx.CBF[0:1, 384:512], cx.ROWS[0:1, 3200:3200 + E],
                       start=False, stop=True), incs=[(pp.mm, 1)])
        mmv = pp.mm.n
        lg = cx.SMALL[:, 3584:3584 + E]
        ex = cx.SMALL[:, 3648:3648 + E]
        m1 = cx.SMALL[:, 3712:3713]
        m2 = cx.SMALL[:, 3713:3714]
        sm = cx.SMALL[:, 3714:3715]
        co = cx.SMALL[:, coeff_col0 + tt * E: coeff_col0 + (tt + 1) * E]
        v0 = va.n
        p.emit("vector", lambda e, psum=psum, lg=lg: e.tensor_copy(lg, psum),
               waits=[(pp.mm, mmv)], incs=[prel])
        p.emit("vector", lambda e, lg=lg, m1=m1: e.reduce_max(m1, lg, axis=AX.X, negate=True),
               waits=[(prel[0], prel[0].n)], incs=[(va, 1)])
        p.emit("scalar", lambda e, lg=lg, ex=ex, m1=m1: e.activation(
                   out=ex, in_=lg, func=AF.Exp, bias=m1, accum_out=sm),
               waits=[(va, v0 + 1)], incs=[(sa, 1)])
        if mode in ("soft_raw", "soft_renorm"):
            p.emit("vector", lambda e, ex=ex, sm=sm: e.reciprocal(cx.SMALL[:, 3715:3716], sm),
                   waits=[(sa, sa.n)], incs=[(va, 1)])
            p.emit("vector", lambda e, ex=ex: e.tensor_scalar(
                       out=ex, in0=ex, scalar1=cx.SMALL[:, 3715:3716], scalar2=1.0,
                       op0=OP.mult, op1=OP.mult),
                   waits=[(va, v0 + 2)], incs=[(va, 1)])
        p.emit("vector", lambda e, ex=ex, m1=m1: e.reduce_max(m1, ex, axis=AX.X),
               waits=[(va, va.n), (sa, sa.n)], incs=[(va, 1)])
        p.emit("vector", lambda e, ex=ex, m1=m1: e.tensor_scalar(
                   out=cx.SMALL[:, 3716:3716 + E],
                   in0=ex, scalar1=m1, scalar2=-1e30, op0=OP.is_ge, op1=OP.mult),
               waits=[(va, va.n)], incs=[(va, 1)])
        p.emit("vector", lambda e, ex=ex: e.tensor_add(
                   cx.SMALL[:, 3716:3716 + E], cx.SMALL[:, 3716:3716 + E], ex),
               waits=[(va, va.n)], incs=[(va, 1)])
        p.emit("vector", lambda e: e.reduce_max(m2, cx.SMALL[:, 3716:3716 + E], axis=AX.X),
               waits=[(va, va.n)], incs=[(va, 1)])
        p.emit("vector", lambda e, ex=ex, co=co: e.tensor_scalar(
                   out=co, in0=ex, scalar1=m2, scalar2=1.0, op0=OP.is_ge, op1=OP.mult),
               waits=[(va, va.n)], incs=[(va, 1)])
        p.emit("vector", lambda e, ex=ex, co=co: e.tensor_mul(co, co, ex),
               waits=[(va, va.n)], incs=[(va, 1)])
        if mode in ("soft_renorm", "logit_soft"):
            p.emit("vector", lambda e, m1=m1, m2=m2: e.tensor_add(sm, m1, m2),
                   waits=[(va, va.n)], incs=[(va, 1)])
            p.emit("vector", lambda e, sm=sm: e.reciprocal(cx.SMALL[:, 3715:3716], sm),
                   waits=[(va, va.n)], incs=[(va, 1)])
            p.emit("vector", lambda e, co=co: e.tensor_scalar(
                       out=co, in0=co, scalar1=cx.SMALL[:, 3715:3716], scalar2=1.0,
                       op0=OP.mult, op1=OP.mult),
                   waits=[(va, va.n)], incs=[(va, 1)])
    return [(va, va.n)]



def emit_softmax_attn(cx, pp, *, q_dram, k_dram, vext_dram, out_dram, causal,
                      v8q_dram=None, v8t_dram=None, src_ready=()):
    p = cx.p
    bias = v8q_dram is not None
    QP, KP = cx.SBF[:, 0:2048], cx.SBF[:, 2048:4096]
    QE, KE = cx.SBF[:, 4096:6144], cx.SBF[:, 6144:8192]
    PS = [cx.SBF[:, 8192:10240], cx.SBF[:, 10240:12288]]
    VE = [cx.SBF[:, 12288:13328], cx.SBF[:, 13328:14368]]
    TQ = [cx.SBF[:, 14368:14880], cx.SBF[:, 14880:15392]]
    OUTB = [cx.ROWS[64:128, 0:512], cx.ROWS[64:128, 512:1024]]
    SFST = [cx.SF[0:64, 0:512], cx.SF[0:64, 512:1024]]
    MGB = cx.ROWS[:, 3584:3600]
    MRS = cx.ROWS[0:16, 3600:3728]
    DEN = cx.SMALL[0:1, 1024:3072]
    DENB = cx.ROWS[0:1, 0:2048]
    ONE1 = cx.ROWS[0:1, 2048:2176]
    GRID = cx.SMALL[:, 976:980]
    NEGM = cx.SMALL[:, 960:976]
    tri = cx.CONST[:, 0:128]
    triT = cx.CONST[:, 128:256]
    J = cx.CBF[:, 128:256]

    qk = p.sem("at_qk")
    ve2 = [p.sem("at_ve0"), p.sem("at_ve1")]
    tq2 = [p.sem("at_tq0"), p.sem("at_tq1")]
    st2 = [p.sem("at_st0"), p.sem("at_st1")]
    sk0 = [st2[0].n // 16, st2[1].n // 16]
    mq = p.sem("at_mq")
    va = p.sem("at_va")
    kslot = p.pool("at_kslot", lambda: Pool(p, [0, 1], "at_ks"))
    tqp = p.pool("at_tqp", lambda: Pool(p, [0, 1], "at_tqp"))
    tpool = p.pool("at_mT", lambda: Pool(p, [cx.PT[:, 0:128]], "at_mT"))

    p.emit("vector", lambda e: e.memset(KE[64:65, :], 1.0), incs=[(va, 1)])
    p.emit("vector", lambda e: e.memset(ONE1, 1.0), incs=[(va, 1)])

    out_slot = 0
    for h in range(16):
        hp, base = h // 2, (h % 2) * 64
        if h % 2 == 0:
            w = [(va, va.n)] + (list(src_ready) if h == 0 else [])
            p.dma("sync", QP, q_dram[hp * 128:(hp + 1) * 128, :], qk, waits=w)
            p.dma("sync", KP, k_dram[hp * 128:(hp + 1) * 128, :], qk)
        qktarget = qk.n
        for qt in range(NT):
            krange = (qt + 1) * 128 if causal else T
            nch = (krange + 511) // 512
            for kc in range(nch):
                k0 = kc * 512
                kw = min(512, krange - k0)
                ps, pw, prel = pp.pb512.acquire()
                psum = ps[:, 0:kw]
                p.emit("tensor", lambda e, psum=psum, qt=qt, k0=k0, kw=kw, base=base:
                           e.matmul(psum, QP[base:base + 64, qt * 128:(qt + 1) * 128],
                                    KP[base:base + 64, k0:k0 + kw],
                                    start=True, stop=not bias),
                       waits=[(qk, qktarget)] + pw,
                       incs=[(pp.mm, 1)] if not bias else ())
                if bias:
                    tsl, tw_, trel = None, None, None
                    sl, twv = tqp.acquire()
                    tq = TQ[sl][:, 0:kw]
                    b0 = 1920 - qt * 128 + k0
                    srcb = flat_view(v8q_dram, h * 4096 + b0, [[1, 128], [1, kw]])
                    p.dma("sync", tq, srcb, tq2[sl], waits=[(tqp.free, twv)])
                    p.emit("tensor", lambda e, psum=psum, tq=tq:
                               e.matmul(psum, J, tq, start=False, stop=True),
                           waits=[(tq2[sl], tq2[sl].n)], incs=[(pp.mm, 1)])
                mmv = pp.mm.n
                mwait = [(pp.mm, mmv)]
                if causal and k0 + kw == krange:
                    doff = kw - 128
                    p.emit("vector", lambda e, psum=psum, doff=doff:
                               e.tensor_add(psum[:, doff:doff + 128], psum[:, doff:doff + 128], tri),
                           waits=mwait, incs=[(va, 1)])
                    mwait = [(va, va.n)]
                p.emit("vector", lambda e, psum=psum, kc=kc:
                           e.reduce_max(GRID[:, kc:kc + 1], psum, axis=AX.X),
                       waits=mwait + ([(va, va.n)] if kc == 0 else []), incs=[prel])
                if bias:
                    p.emit("gpsimd", lambda e, tq=tq: e.tensor_copy(tq[0:1, 0:1], tq[0:1, 0:1]),
                           waits=[(pp.mm, mmv)], incs=[(tqp.free, 1)])
            p.emit("vector", lambda e, qt=qt, nch=nch: e.reduce_max(
                       NEGM[:, qt:qt + 1], GRID[:, 0:nch], axis=AX.X, negate=True),
                   waits=pp.pb512.all_done(), incs=[(va, 1)])
        p.emit("vector", lambda e: e.tensor_copy(MGB, NEGM), waits=[(va, va.n)], incs=[(va, 1)])
        reg, twv = tpool.acquire()
        p.emit("tensor", lambda e, reg=reg: e.transpose(reg[0:16, :], MGB, ident(cx)),
               waits=[(va, va.n), (tpool.free, twv)], incs=[(pp.mm, 1)])
        p.emit("vector", lambda e, reg=reg: e.tensor_copy(MRS, reg[0:16, :]),
               waits=[(pp.mm, pp.mm.n)], incs=[(tpool.free, 1)])
        p.dma("sync", QE[64:65, :], MRS, mq,
              waits=[(tpool.free, tpool.uses)] + pp.pb512.all_done())
        p.emit("vector", lambda e, base=base: e.tensor_copy(QE[0:64, :], QP[base:base + 64, :]),
               waits=pp.pb512.all_done() + [(qk, qktarget)], incs=[(va, 1)])
        p.emit("vector", lambda e, base=base: e.tensor_copy(KE[0:64, :], KP[base:base + 64, :]),
               incs=[(va, 1)])
        qetarget = va.n
        mqtarget = mq.n
        pa_ready = [(va, va.n)]
        for kt in range(NT):
            qlo = kt * 128 if causal else 0
            sl, kwv = kslot.acquire()
            Pslot, VEslot = PS[sl], VE[sl]
            p.dma("sync", VEslot, vext_dram[kt * 128:(kt + 1) * 128, :], ve2[sl],
                  waits=[(kslot.free, kwv)])
            vetarget = ve2[sl].n
            qcs = []
            for qc in range(qlo // 512, 4):
                q0 = max(qlo, qc * 512)
                qw = (qc + 1) * 512 - q0
                qcs.append((q0, qw, qc))
            first_exp = True
            for (q0, qw, qc) in qcs:
                ps, pw, prel = pp.pb512.acquire()
                psum = ps[:, 0:qw]
                p.emit("tensor", lambda e, psum=psum, kt=kt, q0=q0, qw=qw:
                           e.matmul(psum, KE[0:65, kt * 128:(kt + 1) * 128], QE[0:65, q0:q0 + qw],
                                    start=True, stop=not bias),
                       waits=[(va, qetarget), (mq, mqtarget)] + pw,
                       incs=[(pp.mm, 1)] if not bias else ())
                if bias:
                    sl2, twv2 = tqp.acquire()
                    tq = TQ[sl2][:, 0:qw]
                    b0 = 1920 + q0 - kt * 128
                    srcb = flat_view(v8t_dram, h * 4096 + b0, [[1, 128], [1, qw]])
                    p.dma("sync", tq, srcb, tq2[sl2], waits=[(tqp.free, twv2)])
                    p.emit("tensor", lambda e, psum=psum, tq=tq:
                               e.matmul(psum, J, tq, start=False, stop=True),
                           waits=[(tq2[sl2], tq2[sl2].n)], incs=[(pp.mm, 1)])
                mmv = pp.mm.n
                ewait = [(pp.mm, mmv)]
                if causal and q0 < kt * 128 + 128 and kt * 128 >= q0:
                    off = kt * 128 - q0
                    p.emit("vector", lambda e, psum=psum, off=off:
                               e.tensor_add(psum[:, off:off + 128], psum[:, off:off + 128], triT),
                           waits=ewait, incs=[(va, 1)])
                    ewait = [(va, va.n)]
                p.emit("scalar", lambda e, psum=psum, Pslot=Pslot, q0=q0, qw=qw:
                           e.activation(out=Pslot[:, q0:q0 + qw], in_=psum, func=AF.Exp),
                       waits=ewait + ([(kslot.free, kwv)] if first_exp else []),
                       incs=[prel])
                first_exp = False
                if bias:
                    p.emit("gpsimd", lambda e, tq=tq: e.tensor_copy(tq[0:1, 0:1], tq[0:1, 0:1]),
                           waits=[(pp.mm, mmv)], incs=[(tqp.free, 1)])
            exps_done = pp.pb512.all_done()
            for j, (q0, qw, qc) in enumerate(qcs):
                lastkt_for_qc = min(15, 4 * qc + 3) if causal else 15
                p.emit("tensor", lambda e, q0=q0, qw=qw, kt=kt, VEslot=VEslot, Pslot=Pslot, h=h,
                                  lk=lastkt_for_qc:
                           e.matmul(cx.PA[0:65, q0:q0 + qw], VEslot[:, h * 65:h * 65 + 65],
                                    Pslot[:, q0:q0 + qw],
                                    start=(kt == 0), stop=(kt == lk), skip_group_check=True),
                       waits=(exps_done + [(ve2[sl], vetarget)] + (pa_ready if kt == 0 else []))
                             if j == 0 else (),
                       incs=[(kslot.free, 1)] if j == len(qcs) - 1 else ())
        p.emit("vector", lambda e: e.tensor_copy(DEN, cx.PA[64:65, 0:2048]),
               waits=[(kslot.free, kslot.uses)], incs=[(va, 1)])
        p.emit("vector", lambda e: e.reciprocal(DEN, DEN), waits=[(va, va.n)], incs=[(va, 1)])
        p.emit("vector", lambda e: e.tensor_copy(DENB, DEN), waits=[(va, va.n)], incs=[(va, 1)])
        dentarget = va.n
        for qc in range(4):
            ps, pw, prel = pp.pb512.acquire()
            bc = ps[0:64, 0:512]
            p.emit("tensor", lambda e, bc=bc, qc=qc: e.matmul(
                       bc, ONE1[:, 0:64], DENB[:, qc * 512:(qc + 1) * 512],
                       start=True, stop=True),
                   waits=[(va, dentarget)] + pw, incs=[(pp.mm, 1)])
            mmv = pp.mm.n
            so = out_slot % 2
            out_slot += 1
            p.emit("vector", lambda e, so=so, qc=qc: e.tensor_copy(
                       SFST[so], cx.PA[0:64, qc * 512:(qc + 1) * 512]),
                   waits=([(st2[so], st2[so].n)] if st2[so].n > 0 else []) + [(pp.mm, mmv)],
                   incs=[(va, 1)])
            p.emit("vector", lambda e, so=so, bc=bc: e.tensor_mul(OUTB[so], SFST[so], bc),
                   waits=[(va, va.n)], incs=[prel])
            p.dma("gpsimd", out_dram[h * 64:(h + 1) * 64, qc * 512:(qc + 1) * 512], OUTB[so],
                  st2[so], waits=[(prel[0], prel[0].n)])
    return [(st2[0], st2[0].n), (st2[1], st2[1].n)]



def emit_L3(cx, pp, x_dram, xout, prev):
    p = cx.p
    dram, wbf2, inp = cx.dram, cx.wbf2, cx.inp
    xn = dram("l3_xn", [D, T], BF16)
    lnst = emit_ln(cx, pp, x_dram, xn, 0, 16, cx.consts_sem, prev_done=prev)
    xl = p.sem("lf")
    load_fm(cx, xn, D, T, cx.SRC, 0, xl, waits=lnst + [(cx.pcsem, cx.pctarget)])
    srcr = [(xl, xl.n)]
    srcB = lambda dc: cx.SRC[:, dc * T:(dc + 1) * T]
    srcA = lambda dc, tt: cx.SRC[:, dc * T + tt * 128: dc * T + tt * 128 + 128]
    vecs = [(inp["L3_attn_bq"], 0, 1024, 0.125), (inp["L3_attn_bk"], 0, 1024, 1.0)]
    for e in range(8):
        vecs.append((inp["L3_ffn_b1"], e * 4096, 4096, 1.0))
    colf, bdone, _ = vecs_to_cols(cx, pp, vecs, 64)
    qd = dram("l3_q", [D, T], BF16)
    kd = dram("l3_k", [D, T], BF16)
    d1 = projB(cx, pp, src_tile=srcB, K=D, Tn=T, wdram=wbf2("L3_attn_Wq", (1024, 1024)),
               wT=False, N=D, dst_dram=qd, src_ready=srcr + bdone,
               bias_col=lambda oc: colf(0, oc), scale=0.125)
    d2 = projB(cx, pp, src_tile=srcB, K=D, Tn=T, wdram=wbf2("L3_attn_Wk", (1024, 1024)),
               wT=False, N=D, dst_dram=kd, src_ready=srcr,
               bias_col=lambda oc: colf(1, oc))
    vext = dram("l3_vext", [T, 1040], BF16)
    bvb = emit_bias_bcast(cx, pp, inp["L3_attn_bv"], 0, 1024, cx.SF[:, 6144:7168])
    vsl = [cx.SF[:, 0:1040], cx.SF[:, 1040:2080]]
    onesem = p.sem("l3_ones")
    for i in range(2):
        ov = vsl[i].rearrange("p (a b) -> p a b", b=65)[:, :, 64:65]
        p.emit("vector", lambda e, ov=ov: e.memset(ov, 1.0), incs=[(onesem, 1)])
    onest = onesem.n
    vst2 = [p.sem("vx_st0"), p.sem("vx_st1")]
    def vext_cb(tt, psum, stage, waits, prel):
        i = tt % 2
        slot = vsl[i]
        sv = slot.rearrange("p (a b) -> p a b", b=65)[:, :, 0:64]
        p.emit("vector", lambda e, psum=psum, sv=sv: e.tensor_add(
                   sv, psum.rearrange("p (a b) -> p a b", b=64),
                   cx.SF[:, 6144:7168].rearrange("p (a b) -> p a b", b=64)),
               waits=list(waits) + bvb + [(onesem, onest)] +
                     ([(vst2[i], vst2[i].n)] if vst2[i].n > 0 else []),
               incs=[prel])
        p.dma("gpsimd", vext[tt * 128:(tt + 1) * 128, :], slot, vst2[i],
              waits=[(prel[0], prel[0].n)])
    d3 = projA(cx, pp, src_tile=srcA, K=D, Tn=T, wdram=wbf2("L3_attn_Wv", (1024, 1024)),
               wT=False, N=D, src_ready=srcr, evac_cb=vext_cb, store=False)
    attnd = dram("l3_attn", [D, T], BF16)
    adone = emit_softmax_attn(cx, pp, q_dram=qd, k_dram=kd, vext_dram=vext,
                              out_dram=attnd, causal=True,
                              src_ready=d1 + d2 + [(vst2[0], vst2[0].n), (vst2[1], vst2[1].n)])
    al = p.sem("lf")
    load_fm(cx, attnd, D, T, cx.SBF, 0, al, waits=adone)
    bob = emit_bias_bcast(cx, pp, inp["L3_attn_bo"], 0, 1024, cx.SF[:, 6144:7168])
    sd = dram("l3_s", [T, D], F32)
    sdone = projA(cx, pp,
                  src_tile=lambda dc, tt: cx.SBF[:, dc * T + tt * 128: dc * T + tt * 128 + 128],
                  K=D, Tn=T, wdram=wbf2("L3_attn_Wo", (1024, 1024)), wT=False, N=D,
                  dst_dram=sd, src_ready=[(al, al.n)] + bob, bias_bcast=cx.SF[:, 6144:7168])
    COEF = 416
    rdone = emit_router(cx, pp, src_tile=srcA, src_ready=srcr,
                        wdram=wbf2("L3_ffn_Wr", (1024, 8)), wT=False, E=8,
                        bias_row_dram=inp["L3_ffn_br"], mode="soft_raw",
                        coeff_col0=COEF, extra_waits=sdone)
    acc = dram("l3_acc", [T, D], F32)
    acld = [p.sem("mo_ld0"), p.sem("mo_ld1")]
    msem = [p.sem("mo_st0"), p.sem("mo_st1")]
    m0 = [msem[0].n, msem[1].n]
    a0 = [acld[0].n, acld[1].n]
    mst = [cx.SF[:, 4096:5120], cx.SF[:, 5120:6144]]
    b2sem = p.sem("l3_b2")
    for e in range(8):
        hd = dram(f"l3_h{e}", [4096, T], BF16)
        w1v = flat_view(cx.wb["L3_ffn_W1"], e * 4096 * 1024, [[4096, 1024], [1, 4096]])
        projB(cx, pp, src_tile=srcB, K=D, Tn=T, wdram=w1v, wT=False, N=4096,
              dst_dram=hd, src_ready=srcr + (rdone if e == 0 else []),
              bias_col=lambda oc, e=e: colf(2 + e, oc), act=AF.Gelu)
        p.dma("gpsimd", cx.ROWS[0:1, 2176:3200],
              flat_view(inp["L3_ffn_b2"], e * 1024, [[1, 1], [1, 1024]]), b2sem,
              waits=[(pp.mm, pp.mm.n)])
        b2t = b2sem.n
        for half in range(2):
            pas = e * 2 + half
            w2v = flat_view(cx.wb["L3_ffn_W2"],
                            e * 4096 * 1024 + half * 2048 * 1024, [[1024, 2048], [1, 1024]])
            hstream = (hd, cx.SBF[:, 8192:10240], cx.SBF[:, 10240:12288], half * 2048)
            def moe_cb(tt, psum, stage, waits, prel, pas=pas, e=e):
                i = tt % 2
                nst = pas * 8 + tt // 2
                ms = mst[i]
                w = list(waits) + ([(msem[i], m0[i] + 16 * nst)] if nst > 0 else [])
                if pas == 0:
                    p.emit("vector", lambda e2, psum=psum, ms=ms, tt=tt: e2.tensor_scalar(
                               out=ms, in0=psum,
                               scalar1=cx.SMALL[:, COEF + tt * 8 + e:COEF + tt * 8 + e + 1],
                               scalar2=1.0, op0=OP.mult, op1=OP.mult),
                           waits=w, incs=[prel])
                else:
                    slot = cx.SF[:, i * 1024: i * 1024 + 1024]
                    p.dma("sync", slot, acc[tt * 128:(tt + 1) * 128, :], acld[i],
                          waits=[(msem[i], m0[i] + 16 * (pas * 8 + tt // 2))])
                    p.emit("vector", lambda e2, psum=psum, ms=ms, slot=slot, tt=tt, e=e:
                               e2.scalar_tensor_tensor(
                                   out=ms, in0=psum,
                                   scalar=cx.SMALL[:, COEF + tt * 8 + e:COEF + tt * 8 + e + 1],
                                   in1=slot, op0=OP.mult, op1=OP.add),
                           waits=w + [(acld[i], acld[i].n)], incs=[prel])
                p.dma("gpsimd", acc[tt * 128:(tt + 1) * 128, :], ms, msem[i],
                      waits=[(prel[0], prel[0].n)])
            projA(cx, pp, src_stream=hstream, K=2048, Tn=T, wdram=w2v, wT=False, N=D,
                  src_ready=[(b2sem, b2t)], evac_cb=moe_cb, store=False,
                  extra_row=cx.ROWS[0:1, 2176:3200] if half == 0 else None)
    upd = emit_update(cx, pp, x_dram, sd, acc, xout,
                      pre_done=[(msem[0], msem[0].n), (msem[1], msem[1].n)] + sdone)
    return upd


LAYER_EMITTERS[3] = emit_L3



def emit_rms(cx, pp, x_dram, out_fm_dram, QC, RC, eps, g_bcast=None, pre_ln=None,
             prev_done=()):
    p = cx.p
    xs2 = [p.sem("rm_x0"), p.sem("rm_x1")]
    used_s = p.sem("rm_us")
    used_v = p.sem("rm_uv")
    us0, uv0 = used_s.n, used_v.n
    slots = [cx.SF[:, 0:1024], cx.SF[:, 1024:2048]]
    trash = cx.SF[:, 2048:3072]
    for tt in range(NT):
        i = tt % 2
        slot = slots[i]
        waits = (([(used_s, us0 + tt - 1)] + ([(used_v, uv0 + tt - 1)] if pre_ln else []))
                 if tt >= 2 else []) + list(prev_done)
        p.dma("sync", slot, x_dram[tt * 128:(tt + 1) * 128, :], xs2[i], waits=waits)
        wv = [(xs2[i], xs2[i].n)]
        if pre_ln is not None:
            MC0, RC0 = pre_ln
            p.emit("vector", lambda e, slot=slot, tt=tt: e.tensor_scalar(
                       out=slot, in0=slot, scalar1=cx.SMALL[:, MC0 + tt:MC0 + tt + 1],
                       scalar2=cx.SMALL[:, RC0 + tt:RC0 + tt + 1],
                       op0=OP.subtract, op1=OP.mult),
                   waits=wv, incs=[(used_v, 1)])
            wv = [(used_v, used_v.n)]
        p.emit("scalar", lambda e, slot=slot, tt=tt: e.activation(
                   out=trash, in_=slot, func=AF.Square,
                   accum_out=cx.SMALL[:, QC + tt:QC + tt + 1]),
               waits=wv + [(used_s, us0 + tt)], incs=[(used_s, 1)])
    sqc = cx.SMALL[:, QC:QC + 16]
    rstd = cx.SMALL[:, RC:RC + 16]
    vs = p.sem("rm_vs")
    v0 = vs.n
    p.emit("vector", lambda e: e.tensor_scalar(out=rstd, in0=sqc, scalar1=1.0 / D,
                                               scalar2=eps, op0=OP.mult, op1=OP.add),
           waits=[(used_s, us0 + 16)] + ([(used_v, uv0 + 16)] if pre_ln else []),
           incs=[(vs, 1)])
    p.emit("scalar", lambda e: e.activation(out=rstd, in_=rstd, func=AF.Sqrt),
           waits=[(vs, v0 + 1)], incs=[(vs, 1)])
    p.emit("vector", lambda e: e.reciprocal(rstd, rstd),
           waits=[(vs, v0 + 2)], incs=[(vs, 1)])
    tmslots = [cx.SBF[:, 0:1024], cx.SBF[:, 1024:2048]]
    fmstage = [cx.SBF[:, 2048:3072], cx.SBF[:, 3072:4096]]
    tmdone = p.sem("rm_tm")
    tpdone = p.sem("rm_tp")
    cpdone = p.sem("rm_cp")
    t0c, tp0, cp0 = tmdone.n, tpdone.n, cpdone.n
    st2 = [p.sem("rm_st0"), p.sem("rm_st1")]
    for tt in range(NT):
        i = tt % 2
        slot = slots[i]
        tm = tmslots[i]
        fs = fmstage[i]
        waits = [(used_s, us0 + 16), (vs, v0 + 3)] + ([(used_v, used_v.n)] if pre_ln else []) \
            if tt < 2 else [(tmdone, t0c + tt - 1)]
        p.dma("sync", slot, x_dram[tt * 128:(tt + 1) * 128, :], xs2[i], waits=waits)
        ch = p.sem("rm_ch")
        w0 = [(xs2[i], xs2[i].n)] + ([(cpdone, cp0 + tt - 1)] if tt >= 2 else [(cx.consts_sem, 112)])
        if pre_ln is not None:
            MC0, RC0 = pre_ln
            p.emit("vector", lambda e, slot=slot, tt=tt: e.tensor_scalar(
                       out=slot, in0=slot, scalar1=cx.SMALL[:, MC0 + tt:MC0 + tt + 1],
                       scalar2=cx.SMALL[:, RC0 + tt:RC0 + tt + 1],
                       op0=OP.subtract, op1=OP.mult),
                   waits=w0, incs=[(ch, 1)])
            w0 = [(ch, ch.n)]
        if g_bcast is not None:
            p.emit("vector", lambda e, slot=slot, tt=tt: e.tensor_scalar(
                       out=slot, in0=slot, scalar1=cx.SMALL[:, RC + tt:RC + tt + 1],
                       scalar2=1.0, op0=OP.mult, op1=OP.mult),
                   waits=w0, incs=[(ch, 1)])
            p.emit("vector", lambda e, slot=slot, tm=tm: e.tensor_mul(tm, slot, g_bcast),
                   waits=[(ch, ch.n)], incs=[(tmdone, 1)])
        else:
            p.emit("vector", lambda e, slot=slot, tm=tm, tt=tt: e.tensor_scalar(
                       out=tm, in0=slot, scalar1=cx.SMALL[:, RC + tt:RC + tt + 1],
                       scalar2=1.0, op0=OP.mult, op1=OP.mult),
                   waits=w0, incs=[(tmdone, 1)])
        for dc in range(ND):
            p.emit("tensor", lambda e, tm=tm, dc=dc: e.transpose(
                       cx.PT[:, dc * 128:(dc + 1) * 128], tm[:, dc * 128:(dc + 1) * 128], ident(cx)),
                   waits=([(tmdone, t0c + tt + 1)] if dc == 0 else []) + ([(cpdone, cp0 + tt)] if (dc == 0 and tt >= 1) else []),
                   incs=[(tpdone, 1)] if dc == ND - 1 else ())
        p.emit("vector", lambda e, fs=fs: e.tensor_copy(fs, cx.PT[:, :]),
               waits=[(tpdone, tp0 + tt + 1)] + ([(st2[i], st2[i].n)] if st2[i].n > 0 else []),
               incs=[(cpdone, 1)])
        dst = flat_view(out_fm_dram, tt * 128, [[T, 128], [128 * T, ND], [1, 128]])
        p.dma("gpsimd", dst, fs.rearrange("p (k t) -> p k t", k=ND), st2[i],
              waits=[(cpdone, cp0 + tt + 1)])
    return [(st2[0], st2[0].n), (st2[1], st2[1].n)]



def emit_l2_bias(cx, pp, relb_dram, v8_dram, v8r_dram, prev_done=()):
    p = cx.p
    ld = p.sem("bR_ld")
    p.dma("gpsimd", cx.SBF[0:32, 0:4096], cx.cdram["c_G"][:, :], ld, waits=list(prev_done))
    p.dma("gpsimd", cx.SBF[0:32, 4096:8192], cx.cdram["c_Gr"][:, :], ld)
    p.dma("gpsimd", cx.ROWS[0:32, 3200:3216], relb_dram[:, :], ld)
    ldt = ld.n
    tp = p.pool("at_mT", lambda: Pool(p, [cx.PT[:, 0:128]], "at_mT"))
    stg = [cx.ROWS[0:16, 3216:3344], cx.ROWS[0:16, 3344:3472]]
    stq = [p.sem("bR_st0"), p.sem("bR_st1")]
    bcol = cx.ROWS[:, 3472:3488]
    for m, (gcol0, dstd) in enumerate([(0, v8_dram), (4096, v8r_dram)]):
        for ch in range(32):
            ps, pw, prel = pp.pb512.acquire()
            psum = ps[:, 0:16]
            p.emit("tensor", lambda e, psum=psum, ch=ch, gcol0=gcol0: e.matmul(
                       psum, cx.SBF[0:32, gcol0 + ch * 128: gcol0 + (ch + 1) * 128],
                       cx.ROWS[0:32, 3200:3216], start=True, stop=True),
                   waits=[(ld, ldt)] + pw, incs=[(pp.mm, 1)])
            p.emit("vector", lambda e, psum=psum: e.tensor_copy(bcol, psum),
                   waits=[(pp.mm, pp.mm.n)], incs=[prel])
            reg, twv = tp.acquire()
            p.emit("tensor", lambda e, reg=reg: e.transpose(reg[0:16, :], bcol, ident(cx)),
                   waits=[(prel[0], prel[0].n), (tp.free, twv), (cx.consts_sem, 112)],
                   incs=[(pp.mm, 1)])
            i = ch % 2
            p.emit("vector", lambda e, reg=reg, i=i: e.tensor_copy(stg[i], reg[0:16, :]),
                   waits=[(pp.mm, pp.mm.n)] + ([(stq[i], stq[i].n)] if stq[i].n > 0 else []),
                   incs=[(tp.free, 1)])
            p.dma("gpsimd", dstd[:, ch * 128:(ch + 1) * 128], stg[i], stq[i],
                  waits=[(tp.free, tp.uses)])
    return [(stq[0], stq[0].n), (stq[1], stq[1].n), (pp.mm, pp.mm.n)]






def emit_L2(cx, pp, x_dram, xout, prev):
    p = cx.p
    dram, wbf2, inp = cx.dram, cx.wbf2, cx.inp
    xn = dram("l2_xn", [D, T], BF16)
    lnst = emit_ln(cx, pp, x_dram, xn, 0, 16, cx.consts_sem, prev_done=prev)
    v8 = dram("l2_v8", [16, 4096], BF16)
    v8r = dram("l2_v8r", [16, 4096], BF16)
    bdoneR = emit_l2_bias(cx, pp, inp["L2_attn_rel_bias"], v8, v8r, prev_done=lnst)
    xl = p.sem("lf")
    load_fm(cx, xn, D, T, cx.SRC, 0, xl, waits=lnst + [(cx.pcsem, cx.pctarget)] + bdoneR)
    srcr = [(xl, xl.n)]
    srcB = lambda dc: cx.SRC[:, dc * T:(dc + 1) * T]
    srcA = lambda dc, tt: cx.SRC[:, dc * T + tt * 128: dc * T + tt * 128 + 128]
    vecs = [(inp["L2_attn_bq"], 0, 1024, 0.125), (inp["L2_attn_bk"], 0, 1024, 1.0)]
    colf, bdone, _ = vecs_to_cols(cx, pp, vecs, 128)
    qd = dram("l2_q", [D, T], BF16)
    kd = dram("l2_k", [D, T], BF16)
    d1 = projB(cx, pp, src_tile=srcB, K=D, Tn=T, wdram=wbf2("L2_attn_Wq", (1024, 1024)),
               wT=False, N=D, dst_dram=qd, src_ready=srcr + bdone,
               bias_col=lambda oc: colf(0, oc), scale=0.125)
    d2 = projB(cx, pp, src_tile=srcB, K=D, Tn=T, wdram=wbf2("L2_attn_Wk", (1024, 1024)),
               wT=False, N=D, dst_dram=kd, src_ready=srcr,
               bias_col=lambda oc: colf(1, oc))
    vext = dram("l2_vext", [T, 1040], BF16)
    bvb = emit_bias_bcast(cx, pp, inp["L2_attn_bv"], 0, 1024, cx.SF[:, 6144:7168])
    vsl = [cx.SF[:, 0:1040], cx.SF[:, 1040:2080]]
    onesem = p.sem("l2_ones")
    for i in range(2):
        ov = vsl[i].rearrange("p (a b) -> p a b", b=65)[:, :, 64:65]
        p.emit("vector", lambda e, ov=ov: e.memset(ov, 1.0), incs=[(onesem, 1)])
    onest = onesem.n
    vst2 = [p.sem("vx_st0"), p.sem("vx_st1")]
    v0st = [vst2[0].n, vst2[1].n]
    def vext_cb(tt, psum, stage, waits, prel):
        i = tt % 2
        slot = vsl[i]
        sv = slot.rearrange("p (a b) -> p a b", b=65)[:, :, 0:64]
        p.emit("vector", lambda e, psum=psum, sv=sv: e.tensor_add(
                   sv, psum.rearrange("p (a b) -> p a b", b=64),
                   cx.SF[:, 6144:7168].rearrange("p (a b) -> p a b", b=64)),
               waits=list(waits) + bvb + [(onesem, onest)] +
                     ([(vst2[i], vst2[i].n)] if vst2[i].n > v0st[i] else []),
               incs=[prel])
        p.dma("gpsimd", vext[tt * 128:(tt + 1) * 128, :], slot, vst2[i],
              waits=[(prel[0], prel[0].n)])
    d3 = projA(cx, pp, src_tile=srcA, K=D, Tn=T, wdram=wbf2("L2_attn_Wv", (1024, 1024)),
               wT=False, N=D, src_ready=srcr, evac_cb=vext_cb, store=False)
    attnd = dram("l2_attn", [D, T], BF16)
    adone = emit_softmax_attn(cx, pp, q_dram=qd, k_dram=kd, vext_dram=vext,
                              out_dram=attnd, causal=False,
                              v8q_dram=v8r, v8t_dram=v8,
                              src_ready=d1 + d2 + [(vst2[0], vst2[0].n), (vst2[1], vst2[1].n)])
    al = p.sem("lf")
    load_fm(cx, attnd, D, T, cx.SBF, 0, al, waits=adone)
    bob = emit_bias_bcast(cx, pp, inp["L2_attn_bo"], 0, 1024, cx.SF[:, 6144:7168])
    sd = dram("l2_s", [T, D], F32)
    sdone = projA(cx, pp,
                  src_tile=lambda dc, tt: cx.SBF[:, dc * T + tt * 128: dc * T + tt * 128 + 128],
                  K=D, Tn=T, wdram=wbf2("L2_attn_Wo", (1024, 1024)), wT=False, N=D,
                  dst_dram=sd, src_ready=[(al, al.n)] + bob, bias_bcast=cx.SF[:, 6144:7168])
    g1b = emit_bias_bcast(cx, pp, inp["L2_ffn_g1"], 0, 1024, cx.SF[:, 7168:8192])
    h1 = dram("l2_h1", [D, T], BF16)
    r1done = emit_rms(cx, pp, x_dram, h1, 64, 80, 1e-6, g_bcast=cx.SF[:, 7168:8192],
                      pre_ln=(0, 16), prev_done=sdone + g1b)
    h1l = p.sem("lf")
    load_fm(cx, h1, D, T, cx.SRC, 0, h1l, waits=r1done)
    h1B = lambda dc: cx.SRC[:, dc * T:(dc + 1) * T]
    h1A = lambda dc, tt: cx.SRC[:, dc * T + tt * 128: dc * T + tt * 128 + 128]
    h1r = [(h1l, h1l.n)]
    mq = dram("l2_mq", [D, T], BF16)
    projB(cx, pp, src_tile=h1B, K=D, Tn=T, wdram=wbf2("L2_ffn_Wq", (1024, 1024)),
          wT=False, N=D, dst_dram=mq, src_ready=h1r, scale=0.125)
    cd = dram("l2_c", [512, T], BF16)
    cdone = projB(cx, pp, src_tile=h1B, K=D, Tn=T, wdram=wbf2("L2_ffn_Wdkv", (1024, 512)),
                  wT=False, N=512, dst_dram=cd, src_ready=h1r)
    cl = p.sem("lf")
    load_fm(cx, cd, 512, T, cx.SRC, 0, cl, waits=cdone)
    cB = lambda dc: cx.SRC[:, dc * T:(dc + 1) * T]
    cA = lambda dc, tt: cx.SRC[:, dc * T + tt * 128: dc * T + tt * 128 + 128]
    cr = [(cl, cl.n)]
    mk = dram("l2_mk", [D, T], BF16)
    d4 = projB(cx, pp, src_tile=cB, K=512, Tn=T, wdram=wbf2("L2_ffn_Wuk", (512, 1024)),
               wT=False, N=D, dst_dram=mk, src_ready=cr)
    mvext = dram("l2_mvext", [T, 1040], BF16)
    onesem2 = p.sem("l2_ones")
    for i in range(2):
        ov = vsl[i].rearrange("p (a b) -> p a b", b=65)[:, :, 64:65]
        p.emit("vector", lambda e, ov=ov: e.memset(ov, 1.0),
               waits=[(vst2[i], vst2[i].n)], incs=[(onesem2, 1)])
    onest2 = onesem2.n
    v1st = [vst2[0].n, vst2[1].n]
    def mvext_cb(tt, psum, stage, waits, prel):
        i = tt % 2
        slot = vsl[i]
        sv = slot.rearrange("p (a b) -> p a b", b=65)[:, :, 0:64]
        p.emit("vector", lambda e, psum=psum, sv=sv: e.tensor_copy(
                   sv, psum.rearrange("p (a b) -> p a b", b=64)),
               waits=list(waits) + [(onesem2, onest2)] +
                     ([(vst2[i], vst2[i].n)] if vst2[i].n > v1st[i] else []),
               incs=[prel])
        p.dma("gpsimd", mvext[tt * 128:(tt + 1) * 128, :], slot, vst2[i],
              waits=[(prel[0], prel[0].n)])
    d5 = projA(cx, pp, src_tile=cA, K=512, Tn=T, wdram=wbf2("L2_ffn_Wuv", (512, 1024)),
               wT=False, N=D, src_ready=cr, evac_cb=mvext_cb, store=False)
    mattn = dram("l2_mattn", [D, T], BF16)
    madone = emit_softmax_attn(cx, pp, q_dram=mq, k_dram=mk, vext_dram=mvext,
                               out_dram=mattn, causal=True,
                               src_ready=d4 + d5 + [(vst2[0], vst2[0].n), (vst2[1], vst2[1].n)])
    mal = p.sem("lf")
    load_fm(cx, mattn, D, T, cx.SBF, 0, mal, waits=madone)
    x2d = dram("l2_x2", [T, D], F32)
    x2ld = [p.sem("x2_ld0"), p.sem("x2_ld1")]
    xsl = [cx.SF[:, 0:1024], cx.SF[:, 1024:2048]]
    x2ch = p.sem("x2_ch")
    last_prel = [None, None]
    def x2_cb(tt, psum, stage, waits, prel):
        i = tt % 2
        slot = xsl[i]
        p.dma("sync", slot, x_dram[tt * 128:(tt + 1) * 128, :], x2ld[i],
              waits=[last_prel[i]] if last_prel[i] is not None else [])
        p.emit("vector", lambda e, slot=slot, tt=tt: e.tensor_scalar(
                   out=slot, in0=slot, scalar1=cx.SMALL[:, 0 + tt:0 + tt + 1],
                   scalar2=cx.SMALL[:, 16 + tt:16 + tt + 1],
                   op0=OP.subtract, op1=OP.mult),
               waits=[(x2ld[i], x2ld[i].n)], incs=[(x2ch, 1)])
        p.emit("vector", lambda e, psum=psum, stage=stage, slot=slot: e.tensor_add(
                   stage, psum, slot),
               waits=list(waits) + [(x2ch, x2ch.n)], incs=[prel])
        last_prel[i] = (prel[0], prel[0].n)
    x2done = projA(cx, pp,
                   src_tile=lambda dc, tt: cx.SBF[:, dc * T + tt * 128: dc * T + tt * 128 + 128],
                   K=D, Tn=T, wdram=wbf2("L2_ffn_Wo", (1024, 1024)), wT=False, N=D,
                   dst_dram=x2d, src_ready=[(mal, mal.n)], evac_cb=x2_cb)
    g2b = emit_bias_bcast(cx, pp, inp["L2_ffn_g2"], 0, 1024, cx.SF[:, 7168:8192])
    h2 = dram("l2_h2", [D, T], BF16)
    r2done = emit_rms(cx, pp, x2d, h2, 64, 96, 1e-6, g_bcast=cx.SF[:, 7168:8192],
                      prev_done=x2done + g2b)
    h2l = p.sem("lf")
    load_fm(cx, h2, D, T, cx.SRC, 0, h2l, waits=r2done)
    h2B = lambda dc: cx.SRC[:, dc * T:(dc + 1) * T]
    h2A = lambda dc, tt: cx.SRC[:, dc * T + tt * 128: dc * T + tt * 128 + 128]
    h2r = [(h2l, h2l.n)]
    COEF = 416
    rdone = emit_router(cx, pp, src_tile=h2A, src_ready=h2r,
                        wdram=wbf2("L2_ffn_gate_w", (1024, 8)), wT=False, E=8,
                        bias_row_dram=None, mode="logit_soft", coeff_col0=COEF)
    acc = dram("l2_acc", [T, D], F32)
    acld = [p.sem("mo_ld0"), p.sem("mo_ld1")]
    msem = [p.sem("mo_st0"), p.sem("mo_st1")]
    m0 = [msem[0].n, msem[1].n]
    mst = [cx.SF[:, 4096:5120], cx.SF[:, 5120:6144]]
    for e in range(8):
        hd = dram(f"l2_he{e}", [2048, T], BF16)
        w1v = flat_view(cx.wb["L2_ffn_w1"], e * 1024 * 2048, [[2048, 1024], [1, 2048]])
        projB(cx, pp, src_tile=h2B, K=D, Tn=T, wdram=w1v, wT=False, N=2048,
              dst_dram=hd, src_ready=h2r + (rdone if e == 0 else []), act=AF.Silu)
        w2v = flat_view(cx.wb["L2_ffn_w2"], e * 2048 * 1024, [[1024, 2048], [1, 1024]])
        hstream = (hd, cx.SBF[:, 8192:10240], cx.SBF[:, 10240:12288], 0)
        def moe_cb(tt, psum, stage, waits, prel, pas=e):
            i = tt % 2
            nst = pas * 8 + tt // 2
            ms = mst[i]
            w = list(waits) + ([(msem[i], m0[i] + 16 * nst)] if nst > 0 else [])
            if pas == 0:
                p.emit("vector", lambda e2, psum=psum, ms=ms, tt=tt, pas=pas: e2.tensor_scalar(
                           out=ms, in0=psum,
                           scalar1=cx.SMALL[:, COEF + tt * 8 + pas:COEF + tt * 8 + pas + 1],
                           scalar2=1.0, op0=OP.mult, op1=OP.mult),
                       waits=w, incs=[prel])
            else:
                slot = cx.SF[:, i * 1024: i * 1024 + 1024]
                p.dma("sync", slot, acc[tt * 128:(tt + 1) * 128, :], acld[i],
                      waits=[(msem[i], m0[i] + 16 * (pas * 8 + tt // 2))])
                p.emit("vector", lambda e2, psum=psum, ms=ms, slot=slot, tt=tt, pas=pas:
                           e2.scalar_tensor_tensor(
                               out=ms, in0=psum,
                               scalar=cx.SMALL[:, COEF + tt * 8 + pas:COEF + tt * 8 + pas + 1],
                               in1=slot, op0=OP.mult, op1=OP.add),
                       waits=w + [(acld[i], acld[i].n)], incs=[prel])
            p.dma("gpsimd", acc[tt * 128:(tt + 1) * 128, :], ms, msem[i],
                  waits=[(prel[0], prel[0].n)])
        projA(cx, pp, src_stream=hstream, K=2048, Tn=T, wdram=w2v, wT=False, N=D,
              src_ready=[], evac_cb=moe_cb, store=False)
    upd = emit_update(cx, pp, x_dram, sd, acc, xout,
                      pre_done=[(msem[0], msem[0].n), (msem[1], msem[1].n)] + sdone,
                      m2_dram=x2d)
    return upd


LAYER_EMITTERS[2] = emit_L2



def emit_lnG(cx, pp, x_dram, out_fm_dram, MC, RC, g_bcast, b_bcast, prev_done=(),
             tm_dram=None, ext_dram=None, skip_fm=False):
    p = cx.p
    xs2 = [p.sem("lg_x0"), p.sem("lg_x1")]
    used_s = p.sem("lg_us")
    used_v = p.sem("lg_uv")
    us0, uv0 = used_s.n, used_v.n
    slots = [cx.SF[:, 0:1024], cx.SF[:, 1024:2048]]
    trash = cx.SF[:, 2048:3072]
    for tt in range(NT):
        i = tt % 2
        slot = slots[i]
        waits = ([(used_s, us0 + tt - 1), (used_v, uv0 + tt - 1)] if tt >= 2 else []) + list(prev_done)
        p.dma("sync", slot, x_dram[tt * 128:(tt + 1) * 128, :], xs2[i], waits=waits)
        wv = [(xs2[i], xs2[i].n)]
        p.emit("scalar", lambda e, slot=slot, tt=tt: e.activation(
                   out=trash, in_=slot, func=AF.Square,
                   accum_out=cx.SMALL[:, RC + 32 + tt:RC + 32 + tt + 1]),
               waits=wv + [(used_s, us0 + tt)], incs=[(used_s, 1)])
        p.emit("vector", lambda e, slot=slot, tt=tt: e.reduce_sum(
                   cx.SMALL[:, RC + 16 + tt:RC + 16 + tt + 1], slot, axis=AX.X),
               waits=wv, incs=[(used_v, 1)])
    mean = cx.SMALL[:, MC:MC + 16]
    rstd = cx.SMALL[:, RC:RC + 16]
    sumc = cx.SMALL[:, RC + 16:RC + 32]
    sqc = cx.SMALL[:, RC + 32:RC + 48]
    vs = p.sem("lg_vs")
    v0 = vs.n
    p.emit("vector", lambda e: e.tensor_scalar_mul(mean, sumc, 1.0 / D),
           waits=[(used_s, us0 + 16), (used_v, uv0 + 16)], incs=[(vs, 1)])
    p.emit("vector", lambda e: e.tensor_scalar_mul(sqc, sqc, 1.0 / D),
           waits=[(vs, v0 + 1)], incs=[(vs, 1)])
    p.emit("vector", lambda e: e.tensor_mul(sumc, mean, mean),
           waits=[(vs, v0 + 2)], incs=[(vs, 1)])
    p.emit("vector", lambda e: e.tensor_sub(sqc, sqc, sumc),
           waits=[(vs, v0 + 3)], incs=[(vs, 1)])
    p.emit("vector", lambda e: e.tensor_scalar_add(sqc, sqc, 1e-5),
           waits=[(vs, v0 + 4)], incs=[(vs, 1)])
    p.emit("scalar", lambda e: e.activation(out=sqc, in_=sqc, func=AF.Sqrt),
           waits=[(vs, v0 + 5)], incs=[(vs, 1)])
    p.emit("vector", lambda e: e.reciprocal(rstd, sqc),
           waits=[(vs, v0 + 6)], incs=[(vs, 1)])
    tmslots = [cx.SBF[:, 0:1024], cx.SBF[:, 1024:2048]]
    fmstage = [cx.SBF[:, 2048:3072], cx.SBF[:, 3072:4096]]
    tmdone = p.sem("lg_tm")
    tpdone = p.sem("lg_tp")
    cpdone = p.sem("lg_cp")
    t0c, tp0, cp0 = tmdone.n, tpdone.n, cpdone.n
    st2 = [p.sem("lg_st0"), p.sem("lg_st1")]
    ch = p.sem("lg_ch")
    exsl = [cx.SF[:, 3072:4112], cx.SF[:, 4112:5152]]
    exst = [p.sem("lg_ex0"), p.sem("lg_ex1")]
    onesem = p.sem("lg_on")
    if ext_dram is not None:
        for i in range(2):
            ov = exsl[i].rearrange("p (a b) -> p a b", b=65)[:, :, 64:65]
            p.emit("vector", lambda e, ov=ov: e.memset(ov, 1.0),
                   waits=[(exst[i], exst[i].n)], incs=[(onesem, 1)])
    onest = onesem.n
    for tt in range(NT):
        i = tt % 2
        slot = slots[i]
        tm = tmslots[i]
        fs = fmstage[i]
        waits = [(used_s, us0 + 16), (used_v, uv0 + 16), (vs, v0 + 7)] if tt < 2 else [(tmdone, t0c + tt - 1)]
        p.dma("sync", slot, x_dram[tt * 128:(tt + 1) * 128, :], xs2[i], waits=waits)
        p.emit("vector", lambda e, slot=slot, tt=tt: e.tensor_scalar(
                   out=slot, in0=slot, scalar1=cx.SMALL[:, MC + tt:MC + tt + 1],
                   scalar2=cx.SMALL[:, RC + tt:RC + tt + 1],
                   op0=OP.subtract, op1=OP.mult),
               waits=[(xs2[i], xs2[i].n)] + ([(cpdone, cp0 + tt - 1)] if tt >= 2 else [(cx.consts_sem, 112)])
                     + ([(st2[i], st2[i].n)] if tm_dram is not None else []),
               incs=[(ch, 1)])
        p.emit("vector", lambda e, slot=slot: e.tensor_mul(slot, slot, g_bcast),
               waits=[(ch, ch.n)], incs=[(ch, 1)])
        p.emit("vector", lambda e, slot=slot, tm=tm: e.tensor_add(tm, slot, b_bcast),
               waits=[(ch, ch.n)], incs=[(tmdone, 1)])
        nuse = 0
        if tm_dram is not None:
            p.dma("gpsimd", tm_dram[tt * 128:(tt + 1) * 128, :], tm, st2[i],
                  waits=[(tmdone, tmdone.n)])
        if ext_dram is not None:
            ex = exsl[i]
            sv = ex.rearrange("p (a b) -> p a b", b=65)[:, :, 0:64]
            p.emit("vector", lambda e, sv=sv, tm=tm: e.tensor_copy(
                       sv, tm.rearrange("p (a b) -> p a b", b=64)),
                   waits=[(tmdone, tmdone.n), (onesem, onest)] +
                         ([(exst[i], exst[i].n)] if exst[i].n > 0 else []),
                   incs=[(ch, 1)])
            p.dma("gpsimd", ext_dram[tt * 128:(tt + 1) * 128, :], ex, exst[i],
                  waits=[(ch, ch.n)])
        if not skip_fm:
            for dc in range(ND):
                p.emit("tensor", lambda e, tm=tm, dc=dc: e.transpose(
                           cx.PT[:, dc * 128:(dc + 1) * 128], tm[:, dc * 128:(dc + 1) * 128], ident(cx)),
                       waits=([(tmdone, t0c + tt + 1)] if dc == 0 else []) + ([(cpdone, cp0 + tt)] if (dc == 0 and tt >= 1) else []),
                       incs=[(tpdone, 1)] if dc == ND - 1 else ())
            p.emit("vector", lambda e, fs=fs: e.tensor_copy(fs, cx.PT[:, :]),
                   waits=[(tpdone, tp0 + tt + 1)] + ([(st2[i], st2[i].n)] if st2[i].n > 0 else []),
                   incs=[(cpdone, 1)])
            dst = flat_view(out_fm_dram, tt * 128, [[T, 128], [128 * T, ND], [1, 128]])
            p.dma("gpsimd", dst, fs.rearrange("p (k t) -> p k t", k=ND), st2[i],
                  waits=[(cpdone, cp0 + tt + 1)])
        else:
            p.emit("vector", lambda e, tm=tm: e.tensor_copy(trash[:, 0:1], tm[:, 0:1]),
                   waits=[(tmdone, t0c + tt + 1), (ch, ch.n)], incs=[(cpdone, 1)])
    done = [(st2[0], st2[0].n), (st2[1], st2[1].n), (cpdone, cpdone.n)]
    if ext_dram is not None:
        done += [(exst[0], exst[0].n), (exst[1], exst[1].n)]
    return done


def emit_tm2fm(cx, pp, src_dram, dst_fm_dram, col_off=0, row_stride=T, prev_done=(),
               cast=False):
    p = cx.p
    eng = "gpsimd" if cast else "sync"
    ld2 = [p.sem("t2f_l0" + eng), p.sem("t2f_l1" + eng)]
    st2 = [p.sem("t2f_s0"), p.sem("t2f_s1")]
    tpdone = p.sem("t2f_tp")
    cpdone = p.sem("t2f_cp")
    tp0, cp0 = tpdone.n, cpdone.n
    tmslots = [cx.SBF[:, 0:1024], cx.SBF[:, 1024:2048]]
    fmstage = [cx.SBF[:, 2048:3072], cx.SBF[:, 3072:4096]]
    for tt in range(NT):
        i = tt % 2
        tm = tmslots[i]
        fs = fmstage[i]
        waits = ([(cpdone, cp0 + tt - 1)] if tt >= 2 else list(prev_done)) \
            + [(cx.consts_sem, 112)]
        p.dma(eng, tm, src_dram[tt * 128:(tt + 1) * 128, :], ld2[i], waits=waits)
        for dc in range(ND):
            p.emit("tensor", lambda e, tm=tm, dc=dc: e.transpose(
                       cx.PT[:, dc * 128:(dc + 1) * 128], tm[:, dc * 128:(dc + 1) * 128], ident(cx)),
                   waits=([(ld2[i], ld2[i].n)] if dc == 0 else []) + ([(cpdone, cp0 + tt)] if (dc == 0 and tt >= 1) else []),
                   incs=[(tpdone, 1)] if dc == ND - 1 else ())
        p.emit("vector", lambda e, fs=fs: e.tensor_copy(fs, cx.PT[:, :]),
               waits=[(tpdone, tp0 + tt + 1)] + ([(st2[i], st2[i].n)] if st2[i].n > 0 else []),
               incs=[(cpdone, 1)])
        dst = flat_view(dst_fm_dram, col_off + tt * 128,
                        [[row_stride, 128], [128 * row_stride, ND], [1, 128]])
        p.dma("gpsimd", dst, fs.rearrange("p (k t) -> p k t", k=ND), st2[i],
              waits=[(cpdone, cp0 + tt + 1)])
    return [(st2[0], st2[0].n), (st2[1], st2[1].n)]



def emit_L0(cx, pp, x_dram, xout, prev):
    p = cx.p
    dram, wbf2, inp = cx.dram, cx.wbf2, cx.inp
    xn = dram("l0_xn", [D, T], BF16)
    lnst = emit_ln(cx, pp, x_dram, xn, 0, 16, cx.consts_sem, prev_done=prev)
    xl = p.sem("lf")
    load_fm(cx, xn, D, T, cx.SRC, 0, xl, waits=lnst + [(cx.pcsem, cx.pctarget)])
    srcr = [(xl, xl.n)]
    srcB = lambda dc: cx.SRC[:, dc * T:(dc + 1) * T]
    srcA = lambda dc, tt: cx.SRC[:, dc * T + tt * 128: dc * T + tt * 128 + 128]
    GB, BB = cx.SF[:, 6144:7168], cx.SF[:, 7168:8192]
    qraw = dram("l0_qraw", [T, D], F32)
    kraw = dram("l0_kraw", [T, D], F32)
    vraw = dram("l0_vraw", [T, D], F32)
    qfm = dram("l0_qfm", [D, T], BF16)
    kfm = dram("l0_kfm", [D, T], BF16)
    vext = dram("l0_vext", [T, 1040], BF16)
    for (W, bvec, raw) in [("L0_attn_Wq", "L0_attn_bq", qraw),
                           ("L0_attn_Wk", "L0_attn_bk", kraw),
                           ("L0_attn_Wv", "L0_attn_bv", vraw)]:
        bb = emit_bias_bcast(cx, pp, inp[bvec], 0, 1024, GB)
        projA(cx, pp, src_tile=srcA, K=D, Tn=T, wdram=wbf2(W, (1024, 1024)),
              wT=False, N=D, dst_dram=raw, src_ready=srcr + bb, bias_bcast=GB)
    rawdone = pp.stage1024.all_done()
    vdone = None
    for (raw, gv, bv, outfm, extd, mc) in [
            (qraw, "L0_attn_lnq_g", "L0_attn_lnq_b", qfm, None, 2752),
            (kraw, "L0_attn_lnk_g", "L0_attn_lnk_b", kfm, None, 2832),
            (vraw, "L0_attn_lnv_g", "L0_attn_lnv_b", None, vext, 2912)]:
        gb = emit_bias_bcast(cx, pp, inp[gv], 0, 1024, GB)
        bb = emit_bias_bcast(cx, pp, inp[bv], 0, 1024, BB)
        vdone = emit_lnG(cx, pp, raw, outfm, mc, mc + 16, GB, BB, prev_done=gb + bb + rawdone,
                         ext_dram=extd, skip_fm=(outfm is None))
    ql = p.sem("lf")
    load_fm(cx, qfm, D, T, cx.SRC, 0, ql, waits=[(p.sem("lg_st0"), p.sem("lg_st0").n),
                                                 (p.sem("lg_st1"), p.sem("lg_st1").n)])
    qr = [(ql, ql.n)]
    vq = p.sem("l0_vq")
    for dc in range(ND):
        p.emit("vector", lambda e, dc=dc: e.reduce_sum(
                   cx.SMALL[:, 144 + dc:144 + dc + 1], cx.SRC[:, dc * T:(dc + 1) * T], axis=AX.X),
               waits=qr, incs=[(vq, 1)])
    p.emit("vector", lambda e: e.tensor_scalar_mul(cx.SMALL[:, 144:152],
                                                   cx.SMALL[:, 144:152], 1.0 / (T * 8.0)),
           waits=[(vq, vq.n)], incs=[(vq, 1)])
    qgdone = vq.n
    vecsP = [(inp["L0_attn_bphi_q"], 0, 1024, 1.0)]
    colf, bpdone, _ = vecs_to_cols(cx, pp, vecsP, 720)
    phiq = dram("l0_phiq", [D, T], BF16)
    def plus1_cb(ocg, t0, tw, psum, stage, waits, prel):
        p.emit("scalar", lambda e, psum=psum, stage=stage, ocg=ocg: e.activation(
                   out=stage, in_=psum, func=AF.Tanh, bias=colf(0, ocg)),
               waits=waits, incs=[(pp.evs, 1)])
        p.emit("vector", lambda e, stage=stage: e.tensor_scalar_add(stage, stage, 1.0),
               waits=[(pp.evs, pp.evs.n)], incs=[prel])
    d1 = projB(cx, pp, src_tile=srcB, K=D, Tn=T, wdram=wbf2("L0_attn_Wphi_q", (1024, 1024)),
               wT=False, N=D, dst_dram=phiq, src_ready=qr + bpdone, evac_cb=plus1_cb)
    kl = p.sem("lf")
    load_fm(cx, kfm, D, T, cx.SRC, 0, kl, waits=d1)
    kr = [(kl, kl.n)]
    vqg = p.sem("l0_qgf")
    for dc in range(ND):
        reg = cx.ROWS[:, 2048 + dc * 16: 2048 + (dc + 1) * 16]
        p.emit("vector", lambda e, reg=reg: e.memset(reg, 0.0),
               waits=[(vq, qgdone)], incs=[(vqg, 1)])
        p.emit("vector", lambda e, reg=reg, dc=dc: e.tensor_copy(
                   reg[0:64, 2 * dc:2 * dc + 1], cx.SMALL[0:64, 144 + dc:144 + dc + 1]),
               waits=[(vqg, vqg.n)], incs=[(vqg, 1)])
        p.emit("vector", lambda e, reg=reg, dc=dc: e.tensor_copy(
                   reg[64:128, 2 * dc + 1:2 * dc + 2], cx.SMALL[64:128, 144 + dc:144 + dc + 1]),
               waits=[(vqg, vqg.n)], incs=[(vqg, 1)])
    qgf = vqg.n
    sc = p.sem("l0_sc")
    for dc in range(ND):
        for ncc in range(4):
            p.emit("tensor", lambda e, dc=dc, ncc=ncc: e.matmul(
                       cx.PA[0:16, ncc * 512:(ncc + 1) * 512],
                       cx.ROWS[:, 2048 + dc * 16: 2048 + (dc + 1) * 16],
                       cx.SRC[:, dc * T + ncc * 512: dc * T + ncc * 512 + 512],
                       start=(dc == 0), stop=(dc == ND - 1)),
                   waits=([(vqg, qgf)] + kr + pp.pa1024.all_done() + pp.pb512.all_done())
                         if (dc == 0 and ncc == 0) else (),
                   incs=[(pp.mm, 1)] if (dc == ND - 1 and ncc == 3) else ())
    vs = p.sem("l0_sm")
    ALC = 160
    m1 = cx.SMALL[:, 672:673]
    nm = cx.SMALL[:, 673:674]
    den = cx.SMALL[:, 674:675]
    enm = cx.SMALL[:, 675:676]
    A0 = cx.SMALL[:, 676:677]
    exps = cx.SBF[0:16, 4096:6144]
    p.emit("vector", lambda e: e.reduce_max(m1[0:16, :], cx.PA[0:16, 0:2048], axis=AX.X),
           waits=[(pp.mm, pp.mm.n)], incs=[(vs, 1)])
    p.emit("vector", lambda e: e.tensor_scalar_max(m1[0:16, :], m1[0:16, :], 0.0),
           waits=[(vs, vs.n)], incs=[(vs, 1)])
    p.emit("vector", lambda e: e.tensor_scalar_mul(nm[0:16, :], m1[0:16, :], -1.0),
           waits=[(vs, vs.n)], incs=[(vs, 1)])
    vsm = vs.n
    p.emit("scalar", lambda e: e.activation(out=exps, in_=cx.PA[0:16, 0:2048], func=AF.Exp,
                                            bias=nm[0:16, :], accum_out=den[0:16, :]),
           waits=[(vs, vsm)], incs=[(vs, 1)])
    p.emit("scalar", lambda e: e.activation(out=enm[0:16, :], in_=nm[0:16, :], func=AF.Exp),
           waits=[(vs, vs.n)], incs=[(vs, 1)])
    p.emit("vector", lambda e: e.tensor_add(den[0:16, :], den[0:16, :], enm[0:16, :]),
           waits=[(vs, vs.n)], incs=[(vs, 1)])
    p.emit("vector", lambda e: e.reciprocal(den[0:16, :], den[0:16, :]),
           waits=[(vs, vs.n)], incs=[(vs, 1)])
    p.emit("vector", lambda e: e.tensor_scalar(out=den[0:16, :], in0=den[0:16, :],
                                               scalar1=float(T), scalar2=1.0,
                                               op0=OP.mult, op1=OP.mult),
           waits=[(vs, vs.n)], incs=[(vs, 1)])
    p.emit("vector", lambda e: e.tensor_scalar(out=exps, in0=exps, scalar1=den[0:16, :],
                                               scalar2=1.0, op0=OP.mult, op1=OP.mult),
           waits=[(vs, vs.n)], incs=[(vs, 1)])
    p.emit("vector", lambda e: e.tensor_mul(A0[0:16, :], enm[0:16, :], den[0:16, :]),
           waits=[(vs, vs.n)], incs=[(vs, 1)])
    aldone = vs.n
    tp = p.pool("at_mT", lambda: Pool(p, [cx.PT[:, 0:128]], "at_mT"))
    for lt in range(NT):
        reg, twv = tp.acquire()
        p.emit("tensor", lambda e, reg=reg, lt=lt: e.transpose(
                   reg[:, 0:16], exps[:, lt * 128:(lt + 1) * 128], ident(cx)[0:16, 0:16]),
               waits=[(vs, aldone), (tp.free, twv), (cx.consts_sem, 112)], incs=[(pp.mm, 1)])
        p.emit("vector", lambda e, reg=reg, lt=lt: e.tensor_copy(
                   cx.SMALL[:, ALC + lt * 16: ALC + (lt + 1) * 16], reg[0:16, :].transpose() if False else reg[0:16, :]),
               waits=[(pp.mm, pp.mm.n)], incs=[(tp.free, 1)]) if False else None
        p.emit("vector", lambda e, reg=reg, lt=lt: e.tensor_copy(
                   cx.SMALL[:, ALC + lt * 16: ALC + (lt + 1) * 16], reg[:, 0:16]),
               waits=[(pp.mm, pp.mm.n)], incs=[(tp.free, 1)])
    alT = [(tp.free, tp.uses)]
    return _emit_L0_part2(cx, pp, x_dram, xout, prev, locals())


def _emit_L0_part2(cx, pp, x_dram, xout, prev, env):
    p = cx.p
    dram, wbf2, inp = cx.dram, cx.wbf2, cx.inp
    srcB, srcA = env["srcB"], env["srcA"]
    vext, kfm, phiq = env["vext"], env["kfm"], env["phiq"]
    kr, alT, vs, aldone = env["kr"], env["alT"], env["vs"], env["aldone"]
    A0 = env["A0"]
    GB, BB = env["GB"], env["BB"]
    xn = env["xn"]
    ALC = 160
    bkb = emit_bias_bcast(cx, pp, inp["L0_attn_bphi_k"], 0, 1024, BB)
    pks = dram("l0_pks", [T, D], BF16)
    pkch = p.sem("l0_pk")
    def pks_cb(tt, psum, stage, waits, prel):
        p.emit("vector", lambda e, psum=psum, stage=stage: e.tensor_add(stage, psum, BB),
               waits=list(waits) + bkb + [(vs, aldone)], incs=[(pkch, 1)])
        p.emit("scalar", lambda e, stage=stage: e.activation(out=stage, in_=stage, func=AF.Tanh),
               waits=[(pkch, pkch.n)], incs=[(pp.evs, 1)])
        p.emit("vector", lambda e, stage=stage: e.tensor_scalar_add(stage, stage, 1.0),
               waits=[(pp.evs, pp.evs.n)], incs=[(pkch, 1)])
        for h in range(16):
            p.emit("vector", lambda e, stage=stage, h=h, tt=tt: e.tensor_scalar(
                       out=stage[:, h * 64:(h + 1) * 64], in0=stage[:, h * 64:(h + 1) * 64],
                       scalar1=cx.SMALL[:, ALC + tt * 16 + h: ALC + tt * 16 + h + 1],
                       scalar2=1.0, op0=OP.mult, op1=OP.mult),
                   waits=[(pkch, pkch.n)] + (alT if h == 0 else []),
                   incs=[prel] if h == 15 else ())
    dpk = projA(cx, pp, src_tile=lambda dc, tt: cx.SRC[:, dc * T + tt * 128: dc * T + tt * 128 + 128],
                K=D, Tn=T, wdram=wbf2("L0_attn_Wphi_k", (1024, 1024)),
                wT=False, N=D, dst_dram=pks, src_ready=kr + [(vs, aldone)], evac_cb=pks_cb)
    r0 = p.sem("l0_r0")
    p.dma("gpsimd", cx.ROWS[0:1, 2080:3104],
          flat_view(inp["L0_attn_bphi_k"], 0, [[1, 1], [1, 1024]]), r0,
          waits=[(pp.mm, pp.mm.n)])
    p.emit("scalar", lambda e: e.activation(out=cx.ROWS[0:1, 2080:3104],
                                            in_=cx.ROWS[0:1, 2080:3104], func=AF.Tanh),
           waits=[(r0, r0.n)], incs=[(pp.evs, 1)])
    p.emit("vector", lambda e: e.tensor_scalar_add(cx.ROWS[0:1, 2080:3104],
                                                   cx.ROWS[0:1, 2080:3104], 1.0),
           waits=[(pp.evs, pp.evs.n)], incs=[(pkch, 1)])
    tp = p.pool("at_mT", lambda: Pool(p, [cx.PT[:, 0:128]], "at_mT"))
    a0b = cx.ROWS[:, 3480:3481]
    p.emit("vector", lambda e: e.tensor_copy(a0b[0:16, :], A0[0:16, :]),
           waits=[(vs, aldone)], incs=[(pkch, 1)])
    reg, twv = tp.acquire()
    p.emit("tensor", lambda e, reg=reg: e.transpose(reg[0:1, 0:16], a0b[0:16, :],
                                                    ident(cx)[0:16, 0:16]),
           waits=[(pkch, pkch.n), (tp.free, twv), (cx.consts_sem, 112)], incs=[(pp.mm, 1)])
    p.emit("vector", lambda e, reg=reg: e.tensor_copy(cx.SMALL[0:1, 744:760], reg[0:1, 0:16]),
           waits=[(pp.mm, pp.mm.n)], incs=[(tp.free, 1)])
    a0t = [(tp.free, tp.uses)]
    for h in range(16):
        p.emit("vector", lambda e, h=h: e.tensor_scalar(
                   out=cx.ROWS[0:1, 2080 + h * 64:2080 + (h + 1) * 64],
                   in0=cx.ROWS[0:1, 2080 + h * 64:2080 + (h + 1) * 64],
                   scalar1=cx.SMALL[0:1, 744 + h:744 + h + 1],
                   scalar2=1.0, op0=OP.mult, op1=OP.mult),
               waits=[(pkch, pkch.n)] + (a0t if h == 0 else []), incs=[(pkch, 1)])
    pks0done = [(pkch, pkch.n)]
    zv = p.sem("l0_zv")
    p.emit("vector", lambda e: e.memset(cx.ROWS[0:1, 3392:3457], 0.0), incs=[(zv, 1)])
    p.emit("vector", lambda e: e.memset(cx.ROWS[0:1, 3456:3457], 1.0),
           waits=[(zv, zv.n)], incs=[(zv, 1)])
    zvdone = zv.n
    return _emit_L0_part3(cx, pp, x_dram, xout, prev, env, locals())


def _emit_L0_part3(cx, pp, x_dram, xout, prev, env, env2):
    p = cx.p
    dram, wbf2, inp = cx.dram, cx.wbf2, cx.inp
    vext, phiq, xn = env["vext"], env["phiq"], env["xn"]
    pks = env2["pks"]
    dpk, pks0done, zvdone, zv = env2["dpk"], env2["pks0done"], env2["zvdone"], env2["zv"]
    srcB, srcA = env["srcB"], env["srcA"]
    pl = p.sem("lf")
    load_fm(cx, flat_view(pks, 0, [[1024, 2048], [1, 1024]]) if False else pks,
            0, 0, None, 0, pl) if False else None
    pkl = p.sem("l0_pkl")
    src = flat_view(pks, 0, [[1024, 128], [128 * 1024, 16], [1, 1024]])
    p.dma("sync", cx.SBF[:, 0:16384].rearrange("p (a b) -> p a b", b=1024), src, pkl,
          waits=dpk + pp.pb512.all_done() + pp.pa1024.all_done() + [(pkl, pkl.n)])
    pklt = pkl.n
    ve2 = [p.sem("l0_ve0"), p.sem("l0_ve1")]
    VE = [cx.ROWS[:, 0:1040], cx.ROWS[:, 1040:2080]]
    kvavail = p.sem("l0_kva")
    def kvp(h):
        return cx.PA[0:64, 512 * (h // 7) + (h % 7) * 65: 512 * (h // 7) + (h % 7) * 65 + 65]
    vdone = env["vdone"]
    for lt in range(NT):
        sl = lt % 2
        p.dma("sync", VE[sl], vext[lt * 128:(lt + 1) * 128, :], ve2[sl],
              waits=([(pp.mm, pp.mm.n)] + list(vdone)) if lt < 2 else [(kvavail, kvavail.n)])
        vet = ve2[sl].n
        for h in range(16):
            p.emit("tensor", lambda e, h=h, lt=lt, sl=sl: e.matmul(
                       kvp(h), cx.SBF[:, lt * 1024 + h * 64: lt * 1024 + h * 64 + 64],
                       VE[sl][:, h * 65:(h + 1) * 65],
                       start=(lt == 0), stop=False, skip_group_check=True),
                   waits=([(pkl, pklt), (ve2[sl], vet)] if h == 0 else ()),
                   incs=[(kvavail, 1)] if h == 15 else ())
    kve = p.sem("l0_kve")
    for h in range(16):
        p.emit("tensor", lambda e, h=h: e.matmul(
                   kvp(h), cx.ROWS[0:1, 2080 + h * 64:2080 + (h + 1) * 64],
                   cx.ROWS[0:1, 3392:3457], start=False, stop=True, skip_group_check=True),
               waits=(pks0done + [(zv, zvdone), (kvavail, kvavail.n)]) if h == 0 else (),
               incs=[(pp.mm, 1)] if h == 15 else ())
    kvm = pp.mm.n
    def kvsb(h):
        b = (h % 2) * 64
        c = 14336 + (h // 2) * 65
        return cx.SBF[b:b + 64, c:c + 65]
    for h in range(16):
        p.emit("vector", lambda e, h=h: e.tensor_copy(kvsb(h), kvp(h)),
               waits=[(pp.mm, kvm)] if h == 0 else (), incs=[(kve, 1)])
    kved = kve.n
    phl = p.sem("lf")
    load_fm(cx, phiq, D, T, cx.SRC, 0, phl, waits=[(kve, kved)])
    phr = [(phl, phl.n)]
    attn = dram("l0_attn", [T, D], BF16)
    nd = p.sem("l0_nd")
    ast2 = [p.sem("l0_as0"), p.sem("l0_as1")]
    lastp = [None]
    for h in range(16):
        for tt in range(NT):
            ps, pw, prel = pp.pb512.acquire()
            psum = ps[:, 0:65]
            p.emit("tensor", lambda e, h=h, tt=tt, psum=psum: e.matmul(
                       psum, cx.SRC[(h % 2) * 64:(h % 2) * 64 + 64,
                                    (h // 2) * T + tt * 128: (h // 2) * T + tt * 128 + 128],
                       kvsb(h), start=True, stop=True),
                   waits=phr + pw + ([(kve, kved)] if (h == 0 and tt == 0) else []),
                   incs=[(pp.mm, 1)])
            mmv = pp.mm.n
            dcol = cx.SMALL[:, 728:729]
            p.emit("vector", lambda e, psum=psum: e.tensor_scalar_add(dcol, psum[:, 64:65], 1e-6),
                   waits=[(pp.mm, mmv), (nd, nd.n)] + ([lastp[0]] if lastp[0] else []),
                   incs=[(nd, 1)])
            p.emit("vector", lambda e: e.reciprocal(dcol, dcol),
                   waits=[(nd, nd.n)], incs=[(nd, 1)])
            i = (h * NT + tt) % 2
            stg = cx.SF[0:128, 3072 + i * 64: 3072 + i * 64 + 64]
            p.emit("vector", lambda e, psum=psum, stg=stg: e.tensor_scalar(
                       out=stg, in0=psum[:, 0:64], scalar1=dcol, scalar2=1.0,
                       op0=OP.mult, op1=OP.mult),
                   waits=[(nd, nd.n)] + ([(ast2[i], ast2[i].n)] if ast2[i].n > 0 else []),
                   incs=[prel])
            p.dma("gpsimd", attn[tt * 128:(tt + 1) * 128, h * 64:(h + 1) * 64], stg,
                  ast2[i], waits=[(prel[0], prel[0].n)])
            lastp[0] = (prel[0], prel[0].n)
    attn_done = [(ast2[0], ast2[0].n), (ast2[1], ast2[1].n)]
    return _emit_L0_part4(cx, pp, x_dram, xout, prev, env, env2, locals())


def _emit_L0_part4(cx, pp, x_dram, xout, prev, env, env2, env3):
    p = cx.p
    dram, wbf2, inp = cx.dram, cx.wbf2, cx.inp
    attn, attn_done = env3["attn"], env3["attn_done"]
    srcB, srcA, xn = env["srcB"], env["srcA"], env["xn"]
    GB, BB = env["GB"], env["BB"]
    TP = T + 4
    afp = dram("l0_afp", [D, TP], BF16)
    zb = p.sem("l0_zb")
    zbd = p.sem("l0_zbd")
    zb0 = zb.n
    p.emit("vector", lambda e: e.memset(cx.ROWS[0:128, 3488:3490], 0.0), incs=[(zb, 1)])
    for dc in range(ND):
        for side0 in (0, T + 2):
            p.dma("gpsimd", afp[dc * 128:(dc + 1) * 128, side0:side0 + 2],
                  cx.ROWS[0:128, 3488:3490], zbd, waits=[(zb, zb0 + 1)])
    zbt = zbd.n
    afb = emit_tm2fm(cx, pp, attn, afp, col_off=2, row_stride=TP, prev_done=attn_done)
    gsem = p.sem("l0_g")
    gsemd = p.sem("l0_gd")
    p.dma("gpsimd", cx.ROWS[0:1, 3496:3497],
          flat_view(inp["L0_attn_gamma"], 0, [[1, 1], [1, 1]]), gsemd)
    ps0, pw0, prel0 = pp.pb512.acquire()
    p.emit("tensor", lambda e, ps0=ps0: e.matmul(ps0[:, 0:1], cx.CBF[0:1, 384:512],
                                                 cx.ROWS[0:1, 3496:3497], start=True, stop=True),
           waits=[(gsemd, gsemd.n), (cx.consts_sem, 112)] + pw0, incs=[(pp.mm, 1)])
    p.emit("vector", lambda e, ps0=ps0: e.tensor_copy(cx.SMALL[:, 740:741], ps0[:, 0:1]),
           waits=[(pp.mm, pp.mm.n)], incs=[prel0])
    gcol = [(prel0[0], prel0[0].n)]
    cbb = emit_bias_bcast(cx, pp, inp["L0_attn_conv_b"], 0, 1024, GB)
    p.emit("vector", lambda e: e.tensor_scalar(out=GB, in0=GB, scalar1=cx.SMALL[:, 740:741],
                                               scalar2=1.0, op0=OP.mult, op1=OP.mult),
           waits=cbb + gcol, incs=[(gsem, 1)])
    gcbd = gsem.n
    cl = p.sem("l0_cl")
    srcv = flat_view(afp, 0, [[TP, 128], [128 * TP, 7], [1, TP]])
    p.dma("sync", cx.SRC[:, 0:7 * TP].rearrange("p (k t) -> p k t", k=7), srcv, cl,
          waits=afb + [(zbd, zbt), (cl, cl.n)])
    srcv7 = flat_view(afp, 7 * 128 * TP, [[TP, 128], [1, TP]])
    p.dma("sync", cx.SRC2[:, 0:TP], srcv7, cl, waits=[(cl, cl.n)])
    clt = cl.n
    def afp_sb(dc):
        return cx.SRC[:, dc * TP:(dc + 1) * TP] if dc < 7 else cx.SRC2[:, 0:TP]
    twv = cx.wb["L0_attn_conv_w"]
    cw = p.sem("l0_cw")
    p.dma("sync", cx.WRES[0:64, 0:5120], flat_view(twv, 0, [[5120, 64], [1, 5120]]), cw,
          waits=pp.wguard() + [(cw, cw.n)])
    p.dma("sync", cx.WRES[64:128, 0:5120], flat_view(twv, 0, [[5120, 64], [1, 5120]]), cw,
          waits=[(cw, cw.n)])
    cwt = cw.n
    a2 = dram("l0_a2", [T, D], F32)
    a2st = [p.sem("l0_a2s0"), p.sem("l0_a2s1")]
    avl = [p.sem("l0_avl0"), p.sem("l0_avl1")]
    avs = [cx.SF[:, 0:1024], cx.SF[:, 1024:2048]]
    cch = p.sem("l0_cch")
    for tt in range(NT):
        i = tt % 2
        p.dma("gpsimd", avs[i], attn[tt * 128:(tt + 1) * 128, :], avl[i],
              waits=([(a2st[i], a2st[i].n)] if a2st[i].n > 0 else list(attn_done)) + [(gsem, gcbd)])
        stg = cx.SF[:, 2048 + i * 1024:2048 + i * 1024 + 1024]
        for g in range(16):
            ps, pw, prel = pp.pb512.acquire()
            psum = ps[:, 0:64]
            dc, half = g // 2, (g % 2) * 64
            for k in range(5):
                p.emit("tensor", lambda e, psum=psum, dc=dc, half=half, g=g, k=k, tt=tt: e.matmul(
                           psum,
                           afp_sb(dc)[half:half + 64, tt * 128 + k: tt * 128 + k + 128],
                           cx.WRES[half:half + 64, k * 1024 + g * 64: k * 1024 + g * 64 + 64],
                           start=(k == 0), stop=(k == 4)),
                       waits=([(cl, clt), (cw, cwt)] + pw) if k == 0 else (),
                       incs=[(pp.mm, 1)] if k == 4 else ())
            mmv = pp.mm.n
            p.emit("vector", lambda e, psum=psum, stg=stg, g=g: e.tensor_scalar(
                       out=stg[:, g * 64:(g + 1) * 64], in0=psum,
                       scalar1=cx.SMALL[:, 740:741], scalar2=1.0, op0=OP.mult, op1=OP.mult),
                   waits=[(pp.mm, mmv), (avl[i], avl[i].n)], incs=[prel])
        p.emit("vector", lambda e, stg=stg: e.tensor_add(stg, stg, GB),
               waits=[(s, s.n) for s in pp.pb512.sems], incs=[(cch, 1)])
        p.emit("vector", lambda e, stg=stg, i=i: e.tensor_add(stg, stg, avs[i]),
               waits=[(cch, cch.n)], incs=[(cch, 1)])
        p.dma("gpsimd", a2[tt * 128:(tt + 1) * 128, :], stg, a2st[i],
              waits=[(cch, cch.n)])
    a2done = [(a2st[0], a2st[0].n), (a2st[1], a2st[1].n)]
    return _emit_L0_part5(cx, pp, x_dram, xout, prev, env, env2, env3, locals())


def _emit_L0_part5(cx, pp, x_dram, xout, prev, env, env2, env3, env4):
    p = cx.p
    dram, wbf2, inp = cx.dram, cx.wbf2, cx.inp
    a2, a2done = env4["a2"], env4["a2done"]
    srcB, srcA, xn = env["srcB"], env["srcA"], env["xn"]
    GB, BB = env["GB"], env["BB"]
    gb = emit_bias_bcast(cx, pp, inp["L0_attn_lna_g"], 0, 1024, GB)
    bb = emit_bias_bcast(cx, pp, inp["L0_attn_lna_b"], 0, 1024, BB)
    a2f = dram("l0_a2f", [D, T], BF16)
    a2t = dram("l0_a2t", [T, D], BF16)
    lnd = emit_lnG(cx, pp, a2, a2f, 64, 80, GB, BB, prev_done=a2done + gb + bb,
                   tm_dram=a2t)
    al = p.sem("lf")
    load_fm(cx, a2f, D, T, cx.SRC, 0, al, waits=lnd)
    ar = [(al, al.n)]
    aB = lambda dc: cx.SRC[:, dc * T:(dc + 1) * T]
    colf, b1done, _ = vecs_to_cols(cx, pp, [(inp["L0_attn_b1"], 0, 4096, 1.0)], 688)
    hd = dram("l0_h", [4096, T], BF16)
    projB(cx, pp, src_tile=aB, K=D, Tn=T, wdram=wbf2("L0_attn_W1", (1024, 4096)),
          wT=False, N=4096, dst_dram=hd, src_ready=ar + b1done,
          bias_col=lambda oc: colf(0, oc), act=AF.Gelu)
    b2b = p.sem("l0_b2")
    p.dma("gpsimd", cx.ROWS[0:1, 2176:3200],
          flat_view(inp["L0_attn_b2"], 0, [[1, 1], [1, 1024]]), b2b,
          waits=[(pp.mm, pp.mm.n)])
    b2t = b2b.n
    s0 = dram("l0_s", [T, D], F32)
    sst = [p.sem("l0_ss0"), p.sem("l0_ss1")]
    sld = [p.sem("l0_sl0"), p.sem("l0_sl1")]
    s_sl = [cx.SF[:, 0:1024], cx.SF[:, 1024:2048]]
    x_sl = [cx.SF[:, 2048:3072], cx.SF[:, 3072:4096]]
    sch = p.sem("l0_sch")
    al2 = [p.sem("l0_al0"), p.sem("l0_al1")]
    xl2 = [p.sem("l0_xl0"), p.sem("l0_xl1")]
    a_sl = [cx.SF[:, 4096:5120], cx.SF[:, 5120:6144]]
    for half in range(2):
        w2v = flat_view(cx.wb["L0_attn_W2"], half * 2048 * 1024, [[1024, 2048], [1, 1024]])
        hstream = (hd, cx.SBF[:, 8192:10240], cx.SBF[:, 10240:12288], half * 2048)
        def s0_cb(tt, psum, stage, waits, prel, half=half):
            i = tt % 2
            if half == 0:
                p.emit("vector", lambda e, psum=psum, stage=stage: e.tensor_copy(stage, psum),
                       waits=list(waits) + ([(sst[i], sst[i].n)] if sst[i].n > 0 else []),
                       incs=[prel])
            else:
                swp = [(sst[i], 16 * (8 + tt // 2))]
                p.dma("sync", s_sl[i], s0[tt * 128:(tt + 1) * 128, :], sld[i], waits=swp)
                p.dma("gpsimd", a_sl[i], a2t[tt * 128:(tt + 1) * 128, :], al2[i], waits=swp)
                p.dma("sync", x_sl[i], x_dram[tt * 128:(tt + 1) * 128, :], xl2[i])
                p.emit("vector", lambda e, psum=psum, stage=stage, i=i: e.tensor_add(
                           stage, psum, s_sl[i]),
                       waits=list(waits) + [(sld[i], sld[i].n)], incs=[(sch, 1)])
                p.emit("vector", lambda e, stage=stage, i=i, tt=tt: e.tensor_scalar(
                           out=x_sl[i], in0=x_sl[i], scalar1=cx.SMALL[:, 0 + tt:0 + tt + 1],
                           scalar2=cx.SMALL[:, 16 + tt:16 + tt + 1],
                           op0=OP.subtract, op1=OP.mult),
                       waits=[(xl2[i], xl2[i].n), (sch, sch.n)], incs=[(sch, 1)])
                p.emit("vector", lambda e, stage=stage, i=i: e.tensor_add(stage, stage, x_sl[i]),
                       waits=[(sch, sch.n)], incs=[(sch, 1)])
                p.emit("vector", lambda e, stage=stage, i=i: e.tensor_add(stage, stage, a_sl[i]),
                       waits=[(al2[i], al2[i].n), (sch, sch.n)], incs=[prel])
            p.dma("gpsimd", s0[tt * 128:(tt + 1) * 128, :],
                  stage, sst[i], waits=[(prel[0], prel[0].n)])
        projA(cx, pp, src_stream=hstream, K=2048, Tn=T, wdram=w2v, wT=False, N=D,
              src_ready=[(b2b, b2t)], evac_cb=s0_cb, store=False,
              extra_row=cx.ROWS[0:1, 2176:3200] if half == 0 else None)
    s0done = [(sst[0], sst[0].n), (sst[1], sst[1].n)]
    return _emit_L0_part6(cx, pp, x_dram, xout, prev, env, s0, s0done)


def _emit_L0_part6(cx, pp, x_dram, xout, prev, env, s0, s0done):
    p = cx.p
    dram, wbf2, inp = cx.dram, cx.wbf2, cx.inp
    srcA, srcB, xn = env["srcA"], env["srcB"], env["xn"]
    GB = env["GB"]
    xl = p.sem("lf")
    load_fm(cx, xn, D, T, cx.SRC, 0, xl, waits=s0done)
    xr = [(xl, xl.n)]
    xA = lambda dc, tt: cx.SRC[:, dc * T + tt * 128: dc * T + tt * 128 + 128]
    COEF = 416
    rdone = emit_router(cx, pp, src_tile=xA, src_ready=xr,
                        wdram=wbf2("L0_ffn_gate_w", (1024, 16)), wT=False, E=16,
                        bias_row_dram=None, mode="soft_raw", coeff_col0=COEF)
    acc = dram("l0_acc", [T, D], F32)
    acld = [p.sem("mo_ld0"), p.sem("mo_ld1")]
    msem = [p.sem("mo_st0"), p.sem("mo_st1")]
    m0 = [msem[0].n, msem[1].n]
    a0 = [acld[0].n, acld[1].n]
    mst = [cx.SF[:, 4096:5120], cx.SF[:, 5120:6144]]
    ech = p.sem("l0_ech")
    for e in range(16):
        bb = emit_bias_bcast(cx, pp, inp["L0_ffn_expert_b"], e * 1024, 1024, GB)
        wv = flat_view(cx.wb["L0_ffn_expert_w"], e * 1024 * 1024, [[1024, 1024], [1, 1024]])
        def moe_cb(tt, psum, stage, waits, prel, pas=e):
            i = tt % 2
            nst = pas * 8 + tt // 2
            ms = mst[i]
            w = list(waits) + bb + ([(msem[i], m0[i] + 16 * nst)] if nst > 0 else [])
            p.emit("vector", lambda e2, psum=psum, stage=stage: e2.tensor_add(stage, psum, GB),
                   waits=w, incs=[(ech, 1)])
            if pas == 0:
                p.emit("vector", lambda e2, stage=stage, ms=ms, tt=tt, pas=pas: e2.tensor_scalar(
                           out=ms, in0=stage, scalar1=0.0,
                           scalar2=cx.SMALL[:, COEF + tt * 16 + pas:COEF + tt * 16 + pas + 1],
                           op0=OP.max, op1=OP.mult),
                       waits=[(ech, ech.n)], incs=[prel])
            else:
                slot = cx.SF[:, i * 1024: i * 1024 + 1024]
                p.dma("sync", slot, acc[tt * 128:(tt + 1) * 128, :], acld[i],
                      waits=[(msem[i], m0[i] + 16 * (pas * 8 + tt // 2))])
                p.emit("vector", lambda e2, stage=stage, tt=tt, pas=pas: e2.tensor_scalar(
                           out=stage, in0=stage, scalar1=0.0,
                           scalar2=cx.SMALL[:, COEF + tt * 16 + pas:COEF + tt * 16 + pas + 1],
                           op0=OP.max, op1=OP.mult),
                       waits=[(ech, ech.n)], incs=[(ech, 1)])
                p.emit("vector", lambda e2, stage=stage, ms=ms, slot=slot: e2.tensor_add(
                           ms, stage, slot),
                       waits=[(ech, ech.n), (acld[i], acld[i].n)], incs=[prel])
            p.dma("gpsimd", acc[tt * 128:(tt + 1) * 128, :], ms, msem[i],
                  waits=[(prel[0], prel[0].n)])
        projA(cx, pp, src_tile=xA, K=D, Tn=T, wdram=wv, wT=False, N=D,
              src_ready=xr + (rdone if e == 0 else []), evac_cb=moe_cb, store=False)
    upd = emit_update(cx, pp, x_dram, s0, acc, xout,
                      pre_done=[(msem[0], msem[0].n), (msem[1], msem[1].n)] + s0done)
    return upd


LAYER_EMITTERS[0] = emit_L0



def emit_fp4(cx, pp, src_dram, dst_fm, R, C, pre_ln=None, src_bf=False, prev_done=(),
             tmul=None):
    p = cx.p
    nb = C // 16
    nr = R // 128
    nc_ = C // 128
    eng = "gpsimd" if src_bf else "sync"
    ld2 = [p.sem("q_l0" + eng), p.sem("q_l1" + eng)]
    st2 = [p.sem("q_s0"), p.sem("q_s1")]
    ch = p.sem("chv")
    tpd = p.sem("q_tp")
    cpd = p.sem("q_cp")
    tp0, cp0 = tpd.n, cpd.n
    slots = [cx.SF[:, 0:1024], cx.SF[:, 1024:2048]]
    big = C > 1024
    if big:
        slots = [cx.SF[:, 0:2816], cx.SF[:, 2816:5632]]
    scs = cx.SF[:, 5632:5632 + 352]
    sgn = cx.SBF[:, 2048:2048 + 2816]
    acc = cx.SF[:, 5888:5888 + 2816] if False else None
    qtm = [cx.SBF[:, 7680:7680 + 2816], cx.SBF[:, 10496:10496 + 2816]]
    fms = [cx.SBF[:, 13312:14336], cx.SBF[:, 14336:15360]]
    for rt in range(nr):
        i = rt % 2
        slot = slots[i]
        w0 = ([(ch, ch.n)] if rt >= 2 else list(prev_done)) + [(ld2[i], ld2[i].n)]
        p.dma(eng, slot, src_dram[rt * 128:(rt + 1) * 128, :], ld2[i], waits=w0)
        wv = [(ld2[i], ld2[i].n)]
        if pre_ln is not None:
            MC0, RC0 = pre_ln
            p.emit("vector", lambda e, slot=slot, rt=rt: e.tensor_scalar(
                       out=slot, in0=slot, scalar1=cx.SMALL[:, pre_ln[0] + rt:pre_ln[0] + rt + 1],
                       scalar2=cx.SMALL[:, pre_ln[1] + rt:pre_ln[1] + rt + 1],
                       op0=OP.subtract, op1=OP.mult),
                   waits=wv, incs=[(ch, 1)])
            wv = [(ch, ch.n)]
        s3 = slot.rearrange("p (b k) -> p b k", k=16)
        p.emit("vector", lambda e, s3=s3: e.reduce_max(
                   scs[:, 0:nb], s3, axis=AX.X, apply_absolute_value=True),
               waits=wv, incs=[(ch, 1)])
        p.emit("vector", lambda e: e.tensor_scalar(out=scs[:, 0:nb], in0=scs[:, 0:nb],
                                                   scalar1=1.0 / 6.0, scalar2=1e-12,
                                                   op0=OP.mult, op1=OP.max),
               waits=[(ch, ch.n)], incs=[(ch, 1)])
        p.emit("vector", lambda e: e.reciprocal(scs[:, nb:2 * nb], scs[:, 0:nb]),
               waits=[(ch, ch.n)], incs=[(ch, 1)])
        r3 = scs[:, nb:2 * nb].rearrange("p b -> p b 1" if False else "p (b o) -> p b o", o=1)
        p.emit("vector", lambda e, s3=s3, r3=r3: e.tensor_tensor(
                   s3, s3, r3.broadcast_to((128, nb, 16)), op=OP.mult),
               waits=[(ch, ch.n)], incs=[(ch, 1)])
        p.emit("scalar", lambda e, slot=slot: e.activation(out=sgn[:, 0:C], in_=slot, func=AF.Sign),
               waits=[(ch, ch.n)], incs=[(pp.evs, 1)])
        p.emit("vector", lambda e, slot=slot: e.tensor_mul(slot, slot, sgn[:, 0:C]),
               waits=[(pp.evs, pp.evs.n)], incs=[(ch, 1)])
        qacc = qtm[i][:, 0:C]
        first = True
        for thr, stepv in [(0.25, 0.5), (0.75, 0.5), (1.25, 0.5), (1.75, 0.5),
                           (2.5, 1.0), (3.5, 1.0), (5.0, 2.0)]:
            if first:
                p.emit("vector", lambda e, slot=slot, qacc=qacc, thr=thr, stepv=stepv:
                           e.tensor_scalar(out=qacc, in0=slot, scalar1=thr, scalar2=stepv,
                                           op0=OP.is_gt, op1=OP.mult),
                       waits=[(ch, ch.n)], incs=[(ch, 1)])
                first = False
            else:
                p.emit("vector", lambda e, slot=slot, thr=thr, stepv=stepv:
                           e.tensor_scalar(out=sgn[:, 2816:2816 + C] if False else cx.SBF[:, 4864:4864 + C],
                                           in0=slot, scalar1=thr, scalar2=stepv,
                                           op0=OP.is_gt, op1=OP.mult),
                       waits=[(ch, ch.n)], incs=[(ch, 1)])
                p.emit("vector", lambda e, qacc=qacc: e.tensor_add(
                           qacc, qacc, cx.SBF[:, 4864:4864 + C]),
                       waits=[(ch, ch.n)], incs=[(ch, 1)])
        p.emit("vector", lambda e, qacc=qacc: e.tensor_mul(qacc, qacc, sgn[:, 0:C]),
               waits=[(ch, ch.n)], incs=[(ch, 1)])
        q3 = qacc.rearrange("p (b k) -> p b k", k=16)
        sc3 = scs[:, 0:nb].rearrange("p (b o) -> p b o", o=1)
        p.emit("vector", lambda e, q3=q3, sc3=sc3: e.tensor_tensor(
                   q3, q3, sc3.broadcast_to((128, nb, 16)), op=OP.mult),
               waits=[(ch, ch.n)], incs=[(ch, 1)])
        qdone = ch.n
        for dc in range(nc_):
            p.emit("tensor", lambda e, qacc=qacc, dc=dc: e.transpose(
                       cx.PT[:, (dc % 8) * 128:(dc % 8) * 128 + 128],
                       qacc[:, dc * 128:(dc + 1) * 128], ident(cx)),
                   waits=([(ch, qdone)] if dc == 0 else []) + ([(cpd, cpd.n)] if dc % 8 == 0 else []),
                   incs=[(tpd, 1)] if (dc % 8 == 7 or dc == nc_ - 1) else ())
            if dc % 8 == 7 or dc == nc_ - 1:
                lo = (dc // 8) * 8
                n8 = dc - lo + 1
                fs = fms[i]
                p.emit("vector", lambda e, fs=fs, n8=n8: e.tensor_copy(
                           fs[:, 0:n8 * 128], cx.PT[:, 0:n8 * 128]),
                       waits=[(tpd, tpd.n)] + ([(st2[i], st2[i].n)] if st2[i].n > 0 else []),
                       incs=[(cpd, 1)])
                dst = flat_view(dst_fm, lo * 128 * R + rt * 128,
                                [[R, 128], [128 * R, n8], [1, 128]])
                p.dma("gpsimd", dst, fs[:, 0:n8 * 128].rearrange("p (k t) -> p k t", k=n8),
                      st2[i], waits=[(cpd, cpd.n)])
    return [(st2[0], st2[0].n), (st2[1], st2[1].n)]


def emit_fm2tm(cx, pp, src_fm, dst_tm, prev_done=()):
    p = cx.p
    ld2 = [p.sem("f2t_l0"), p.sem("f2t_l1")]
    st2 = [p.sem("ln_st0"), p.sem("ln_st1")]
    tpdone = p.sem("ln_tp")
    cpdone = p.sem("ln_cp")
    tp0, cp0 = tpdone.n, cpdone.n
    ins = [cx.SBF[:, 0:1024], cx.SBF[:, 1024:2048]]
    outs = [cx.SBF[:, 2048:3072], cx.SBF[:, 3072:4096]]
    for tt in range(NT):
        i = tt % 2
        blk = ins[i]
        fs = outs[i]
        waits = ([(cpdone, cp0 + tt - 1)] if tt >= 2 else list(prev_done)) \
            + [(cx.consts_sem, 112), (ld2[i], ld2[i].n)]
        srcv = flat_view(src_fm, tt * 128, [[T, 128], [128 * T, ND], [1, 128]])
        p.dma("sync", blk.rearrange("p (k t) -> p k t", k=ND), srcv, ld2[i], waits=waits)
        for dc in range(ND):
            p.emit("tensor", lambda e, blk=blk, dc=dc: e.transpose(
                       cx.PT[:, dc * 128:(dc + 1) * 128], blk[:, dc * 128:(dc + 1) * 128], ident(cx)),
                   waits=([(ld2[i], ld2[i].n)] if dc == 0 else []) + ([(cpdone, cp0 + tt)] if (dc == 0 and tt >= 1) else []),
                   incs=[(tpdone, 1)] if dc == ND - 1 else ())
        p.emit("vector", lambda e, fs=fs: e.tensor_copy(fs, cx.PT[:, :]),
               waits=[(tpdone, tp0 + tt + 1)] + ([(st2[i], st2[i].n)] if st2[i].n > 0 else []),
               incs=[(cpdone, 1)])
        p.dma("gpsimd", dst_tm[tt * 128:(tt + 1) * 128, :], fs, st2[i],
              waits=[(cpdone, cp0 + tt + 1)])
    return [(st2[0], st2[0].n), (st2[1], st2[1].n)]


def emit_ropephi(cx, pp, src_fm, dst_fm, do_phi, prev_done=()):
    p = cx.p
    ld2 = [p.sem("rp_l0"), p.sem("rp_l1")]
    st2 = [p.sem("ln_st0"), p.sem("ln_st1")]
    ch = p.sem("chv")
    COS, SIN = cx.SF[:, 4096:6144], cx.SF[:, 6144:8192]
    ins = [cx.SBF[:, 0:2048], cx.SBF[:, 2048:4096]]
    stg = [cx.SBF[:, 4096:6144], cx.SBF[:, 6144:8192]]
    tmp = cx.SBF[:, 8192:10240]
    tmp2 = cx.SBF[:, 10240:12288]
    for dc in range(ND):
        i = dc % 2
        blk = ins[i]
        out = stg[i]
        w0 = ([(ch, ch.n)] if dc >= 2 else list(prev_done)) + [(ld2[i], ld2[i].n)]
        p.dma("sync", blk, src_fm[dc * 128:(dc + 1) * 128, :], ld2[i], waits=w0)
        for ncc in range(4):
            ps, pw, prel = pp.pb512.acquire()
            psum = ps[:, 0:512]
            sl = slice(ncc * 512, (ncc + 1) * 512)
            p.emit("tensor", lambda e, psum=psum, blk=blk, sl=sl: e.matmul(
                       psum, cx.CBF[:, 256:384], blk[:, sl], start=True, stop=True),
                   waits=[(ld2[i], ld2[i].n)] + pw, incs=[(pp.mm, 1)])
            mmv = pp.mm.n
            p.emit("vector", lambda e, psum=psum, sl=sl: e.tensor_mul(
                       tmp[:, sl], psum, SIN[:, sl]),
                   waits=[(pp.mm, mmv)], incs=[prel])
            p.emit("vector", lambda e, blk=blk, sl=sl: e.tensor_mul(
                       tmp2[:, sl], blk[:, sl], COS[:, sl]),
                   waits=[(prel[0], prel[0].n)], incs=[(ch, 1)])
            p.emit("vector", lambda e, out=out, sl=sl: e.tensor_add(
                       tmp2[:, sl], tmp2[:, sl], tmp[:, sl]),
                   waits=[(ch, ch.n)], incs=[(ch, 1)])
            if do_phi:
                p.emit("vector", lambda e, sl=sl: e.tensor_scalar_min(
                           tmp[:, sl], tmp2[:, sl], 0.0),
                       waits=[(ch, ch.n)] + ([(st2[i], st2[i].n)] if ncc == 0 else []),
                       incs=[(ch, 1)])
                p.emit("scalar", lambda e, sl=sl: e.activation(
                           out=tmp[:, sl], in_=tmp[:, sl], func=AF.Exp),
                       waits=[(ch, ch.n)], incs=[(pp.evs, 1)])
                p.emit("vector", lambda e, sl=sl: e.tensor_scalar_max(
                           tmp2[:, sl], tmp2[:, sl], 0.0),
                       waits=[(pp.evs, pp.evs.n), (ch, ch.n)], incs=[(ch, 1)])
                p.emit("vector", lambda e, out=out, sl=sl: e.tensor_add(
                           out[:, sl], tmp2[:, sl], tmp[:, sl]),
                       waits=[(ch, ch.n)], incs=[(ch, 1)])
            else:
                p.emit("vector", lambda e, out=out, sl=sl: e.tensor_copy(
                           out[:, sl], tmp2[:, sl]),
                       waits=[(ch, ch.n)] + ([(st2[i], st2[i].n)] if ncc == 0 else []),
                       incs=[(ch, 1)])
        p.dma("gpsimd", dst_fm[dc * 128:(dc + 1) * 128, :], out, st2[i],
              waits=[(ch, ch.n)])
    return [(st2[0], st2[0].n), (st2[1], st2[1].n)]



def emit_L1(cx, pp, x_dram, xout, prev):
    p = cx.p
    dram, wbf2, inp = cx.dram, cx.wbf2, cx.inp
    xn = dram("l1_xn", [D, T], BF16)
    lnst = emit_ln(cx, pp, x_dram, xn, 0, 16, cx.consts_sem, prev_done=prev)
    GB, BB = cx.SF[:, 6144:7168], cx.SF[:, 7168:8192]
    xq = dram("l1_xq", [D, T], BF16)
    f1 = emit_fp4(cx, pp, x_dram, xq, T, D, pre_ln=(0, 16), prev_done=lnst)
    wgq = dram("l1_wgq", [D, 2816], BF16)
    wuq = dram("l1_wuq", [D, 2816], BF16)
    wdq = dram("l1_wdq", [2816, D], BF16)
    f2 = emit_fp4(cx, pp, flat_view(cx.wb["L1_ffn_w_gate"], 0, [[1024, 2816], [1, 1024]]),
                  wgq, 2816, 1024, src_bf=True, prev_done=[(cx.pcsem, cx.pctarget)] + f1)
    f3 = emit_fp4(cx, pp, flat_view(cx.wb["L1_ffn_w_up"], 0, [[1024, 2816], [1, 1024]]),
                  wuq, 2816, 1024, src_bf=True, prev_done=f2)
    f4 = emit_fp4(cx, pp, flat_view(cx.wb["L1_ffn_w_down"], 0, [[2816, 1024], [1, 2816]]),
                  wdq, 1024, 2816, src_bf=True, prev_done=f3)
    xl = p.sem("lf")
    load_fm(cx, xq, D, T, cx.SRC, 0, xl, waits=f4)
    xqr = [(xl, xl.n)]
    xqA = lambda dc, tt: cx.SRC[:, dc * T + tt * 128: dc * T + tt * 128 + 128]
    gated = dram("l1_gate", [T, 2816], F32)
    upd_ = dram("l1_up", [T, 2816], F32)
    gsl = [cx.SF[:, 0:1024], cx.SF[:, 1024:2048]]
    gvst = [p.sem("vx_st0"), p.sem("vx_st1")]
    for nb0 in range(3):
        n0 = nb0 * 1024
        nw = min(1024, 2816 - n0)
        for wsrc, dst in [(wgq, gated), (wuq, upd_)]:
            wv = flat_view(wsrc, n0, [[2816, 1024], [1, 2816]])
            def bandcb(tt, psum, stage, waits, prel, dst=dst, n0=n0, nw=nw):
                i = tt % 2
                ms = gsl[i]
                p.emit("vector", lambda e, psum=psum, ms=ms, nw=nw: e.tensor_copy(
                           ms[:, 0:nw], psum[:, 0:nw]),
                       waits=list(waits) + [(gvst[i], gvst[i].n)], incs=[prel])
                p.dma("gpsimd", dst[tt * 128:(tt + 1) * 128, n0:n0 + nw], ms[:, 0:nw],
                      gvst[i], waits=[(prel[0], prel[0].n)])
            projA(cx, pp, src_tile=xqA, K=D, Tn=T, wdram=wv, wT=False, N=nw,
                  src_ready=xqr, evac_cb=bandcb, store=False)
    gdone = [(gvst[0], gvst[0].n), (gvst[1], gvst[1].n)]
    hq = dram("l1_hq", [2816, T], BF16)
    hch = p.sem("chv")
    hst = [p.sem("vx_st0"), p.sem("vx_st1")]
    htm = dram("l1_htm", [T, 2816], F32)
    hl2 = [p.sem("up_ld0"), p.sem("up_ld1")]
    for tt in range(NT):
        i = tt % 2
        ga = cx.SF[:, 0:2816]
        ua = cx.SF[:, 2816:5632]
        w0 = (gdone if tt == 0 else []) + [(hch, hch.n), (hst[0], hst[0].n),
                                           (hst[1], hst[1].n), (hl2[i], hl2[i].n)]
        p.dma("sync", ga, gated[tt * 128:(tt + 1) * 128, :], hl2[i], waits=w0)
        p.dma("sync", ua, upd_[tt * 128:(tt + 1) * 128, :], hl2[i])
        p.emit("scalar", lambda e, ga=ga: e.activation(out=ga, in_=ga, func=AF.Silu),
               waits=[(hl2[i], hl2[i].n)], incs=[(pp.evs, 1)])
        p.emit("vector", lambda e, ga=ga, ua=ua: e.tensor_mul(ga, ga, ua),
               waits=[(pp.evs, pp.evs.n)], incs=[(hch, 1)])
        p.dma("gpsimd", htm[tt * 128:(tt + 1) * 128, :], ga, hst[i],
              waits=[(hch, hch.n)])
    f5 = emit_fp4(cx, pp, htm, hq, T, 2816,
                  prev_done=[(hst[0], hst[0].n), (hst[1], hst[1].n)])
    md = dram("l1_m", [T, D], F32)
    hstream = (hq, cx.SBF[:, 0:2816], cx.SBF[:, 2816:5632], 0)
    wdv = flat_view(wdq, 0, [[1024, 2816], [1, 1024]])
    mdone = projA(cx, pp, src_stream=hstream, K=2816, Tn=T, wdram=wdv, wT=False, N=D,
                  dst_dram=md, src_ready=f5)
    return _emit_L1_part2(cx, pp, x_dram, xout, prev, locals())


def _emit_L1_part2(cx, pp, x_dram, xout, prev, env):
    p = cx.p
    dram, wbf2, inp = cx.dram, cx.wbf2, cx.inp
    xn, lnst, mdone = env["xn"], env["lnst"], env["mdone"]
    GB, BB = env["GB"], env["BB"]
    n1b = emit_bias_bcast(cx, pp, inp["L1_attn_norm1_scale"], 0, 1024, GB)
    r1 = dram("l1_r1", [D, T], BF16)
    r1d = emit_rms(cx, pp, x_dram, r1, 64, 80, 0.0, g_bcast=GB, pre_ln=(0, 16),
                   prev_done=mdone + n1b)
    rl = p.sem("lf")
    load_fm(cx, r1, D, T, cx.SRC, 0, rl, waits=r1d)
    rr = [(rl, rl.n)]
    rB = lambda dc: cx.SRC[:, dc * T:(dc + 1) * T]
    rA = lambda dc, tt: cx.SRC[:, dc * T + tt * 128: dc * T + tt * 128 + 128]
    vecs = [(inp["L1_attn_bq"], 0, 1024, 1.0), (inp["L1_attn_bk"], 0, 1024, 1.0)]
    colf, bdone, _ = vecs_to_cols(cx, pp, vecs, 128)
    q0 = dram("l1_q0", [D, T], BF16)
    k0 = dram("l1_k0", [D, T], BF16)
    projB(cx, pp, src_tile=rB, K=D, Tn=T, wdram=wbf2("L1_attn_Wq", (1024, 1024)),
          wT=False, N=D, dst_dram=q0, src_ready=rr + bdone, bias_col=lambda oc: colf(0, oc))
    projB(cx, pp, src_tile=rB, K=D, Tn=T, wdram=wbf2("L1_attn_Wk", (1024, 1024)),
          wT=False, N=D, dst_dram=k0, src_ready=rr, bias_col=lambda oc: colf(1, oc))
    bvb = emit_bias_bcast(cx, pp, inp["L1_attn_bv"], 0, 1024, BB)
    vext = dram("l1_vext", [T, 1040], BF16)
    vsl = [cx.SF[:, 0:1040], cx.SF[:, 1040:2080]]
    onesem = p.sem("vx_on")
    for i in range(2):
        ov = vsl[i].rearrange("p (a b) -> p a b", b=65)[:, :, 64:65]
        p.emit("vector", lambda e, ov=ov: e.memset(ov, 1.0),
               waits=[(p.sem("vx_st0" if i == 0 else "vx_st1"),
                       p.sem("vx_st0" if i == 0 else "vx_st1").n)], incs=[(onesem, 1)])
    onest = onesem.n
    vst2 = [p.sem("vx_st0"), p.sem("vx_st1")]
    vp = [None, None]
    def vext_cb(tt, psum, stage, waits, prel):
        i = tt % 2
        slot = vsl[i]
        sv = slot.rearrange("p (a b) -> p a b", b=65)[:, :, 0:64]
        p.emit("vector", lambda e, psum=psum, sv=sv: e.tensor_add(
                   sv, psum.rearrange("p (a b) -> p a b", b=64),
                   BB.rearrange("p (a b) -> p a b", b=64)),
               waits=list(waits) + bvb + [(onesem, onest)] +
                     ([vp[i]] if vp[i] else []),
               incs=[prel])
        p.dma("gpsimd", vext[tt * 128:(tt + 1) * 128, :], slot, vst2[i],
              waits=[(prel[0], prel[0].n)])
        vp[i] = (vst2[i], vst2[i].n)
    projA(cx, pp, src_tile=rA, K=D, Tn=T, wdram=wbf2("L1_attn_Wv", (1024, 1024)),
          wT=False, N=D, src_ready=rr, evac_cb=vext_cb, store=False)
    vdone = [(vst2[0], vst2[0].n), (vst2[1], vst2[1].n)]
    bgb = emit_bias_bcast(cx, pp, inp["L1_attn_bg"], 0, 1024, BB)
    gd = dram("l1_g", [T, D], BF16)
    gch = p.sem("chv")
    def g_cb(tt, psum, stage, waits, prel):
        p.emit("vector", lambda e, psum=psum, stage=stage: e.tensor_add(stage, psum, BB),
               waits=list(waits) + bgb, incs=[(gch, 1)])
        p.emit("scalar", lambda e, stage=stage: e.activation(out=stage, in_=stage, func=AF.Sigmoid),
               waits=[(gch, gch.n)], incs=[prel])
    gdone = projA(cx, pp, src_tile=rA, K=D, Tn=T, wdram=wbf2("L1_attn_Wg", (1024, 1024)),
                  wT=False, N=D, dst_dram=gd, src_ready=rr, evac_cb=g_cb)
    rc = p.sem("l1_rc")
    p.dma("sync", cx.SF[:, 4096:6144], cx.cdram["c_ropecos"][:, :], rc,
          waits=list(gdone) + vdone + [(rc, rc.n)])
    p.dma("sync", cx.SF[:, 6144:8192], cx.cdram["c_ropesin"][:, :], rc,
          waits=[(rc, rc.n)])
    rct = rc.n
    q1 = dram("l1_q1", [D, T], BF16)
    k1 = dram("l1_k1", [D, T], BF16)
    rp1 = emit_ropephi(cx, pp, q0, q1, True,
                       prev_done=[(rc, rct)] + pp.stage1024.all_done())
    rp2 = emit_ropephi(cx, pp, k0, k1, True, prev_done=rp1)
    ktm = dram("l1_ktm", [T, D], BF16)
    ktd = emit_fm2tm(cx, pp, k1, ktm, prev_done=rp2)
    return _emit_L1_part3(cx, pp, x_dram, xout, prev, env, locals())


def _emit_L1_part3(cx, pp, x_dram, xout, prev, env, env2):
    p = cx.p
    dram, wbf2, inp = cx.dram, cx.wbf2, cx.inp
    xn = env["xn"]
    q1, ktm, vext, gd = env2["q1"], env2["ktm"], env2["vext"], env2["gd"]
    ktd, gdone, vdone = env2["ktd"], env2["gdone"], env2["vdone"]
    GB, BB = env["GB"], env["BB"]
    ql = p.sem("lf")
    load_fm(cx, q1, D, T, cx.SRC, 0, ql, waits=ktd)
    qr = [(ql, ql.n)]
    def svp(h):
        return cx.PA[0:64, 512 * (h // 7) + (h % 7) * 65: 512 * (h // 7) + (h % 7) * 65 + 65]
    kld = [p.sem("l1_kl0"), p.sem("l1_kl1")]
    vld = [p.sem("l1_vl0"), p.sem("l1_vl1")]
    gld = [p.sem("l1_gl0"), p.sem("l1_gl1")]
    KC = [cx.SBF[:, 0:512], cx.SBF[:, 512:1024]]
    KCs = [cx.SBF[0:64, 0:1024], cx.SBF[0:64, 1024:2048]]
    VCs = [cx.SBF[0:64, 2048:3088], cx.SBF[0:64, 3088:4128]]
    GCs = [cx.SBF[0:64, 4128:5152], cx.SBF[0:64, 5152:6176]]
    def svsb(h):
        b = (h % 2) * 64
        return cx.SBF[b:b + 64, 6176 + (h // 2) * 65: 6176 + (h // 2) * 65 + 65]
    sevac = p.sem("chv")
    avs = p.sem("l1_av")
    ost = [p.sem("vx_st0"), p.sem("vx_st1")]
    odone = p.sem("l1_od")
    attn = dram("l1_attn", [T, D], BF16)
    lastp = [None]
    dch = p.sem("l1_dch")
    p.emit("vector", lambda e: e.memset(cx.SMALL[0:1, 730:731], 0.0))
    for c in range(32):
        i = c % 2
        cw = ([(avs, avs.n)] if c >= 2 else (qr + list(vdone) + list(gdone)))
        p.dma("sync", KCs[i], ktm[c * 64:(c + 1) * 64, :], kld[i],
              waits=cw + [(kld[i], kld[i].n)])
        p.dma("sync", VCs[i], vext[c * 64:(c + 1) * 64, :], vld[i],
              waits=[(vld[i], vld[i].n)])
        p.dma("sync", GCs[i], gd[c * 64:(c + 1) * 64, :], gld[i],
              waits=[(gld[i], gld[i].n)])
        lt = [(kld[i], kld[i].n), (vld[i], vld[i].n), (gld[i], gld[i].n)]
        for h in range(16):
            p.emit("tensor", lambda e, h=h, i=i, c=c: e.matmul(
                       svp(h), KCs[i][:, h * 64:(h + 1) * 64], VCs[i][:, h * 65:(h + 1) * 65],
                       start=(c == 0), stop=(c == 31), skip_group_check=True),
                   waits=(lt + ([(sevac, sevac.n)] if c > 0 else [])) if h == 0 else (),
                   incs=[(pp.mm, 1)] if h == 15 else ())
        mmv = pp.mm.n
        for h in range(16):
            p.emit("vector", lambda e, h=h: e.tensor_copy(svsb(h), svp(h)),
                   waits=[(pp.mm, mmv)] if h == 0 else (), incs=[(sevac, 1)])
        sevt = sevac.n
        for h in range(16):
            ps, pw, prel = pp.pb512.acquire()
            psum = ps[0:64, 0:65]
            hb = (h % 2) * 64
            p.emit("tensor", lambda e, h=h, c=c, psum=psum, hb=hb: e.matmul(
                       psum, cx.SRC[hb:hb + 64, (h // 2) * T + c * 64: (h // 2) * T + c * 64 + 64],
                       svsb(h), start=True, stop=True),
                   waits=[(sevac, sevt)] + pw, incs=[(pp.mm, 1)])
            mv = pp.mm.n
            dcol = cx.SMALL[:, 729:730]
            p.emit("vector", lambda e, psum=psum: e.tensor_scalar_add(
                       dcol[0:64, :], psum[:, 64:65], 1e-6),
                   waits=[(pp.mm, mv), (dch, dch.n)] + ([lastp[0]] if lastp[0] else []),
                   incs=[(dch, 1)])
            p.emit("vector", lambda e: e.reciprocal(dcol[0:64, :], dcol[0:64, :]),
                   waits=[(dch, dch.n)], incs=[(dch, 1)])
            oi = (c * 16 + h) % 2
            stg = cx.SF[0:64, 5632 + oi * 64: 5632 + oi * 64 + 64]
            p.emit("vector", lambda e, psum=psum, stg=stg, h=h, i=i: e.tensor_scalar(
                       out=stg, in0=psum[:, 0:64], scalar1=dcol[0:64, :], scalar2=1.0,
                       op0=OP.mult, op1=OP.mult),
                   waits=[(dch, dch.n)] + ([(ost[oi], ost[oi].n)] if ost[oi].n > 0 else []),
                   incs=[(dch, 1)])
            p.emit("vector", lambda e, stg=stg, h=h, i=i: e.tensor_mul(
                       stg, stg, GCs[i][:, h * 64:(h + 1) * 64]),
                   waits=[(dch, dch.n)], incs=[prel])
            p.dma("gpsimd", attn[c * 64:(c + 1) * 64, h * 64:(h + 1) * 64], stg, ost[oi],
                  waits=[(prel[0], prel[0].n)])
            lastp[0] = (prel[0], prel[0].n)
        p.emit("vector", lambda e: e.tensor_copy(cx.SMALL[0:1, 730:731], cx.SMALL[0:1, 730:731]),
               waits=[lastp[0]], incs=[(avs, 1)])
    adone = [(ost[0], ost[0].n), (ost[1], ost[1].n)]
    return _emit_L1_part4(cx, pp, x_dram, xout, prev, env, env2, locals())


def _emit_L1_part4(cx, pp, x_dram, xout, prev, env, env2, env3):
    p = cx.p
    dram, wbf2, inp = cx.dram, cx.wbf2, cx.inp
    xn = env["xn"]
    attn, adone = env3["attn"], env3["adone"]
    GB, BB = env["GB"], env["BB"]
    md, mdone = env["md"], env["mdone"]
    al = p.sem("lf")
    ad = dram("l1_attnf", [D, T], BF16)
    at = emit_tm2fm(cx, pp, attn, ad, prev_done=adone)
    load_fm(cx, ad, D, T, cx.SBF, 0, al, waits=at)
    bob = emit_bias_bcast(cx, pp, inp["L1_attn_bo"], 0, 1024, BB)
    ix = dram("l1_ix", [T, D], F32)
    x2ld = [p.sem("x2_ld0"), p.sem("x2_ld1")]
    xsl = [cx.SF[:, 0:1024], cx.SF[:, 1024:2048]]
    x2ch = p.sem("chv")
    last_prel = [None, None]
    def ix_cb(tt, psum, stage, waits, prel):
        i = tt % 2
        slot = xsl[i]
        p.dma("sync", slot, x_dram[tt * 128:(tt + 1) * 128, :], x2ld[i],
              waits=([last_prel[i]] if last_prel[i] is not None else []) + [(x2ld[i], x2ld[i].n)])
        p.emit("vector", lambda e, slot=slot, tt=tt: e.tensor_scalar(
                   out=slot, in0=slot, scalar1=cx.SMALL[:, 0 + tt:0 + tt + 1],
                   scalar2=cx.SMALL[:, 16 + tt:16 + tt + 1],
                   op0=OP.subtract, op1=OP.mult),
               waits=[(x2ld[i], x2ld[i].n)], incs=[(x2ch, 1)])
        p.emit("vector", lambda e, psum=psum, stage=stage, slot=slot: e.tensor_add(
                   stage, psum, slot),
               waits=list(waits) + [(x2ch, x2ch.n)], incs=[(x2ch, 1)])
        p.emit("vector", lambda e, stage=stage: e.tensor_add(stage, stage, BB),
               waits=[(x2ch, x2ch.n)] + bob, incs=[prel])
        last_prel[i] = (prel[0], prel[0].n)
    ixd = projA(cx, pp, src_tile=lambda dc, tt: cx.SBF[:, dc * T + tt * 128: dc * T + tt * 128 + 128],
                K=D, Tn=T, wdram=wbf2("L1_attn_Wo", (1024, 1024)), wT=False, N=D,
                dst_dram=ix, src_ready=[(al, al.n)], evac_cb=ix_cb)
    n2b = emit_bias_bcast(cx, pp, inp["L1_attn_norm2_scale"], 0, 1024, GB)
    r2 = dram("l1_r2", [D, T], BF16)
    r2d = emit_rms(cx, pp, ix, r2, 96, 112, 0.0, g_bcast=GB, prev_done=ixd + n2b)
    r2l = p.sem("lf")
    load_fm(cx, r2, D, T, cx.SRC, 0, r2l, waits=r2d)
    r2r = [(r2l, r2l.n)]
    r2B = lambda dc: cx.SRC[:, dc * T:(dc + 1) * T]
    r2A = lambda dc, tt: cx.SRC[:, dc * T + tt * 128: dc * T + tt * 128 + 128]
    COEF = 416
    rdone = emit_router(cx, pp, src_tile=r2A, src_ready=r2r,
                        wdram=wbf2("L1_attn_router_W", (1024, 8)), wT=False, E=8,
                        bias_row_dram=inp["L1_attn_router_b"], mode="soft_renorm",
                        coeff_col0=COEF)
    vecs = []
    for e in range(8):
        vecs.append((inp["L1_attn_b1"], e * 4096, 4096, 1.0))
    colf, b1done, _ = vecs_to_cols(cx, pp, vecs, 688)
    acc = dram("l1_acc", [T, D], F32)
    acld = [p.sem("mo_ld0"), p.sem("mo_ld1")]
    msem = [p.sem("mo_st0"), p.sem("mo_st1")]
    m0 = [msem[0].n, msem[1].n]
    mst = [cx.SF[:, 4096:5120], cx.SF[:, 5120:6144]]
    b2sem = p.sem("gb2")
    for e in range(8):
        hd = dram(f"l1_he{e}", [4096, T], BF16)
        w1v = flat_view(cx.wb["L1_attn_W1"], e * 1024 * 4096, [[4096, 1024], [1, 4096]])
        projB(cx, pp, src_tile=r2B, K=D, Tn=T, wdram=w1v, wT=False, N=4096,
              dst_dram=hd, src_ready=r2r + (rdone if e == 0 else []) + b1done,
              bias_col=lambda oc, e=e: colf(e, oc), act=AF.Gelu)
        p.dma("gpsimd", cx.ROWS[0:1, 2176:3200],
              flat_view(inp["L1_attn_b2"], e * 1024, [[1, 1], [1, 1024]]), b2sem,
              waits=[(pp.mm, pp.mm.n)])
        b2t = b2sem.n
        for half in range(2):
            pas = e * 2 + half
            w2v = flat_view(cx.wb["L1_attn_W2"],
                            e * 4096 * 1024 + half * 2048 * 1024, [[1024, 2048], [1, 1024]])
            hstream = (hd, cx.SBF[:, 8192:10240], cx.SBF[:, 10240:12288], half * 2048)
            def moe_cb(tt, psum, stage, waits, prel, pas=pas, e=e):
                i = tt % 2
                nst = pas * 8 + tt // 2
                ms = mst[i]
                w = list(waits) + ([(msem[i], m0[i] + 16 * nst)] if nst > 0 else [])
                slot = cx.SF[:, i * 1024 + 2048: i * 1024 + 2048 + 1024]
                if pas == 0:
                    p.dma("sync", slot, ix[tt * 128:(tt + 1) * 128, :], acld[i],
                          waits=[(acld[i], acld[i].n)] +
                                ([(msem[i], m0[i] + 16 * (tt // 2))] if tt >= 2 else []))
                    p.emit("vector", lambda e2, psum=psum, ms=ms, tt=tt, slot=slot, e=e:
                               e2.scalar_tensor_tensor(
                                   out=ms, in0=psum,
                                   scalar=cx.SMALL[:, COEF + tt * 8 + e:COEF + tt * 8 + e + 1],
                                   in1=slot, op0=OP.mult, op1=OP.add),
                           waits=w + [(acld[i], acld[i].n)], incs=[prel])
                else:
                    p.dma("sync", slot, acc[tt * 128:(tt + 1) * 128, :], acld[i],
                          waits=[(msem[i], m0[i] + 16 * (pas * 8 + tt // 2)), (acld[i], acld[i].n)])
                    p.emit("vector", lambda e2, psum=psum, ms=ms, slot=slot, tt=tt, e=e:
                               e2.scalar_tensor_tensor(
                                   out=ms, in0=psum,
                                   scalar=cx.SMALL[:, COEF + tt * 8 + e:COEF + tt * 8 + e + 1],
                                   in1=slot, op0=OP.mult, op1=OP.add),
                           waits=w + [(acld[i], acld[i].n)], incs=[prel])
                p.dma("gpsimd", acc[tt * 128:(tt + 1) * 128, :], ms, msem[i],
                      waits=[(prel[0], prel[0].n)])
            projA(cx, pp, src_stream=hstream, K=2048, Tn=T, wdram=w2v, wT=False, N=D,
                  src_ready=[(b2sem, b2t)] + (ixd if pas == 0 else []), evac_cb=moe_cb,
                  store=False, extra_row=cx.ROWS[0:1, 2176:3200] if half == 0 else None)
    upd = emit_update(cx, pp, x_dram, acc, md, xout,
                      pre_done=[(msem[0], msem[0].n), (msem[1], msem[1].n)] + mdone)
    return upd


LAYER_EMITTERS[1] = emit_L1
